# version 03
# baseline (speedup 1.0000x reference)
import numpy as np
import ml_dtypes
from contextlib import ExitStack

import concourse.bass as bass
import concourse.mybir as mybir
from concourse.bass_utils import run_bass_kernel_spmd

F32 = mybir.dt.float32
BF16 = mybir.dt.bfloat16
I32 = mybir.dt.int32
AF = mybir.ActivationFunctionType
ALU = mybir.AluOpType
AX = mybir.AxisListType

T = 2048
D = 2048
HK = 16
HV = 32
C = 128
NCH = T // C
BIG = 1e30
DEBUG = False


class Prog:
    ENGINES = ("tensor", "vector", "scalar", "gpsimd", "sync")

    def __init__(self):
        self.plan = {k: [] for k in self.ENGINES}
        self.cnt = {}
        self.handles = {}
        self.watermark = {k: {} for k in self.ENGINES}
        self.bank_guard = {}

    def op(self, eng, fn, inc=None):
        tok = None
        if inc is not None:
            name, amt = inc
            self.cnt[name] = self.cnt.get(name, 0) + amt
            tok = (name, self.cnt[name])

        def wrapped(e, fn=fn, inc=inc):
            inst = fn(e)
            if inc is not None:
                inst.then_inc(self.handles[inc[0]], inc[1])
        self.plan[eng].append(wrapped)
        return tok

    def wait(self, eng, tok):
        if tok is None:
            return
        name, val = tok
        wm = self.watermark[eng]
        if wm.get(name, 0) >= val:
            return
        wm[name] = val
        self.plan[eng].append(lambda e, n=name, v=val: e.wait_ge(self.handles[n], v))

    def waits(self, eng, toks):
        best = {}
        for t in toks:
            if t is None:
                continue
            n, v = t
            if best.get(n, -1) < v:
                best[n] = v
        for n, v in best.items():
            self.wait(eng, (n, v))


def build_program():
    nc = bass.Bass("TRN2", target_bir_lowering=False)

    def inp(name, shape, dt):
        return nc.dram_tensor(name, shape, dt, kind="ExternalInput").ap()

    x_h = inp("x", [T, D], F32)
    wqkvz_h = inp("wqkvz", [D, 12288], F32)
    wba_h = inp("wba", [D, 64], F32)
    cw_h = inp("cw", [128, 256], F32)
    dtb_h = inp("dtb", [64, 1], F32)
    alog_h = inp("alog", [64, 1], F32)
    normw_h = inp("normw", [128, 1], F32)
    wout_h = inp("wout", [4096, D], F32)
    gw_h = inp("gw", [8192, D], I32)
    uw_h = inp("uw", [8192, D], I32)
    dw_h = inp("dw", [D, 8192], I32)
    sg_h = inp("sgt", [128, 64], F32)
    su_h = inp("sut", [128, 64], F32)
    sd_h = inp("sdt", [128, 16], F32)

    y_h = nc.dram_tensor("y", [T, D], F32, kind="ExternalOutput").ap()

    scr_kind = "ExternalOutput" if DEBUG else "Internal"
    xwtm_h = nc.dram_tensor("xwtm", [HK, T, 512], BF16, kind="Internal").ap()
    z_h = nc.dram_tensor("zscr", [HK, T, 256], BF16, kind=scr_kind).ap()
    o_h = nc.dram_tensor("oscr", [HV, T, 128], BF16, kind=scr_kind).ap()
    act_h = nc.dram_tensor("actscr", [64, 128, T], BF16, kind="Internal").ap()
    x1f_h = nc.dram_tensor("x1fscr", [16, 128, T], F32, kind=scr_kind).ap()
    x1b_h = nc.dram_tensor("x1bscr", [16, 128, T], BF16, kind="Internal").ap()
    wg16_h = nc.dram_tensor("wg16", [8192, D], BF16, kind="Internal").ap()
    wu16_h = nc.dram_tensor("wu16", [8192, D], BF16, kind="Internal").ap()
    wd16_h = nc.dram_tensor("wd16", [D, 8192], BF16, kind="Internal").ap()

    idf_c = nc.inline_tensor(np.eye(128, dtype=np.float32), "idf_c").ap()
    idb_c = nc.inline_tensor(np.eye(128, dtype=ml_dtypes.bfloat16), "idb_c").ap()
    e127_c = nc.inline_tensor(
        np.broadcast_to((np.arange(128) == 127).astype(np.float32)[:, None], (128, 128)).copy(),
        "e127_c").ap()
    onesb_c = nc.inline_tensor(np.ones((128, 8), ml_dtypes.bfloat16), "onesb_c").ap()
    onesrb_c = nc.inline_tensor(np.ones((8, 128), ml_dtypes.bfloat16), "onesrb_c").ap()
    onesrf_c = nc.inline_tensor(np.ones((8, 128), np.float32), "onesrf_c").ap()
    ones64_c = nc.inline_tensor(np.ones((64, 128), np.float32), "ones64_c").ap()
    mN = np.where(np.tril(np.ones((C, C)), -1) > 0, 0.0, BIG).astype(np.float32)
    mAL = np.where(np.triu(np.ones((C, C)), 0) > 0, 0.0, BIG).astype(np.float32)
    maskN_c = nc.inline_tensor(mN, "maskN_c").ap()
    maskAL_c = nc.inline_tensor(mAL, "maskAL_c").ap()
    CONSTVALS = [128.0, 1.0 / 128.0, -1.0, 128.0e-6, 1e-6, 1e-5, 1.0, 0.0]
    cvals_c = nc.inline_tensor(
        np.broadcast_to(np.asarray(CONSTVALS, np.float32)[None, :], (128, len(CONSTVALS))).copy(),
        "cvals_c").ap()

    p = Prog()

    with ExitStack() as ctx:
        def sb(name, shape, dt, st=None):
            return (st or ctx).enter_context(nc.sbuf_tensor(name, shape, dt))

        idf = sb("idf", [128, 128], F32)
        idb = sb("idb", [128, 128], BF16)
        e127 = sb("e127", [128, 128], F32)
        onesb = sb("onesb", [128, 8], BF16)
        onesrb = sb("onesrb", [8, 128], BF16)
        onesrf = sb("onesrf", [8, 128], F32)
        ones64 = sb("ones64", [64, 128], F32)
        maskN = sb("maskN", [128, 128], F32)
        maskAL = sb("maskAL", [128, 128], F32)
        cw = sb("cw_s", [128, 256], F32)
        dtb = sb("dtb_s", [64, 1], F32)
        alog = sb("alog_s", [64, 1], F32)
        nega = sb("nega", [64, 1], F32)
        normw = sb("normw_s", [128, 1], F32)
        sgt = sb("sgt_s", [128, 64], F32)
        sut = sb("sut_s", [128, 64], F32)
        sdt = sb("sdt_s", [128, 16], F32)
        cvals = sb("cvals", [128, 8], F32)
        rows4 = sb("rows4", [1, 2048], F32)
        srow = rows4.ap()[:, 0:512]
        s2row = rows4.ap()[:, 512:1024]
        murow = rows4.ap()[:, 1024:1536]
        invrow = rows4.ap()[:, 1536:2048]

        stSC = ExitStack()
        SCb = sb("SCb", [128, 1024], F32, stSC)
        SCbneg = sb("SCbneg", [128, 1024], F32, stSC)
        SCg = sb("SCg", [128, 1024], F32, stSC)
        SCeg = sb("SCeg", [128, 1024], F32, stSC)
        SCbe = sb("SCbe", [128, 1024], F32, stSC)
        SCkd = sb("SCkd", [128, 1024], F32, stSC)
        SCgl = sb("SCgl", [128, 1024], F32, stSC)
        wbounce = [sb("wb%d" % i, [128, 2048], BF16, stSC) for i in range(2)]

        pb = [ctx.enter_context(nc.psum_tensor("pb%d" % i, [128, 512], F32)) for i in range(6)]
        pbh = [ctx.enter_context(nc.psum_tensor("pbh%d" % i, [128, 128], BF16)) for i in range(2)]

        DMA_SEMS = set()

        def dma(q, sem, outap, inap, transpose=False):
            DMA_SEMS.add(sem)
            return p.op(q, lambda e, o=outap, i=inap, tr=transpose:
                        e.dma_start(out=o, in_=i, transpose=tr),
                        inc=(sem, 16))

        def mm(bank, outap, lhsT, rhs, start=True, stop=True, tok=False, waits=()):
            w = list(waits)
            if start and bank is not None:
                w.append(p.bank_guard.get(bank))
            p.waits("tensor", w)
            return p.op("tensor", lambda e, o=outap, l=lhsT, r=rhs, s1=start, s2=stop:
                        e.matmul(o, l, r, start=s1, stop=s2),
                        inc=("sp", 1) if tok else None)

        def tr(bank, outap, inap, ident, tok=True, waits=()):
            w = list(waits) + [p.bank_guard.get(bank)]
            p.waits("tensor", w)
            return p.op("tensor", lambda e, o=outap, i=inap, d=ident:
                        e.transpose(o, i, d), inc=("sp", 1) if tok else None)

        def vop(eng, fn, tok=True, waits=(), bank=None):
            sem = {"vector": "sv", "scalar": "sa", "gpsimd": "sgp"}[eng]
            p.waits(eng, waits)
            t = p.op(eng, fn, inc=(sem, 1) if tok else None)
            if bank is not None:
                p.bank_guard[bank] = t
            return t

        def colap(tile_ap, col):
            return tile_ap[:, col:col + 1]

        def bcast_lhsT(tile_ap, col, n=128):
            base = tile_ap[:, col:col + 1]
            return bass.AP(base.tensor, base.offset, [base.ap[0], [0, n]])

        def phase_barrier(dma_toks=()):
            toks = [("sp", p.cnt.get("sp", 0)), ("sv", p.cnt.get("sv", 0)),
                    ("sa", p.cnt.get("sa", 0)), ("sgp", p.cnt.get("sgp", 0))]
            toks += [t for t in dma_toks if t is not None]
            for eng in ("tensor", "vector", "scalar", "gpsimd", "sync"):
                p.waits(eng, toks)

        for dst, src in ((idf, idf_c), (idb, idb_c), (e127, e127_c), (onesb, onesb_c),
                         (onesrb, onesrb_c), (onesrf, onesrf_c), (ones64, ones64_c),
                         (maskN, maskN_c), (maskAL, maskAL_c), (cvals, cvals_c),
                         (cw, cw_h), (dtb, dtb_h),
                         (alog, alog_h), (normw, normw_h), (sgt, sg_h), (sut, su_h),
                         (sdt, sd_h)):
            tok_consts = dma("sync", "qcst", dst[:], src)

        consts_tok = tok_consts
        for _eng in ("tensor", "vector", "scalar", "gpsimd"):
            p.waits(_eng, [consts_tok])
        for _i, _v in enumerate(CONSTVALS):
            nc.const_aps.register(F32, _v, cvals.ap()[:, _i:_i + 1])

        bounce_done = []
        prev_b = [None, None]
        pieces = ([(gw_h, wg16_h, s, 0) for s in range(64)]
                  + [(uw_h, wu16_h, s, 0) for s in range(64)]
                  + [(dw_h, wd16_h, s, q) for s in range(16) for q in range(4)])
        for pi, (srct, dstt, s, q) in enumerate(pieces):
            par = pi % 2
            p.waits("gpsimd", [prev_b[par]])
            t_in = dma("gpsimd", "qb%d" % par, wbounce[par][:],
                       srct[s * 128:(s + 1) * 128, q * 2048:(q + 1) * 2048])
            p.waits("gpsimd", [t_in])
            t_out = dma("gpsimd", "qb%d" % par, dstt[s * 128:(s + 1) * 128, q * 2048:(q + 1) * 2048],
                        wbounce[par][:])
            prev_b[par] = t_out
            bounce_done.append(t_out)
        bounce_fin = [prev_b[0], prev_b[1]]

        stA = ExitStack()
        xfm = sb("xfm", [128, 16 * 2048], BF16, stA)
        wbab = sb("wbab", [128, 16 * 64], BF16, stA)
        stA2 = ExitStack()
        whb = sb("whb", [128, 16 * 768], BF16, stA2)
        xwtm_b = [sb("xwtmb%d" % i, [128, 16 * 512], BF16, stA2) for i in range(2)]
        zstr = sb("zstr", [128, 16 * 256], BF16, stA2)
        xin = [sb("xin%d" % i, [128, 2048], F32, stA2) for i in range(2)]

        t_wba = dma("gpsimd", "qwba", wbab[:].rearrange("q (c w) -> q c w", w=64),
                    wba_h.rearrange("(c q) w -> q c w", q=128))
        consts_tok2 = t_wba
        for _eng in ("tensor", "vector", "scalar", "gpsimd"):
            p.waits(_eng, [consts_tok2])

        xin_used = [None, None]
        for tt in range(16):
            par = tt % 2
            p.waits("sync", [xin_used[par]])
            tin = dma("sync", "qx%d" % par, xin[par][:], x_h[tt * 128:(tt + 1) * 128, :])
            for dc in range(16):
                bk = 4 + dc % 2
                ttr = tr(bk, pb[bk][:, 0:128], xin[par][:, dc * 128:(dc + 1) * 128], idf[:],
                         waits=[tin])
                te = vop("scalar", lambda e,
                         o=xfm[:, dc * 2048 + tt * 128: dc * 2048 + (tt + 1) * 128],
                         i=pb[bk][:, 0:128]: e.copy(o, i), waits=[ttr], bank=bk)
            xin_used[par] = ttr
        xfm_ready = te

        wh_free = None
        xw_spill = [None, None]
        z_spill = None
        for h in range(HK):
            par = h % 2
            p.waits("gpsimd", [wh_free])
            wh_tok = dma("gpsimd", "qwh", whb[:].rearrange("q (c w) -> q c w", w=768),
                         wqkvz_h[:, 768 * h:768 * (h + 1)].rearrange("(c q) w -> q c w", q=128))
            ev_mixed = []
            ev_z = []
            for tt in range(16):
                bk = tt % 2
                p.waits("tensor", [xfm_ready, wh_tok])
                for dc in range(16):
                    t_mm = mm(bk if dc == 0 else None, pb[bk][:, 0:512],
                              xfm[:, dc * 2048 + tt * 128: dc * 2048 + (tt + 1) * 128],
                              whb[:, dc * 768: dc * 768 + 512],
                              start=(dc == 0), stop=(dc == 15), tok=(dc == 15))
                te = vop("vector", lambda e, o=xwtm_b[par][:, tt * 512:(tt + 1) * 512],
                         i=pb[bk][:, 0:512]: e.tensor_copy(o, i),
                         waits=[t_mm, xw_spill[par]], bank=bk)
                ev_mixed.append(te)
                bz = 2 + tt % 2
                for dc in range(16):
                    t_mmz = mm(bz if dc == 0 else None, pb[bz][:, 0:256],
                               xfm[:, dc * 2048 + tt * 128: dc * 2048 + (tt + 1) * 128],
                               whb[:, dc * 768 + 512: dc * 768 + 768],
                               start=(dc == 0), stop=(dc == 15), tok=(dc == 15))
                tez = vop("scalar", lambda e, o=zstr[:, tt * 256:(tt + 1) * 256],
                          i=pb[bz][:, 0:256]: e.copy(o, i),
                          waits=[t_mmz, z_spill], bank=bz)
                ev_z.append(tez)
            wh_free = t_mmz
            p.waits("gpsimd", ev_mixed + ev_z)
            xw_spill[par] = dma("gpsimd", "qxw%d" % par,
                                xwtm_h[h].rearrange("(tt q) c -> q tt c", q=128),
                                xwtm_b[par][:].rearrange("q (tt c) -> q tt c", c=512))
            z_spill = dma("gpsimd", "qz", z_h[h].rearrange("(tt q) c -> q tt c", q=128),
                          zstr[:].rearrange("q (tt c) -> q tt c", c=256))
        xw_spill_all = xw_spill[:]
        z_spill_all = [z_spill]

        stA2.close()
        bapk = [sb("bapk%d" % i, [128, 2048], F32, stA) for i in range(3)]
        bafm = bapk[0].ap()[0:64, :]
        bsig = bapk[0].ap()[64:128, :]
        gpre = bapk[1].ap()[0:64, :]
        gcum = bapk[1].ap()[64:128, :]
        spt = bapk[2].ap()[0:64, :]
        spt2 = bapk[2].ap()[64:128, :]

        ba_toks = []
        for ts4 in range(4):
            bk = ts4 % 2
            p.waits("tensor", [xfm_ready, t_wba])
            for dc in range(16):
                t_mm = mm(bk if dc == 0 else None, pb[bk][0:64, 0:512],
                          wbab[:, dc * 64:(dc + 1) * 64],
                          xfm[:, dc * 2048 + ts4 * 512: dc * 2048 + (ts4 + 1) * 512],
                          start=(dc == 0), stop=(dc == 15), tok=(dc == 15))
            te = vop("vector", lambda e, o=bafm[:, ts4 * 512:(ts4 + 1) * 512],
                     i=pb[bk][0:64, 0:512]: e.tensor_copy(o, i), waits=[t_mm], bank=bk)
            ba_toks.append(te)
        a2_pe_done = t_mm

        p.waits("scalar", ba_toks)
        t_sig = vop("scalar", lambda e: e.activation(bsig, bafm, AF.Sigmoid))
        t_sp1 = vop("scalar", lambda e: e.activation(spt, bafm, AF.Abs, bias=dtb[:]))
        t_sp2 = vop("scalar", lambda e: e.activation(spt2, spt, AF.Exp, scale=-1.0), waits=[t_sp1])
        t_sp3 = vop("scalar", lambda e: e.activation(gpre, spt2, AF.Ln, bias=1.0), waits=[t_sp2])
        t_sp4 = vop("scalar", lambda e: e.activation(spt, bafm, AF.Relu, bias=dtb[:]), waits=[t_sp2])
        t_na1 = vop("scalar", lambda e: e.activation(nega[:], alog[:], AF.Exp))
        t_sp5 = vop("vector", lambda e: e.tensor_add(gpre, gpre, spt), waits=[t_sp3, t_sp4])
        t_na2 = vop("vector", lambda e: e.tensor_scalar_mul(nega[:], nega[:], -1.0), waits=[t_na1])
        t_gp = vop("vector", lambda e: e.tensor_scalar(gpre, gpre, nega[:], None, ALU.mult),
                   waits=[t_sp5, t_na2])
        cs_toks = []
        for n in range(NCH):
            tcs = vop("vector", lambda e, o=gcum[:, n * 128:(n + 1) * 128],
                      i=gpre[:, n * 128:(n + 1) * 128]:
                      e.tensor_tensor_scan(o, i, ones64[:], 0.0, ALU.add, ALU.mult),
                      waits=[t_gp])
            cs_toks.append(tcs)
        sc_toks = []
        for n in range(NCH):
            bk = 4 + n % 2
            t1 = tr(bk, pb[bk][0:128, 0:64], bsig[:, n * 128:(n + 1) * 128],
                    idf[64:128, 64:128], waits=[t_sig])
            e1 = vop("scalar", lambda e, o=SCb[:, n * 64:(n + 1) * 64], i=pb[bk][0:128, 0:64]:
                     e.copy(o, i), waits=[t1], bank=bk)
            t2 = tr(bk, pb[bk][0:128, 64:128], gcum[:, n * 128:(n + 1) * 128],
                    idf[64:128, 64:128], waits=[cs_toks[n]])
            e2 = vop("scalar", lambda e, o=SCg[:, n * 64:(n + 1) * 64], i=pb[bk][0:128, 64:128]:
                     e.copy(o, i), waits=[t2], bank=bk)
            sc_toks += [e1, e2]
        t_scbn = vop("vector", lambda e: e.tensor_scalar_mul(SCbneg[:], SCb[:], -1.0), waits=sc_toks)
        t_sceg = vop("scalar", lambda e: e.activation(SCeg[:], SCg[:], AF.Exp), waits=sc_toks)
        gl_toks = []
        for half in range(2):
            bk = half % 2
            tm = mm(bk, pb[bk][:, 0:512], e127[:], SCg[:, half * 512:(half + 1) * 512],
                    tok=True, waits=sc_toks)
            te = vop("vector", lambda e, o=SCkd[:, half * 512:(half + 1) * 512], i=pb[bk][:, 0:512]:
                     e.tensor_copy(o, i), waits=[tm], bank=bk)
            gl_toks.append(te)
        t_gl2 = vop("scalar", lambda e: e.activation(SCgl[:], SCkd[:], AF.Exp), waits=gl_toks)
        t_kd1 = vop("vector", lambda e: e.tensor_sub(SCkd[:], SCkd[:], SCg[:]), waits=gl_toks + [t_gl2])
        t_kd2 = vop("scalar", lambda e: e.activation(SCkd[:], SCkd[:], AF.Exp), waits=[t_kd1])
        t_be = vop("vector", lambda e: e.tensor_tensor(
            SCbe[:].rearrange("q (n m j) -> q n m j", m=16, j=4)[:, :, :, 0:2],
            SCb[:].rearrange("q (n m j) -> q n m j", m=16, j=4)[:, :, :, 0:2],
            SCeg[:].rearrange("q (n m j) -> q n m j", m=16, j=4)[:, :, :, 2:4],
            ALU.mult), waits=[t_sceg] + sc_toks)
        sc_ready = [t_scbn, t_sceg, t_kd2, t_gl2, t_be]

        stA.close()

        stB = ExitStack()
        gy = sb("gy", [128, 4 * 2048], BF16, stB)
        qhat = sb("qhat", [128, 4 * 2048], BF16, stB)
        khat = sb("khat", [128, 4 * 2048], BF16, stB)
        khatT = sb("khatT", [128, 4 * 2048], BF16, stB)
        vTt = sb("vTt", [128, 4 * 2048], BF16, stB)
        acc_b = sb("acc_b", [128, 2048], F32, stB)
        sqb = sb("sqb", [128, 2048], BF16, stB)
        arow = sb("arow", [1, 2048], F32, stB)
        arow2 = sb("arow2", [1, 2048], F32, stB)
        arowb = sb("arowb", [1, 2048], BF16, stB)
        slots = []
        for s in range(2):
            sl = {
                "xnxt": [sb("xnxt%d_%d" % (s, i), [128, 256], BF16, stB) for i in range(2)],
                "xnI": sb("xnI%d" % s, [128, 128], BF16, stB),
                "tch": [sb("tch%d_%d" % (s, i), [128, 128], BF16, stB) for i in range(2)],
                "rti": sb("rti%d" % s, [128, 256], BF16, stB),
                "wtn": sb("wtn%d" % s, [128, 128], BF16, stB),
                "vnew": sb("vnew%d" % s, [128, 128], BF16, stB),
                "kdec": sb("kdec%d" % s, [128, 128], BF16, stB),
                "alT": sb("alT%d" % s, [128, 128], BF16, stB),
                "s32": sb("s32_%d" % s, [128, 128], F32, stB),
                "s16": sb("s16_%d" % s, [128, 128], BF16, stB),
                "qs_b": sb("qs_b%d" % s, [128, 128], F32, stB),
                "dprN": sb("dprN%d" % s, [128, 128], F32, stB),
                "dprA": sb("dprA%d" % s, [128, 128], F32, stB),
                "decN": sb("decN%d" % s, [128, 128], F32, stB),
                "decA": sb("decA%d" % s, [128, 128], F32, stB),
                "ostrip": sb("ostrip%d" % s, [128, NCH * 128], BF16, stB),
                "bkA": pb[2 * s], "bkB": pb[2 * s + 1], "pbh_id": 6 + s,
                "st": {}, "ospill": None,
            }
            slots.append(sl)

        state = {"ospill": [None, None], "o_last": None}

        def conv_head(h, gy_free_tok):
            p.waits("gpsimd", [gy_free_tok, xw_spill_all[h % 2]])
            fold_toks = []
            for tt in range(16):
                gt = tt // 4
                po = (tt % 4) * 32
                tf = dma("gpsimd", "qfold",
                         gy[po:po + 32, gt * 2048:(gt + 1) * 2048].rearrange(
                             "q (r c) -> q r c", r=4),
                         xwtm_h[h, tt * 128:(tt + 1) * 128, :])
                fold_toks.append(tf)
            conv_toks = []
            for gt in range(4):
                gslice = gy[:, gt * 2048:(gt + 1) * 2048]
                w3 = colap(cw[:], h * 16 + gt * 4 + 3)
                tc = vop("vector", lambda e, o=acc_b[:], i=gslice, s=w3:
                         e.tensor_scalar(o, i, s, None, ALU.mult),
                         waits=[fold_toks[-1]] + conv_toks[-1:])
                for k in (2, 1, 0):
                    sft = 3 - k
                    wk = colap(cw[:], h * 16 + gt * 4 + k)
                    tc = vop("vector", lambda e, o=acc_b[:, sft:2048],
                             i=gslice[:, 0:2048 - sft], s=wk, a=acc_b[:, sft:2048]:
                             e.scalar_tensor_tensor(o, i, s, a, ALU.mult, ALU.add),
                             waits=[tc])
                ts_ = vop("scalar", lambda e, o=gslice, i=acc_b[:]:
                          e.activation(o, i, AF.Silu), waits=[tc])
                conv_toks.append(ts_)
            return conv_toks

        def emit_hat(dst_tile, is_q, gy_toks, dst_guard):
            last = None
            tre_prev = None
            tb_prev = None
            for loc in range(4):
                gslice = gy[:, loc * 2048:(loc + 1) * 2048]
                tq = vop("scalar", lambda e, o=sqb[:], i=gslice:
                         e.activation(o, i, AF.Square), waits=gy_toks + [last])
                mv_last = tre_prev
                for q4 in range(4):
                    tm4 = mm(4, pb[4][0:1, 0:512], onesb[:, 0:1], sqb[:, q4 * 512:(q4 + 1) * 512],
                             tok=True, waits=[tq, mv_last])
                    scale = 128.0 if is_q else 1.0
                    bias = 128.0e-6 if is_q else 1e-6
                    mv_last = vop("scalar", lambda e, o=arow[0:1, q4 * 512:(q4 + 1) * 512],
                                  i=pb[4][0:1, 0:512], s=scale, b=bias:
                                  e.activation(o, i, AF.Sqrt, bias=b, scale=s), waits=[tm4],
                                  bank=4)
                tre = vop("vector", lambda e: e.reciprocal(arow2[:], arow[:]),
                          waits=[mv_last, tb_prev])
                trb = vop("vector", lambda e: e.tensor_copy(arowb[:], arow2[:]), waits=[tre])
                tre_prev = tre
                for q4 in range(4):
                    tb = mm(5, pb[5][:, 0:512], onesrb[0:1, :], arowb[0:1, q4 * 512:(q4 + 1) * 512],
                            tok=True, waits=[trb])
                    tb_prev = tb
                    last = vop("vector", lambda e,
                               o=dst_tile[:, loc * 2048 + q4 * 512: loc * 2048 + (q4 + 1) * 512],
                               i=gslice[:, q4 * 512:(q4 + 1) * 512], b=pb[5][:, 0:512]:
                               e.tensor_tensor(o, i, b, ALU.mult),
                               waits=[tb, dst_guard], bank=5)
            return last

        def emit_transposes(src_tile, dst_tile, src_toks, dst_guard):
            last = None
            for loc in range(4):
                for n in range(NCH):
                    bk = 6 + n % 2
                    tt_ = tr(bk, pbh[n % 2][:],
                             src_tile[:, loc * 2048 + n * 128: loc * 2048 + (n + 1) * 128],
                             idb[:], waits=src_toks + [dst_guard])
                    last = vop("scalar", lambda e,
                               o=dst_tile[:, (loc * NCH + n) * 128:(loc * NCH + n + 1) * 128],
                               i=pbh[n % 2][:]: e.copy(o, i), waits=[tt_], bank=bk)
            return last

        def delta_setup(sl, hv, a, vloc):
            st = sl["st"]
            st["hv"] = hv
            st["loc"] = hv // 2 - 4 * a
            st["ach"] = (hv // 2) * 4 + 2 + hv % 2
            st["bch"] = (hv // 2) * 4 + hv % 2
            st["vloc"] = vloc
            t_s0 = vop("gpsimd", lambda e, t=sl["s32"]: e.memset(t[:], 0.0),
                       waits=[st.get("s_used"), st.get("s16_used")])
            t_s1 = vop("gpsimd", lambda e, t=sl["s16"]: e.memset(t[:], 0.0),
                       waits=[st.get("s16_used")])
            st["sready"] = [t_s0, t_s1]

        def delta_chunk(sl, n, qk_toks, vt_tok):
            st = sl["st"]
            loc, ach, bch, vloc = st["loc"], st["ach"], st["bch"], st["vloc"]
            bkA, bkB = sl["bkA"], sl["bkB"]
            kcol = loc * 2048 + n * 128
            ktcol = (loc * NCH + n) * 128
            vtcol = (vloc * NCH + n) * 128
            scol = n * 64
            t_g1 = mm(None, bkA[:, 0:128], khat[:, kcol:kcol + 128], khat[:, kcol:kcol + 128],
                      tok=True, waits=qk_toks + [st.get("A0"), st.get("Abank_dve"),
                                                 st.get("Abank_act")])
            t_g2 = mm(None, bkA[:, 128:256], khat[:, kcol:kcol + 128],
                      qhat[:, kcol:kcol + 128], tok=True, waits=[st.get("A1")])
            t_e = mm(None, bkA[:, 256:384], bcast_lhsT(SCg[:], scol + ach), idf[:],
                     tok=True, waits=sc_ready + [st.get("A2")])
            t_dn = vop("vector", lambda e, ps=bkA[:, 256:384], gc=colap(SCg[:], scol + ach),
                       o=sl["dprN"][:]:
                       e.scalar_tensor_tensor(o, ps, gc, maskN[:], ALU.subtract, ALU.add),
                       waits=[t_e, st.get("dprN_used")])
            t_dn2 = vop("scalar", lambda e, o=sl["decN"][:], i=sl["dprN"][:]:
                        e.activation(o, i, AF.Exp, scale=-1.0),
                        waits=[t_dn, st.get("decN_used")])
            st["dprN_used"] = t_dn2
            t_da = vop("vector", lambda e, ps=bkA[:, 256:384], gc=colap(SCg[:], scol + ach),
                       o=sl["dprA"][:]:
                       e.scalar_tensor_tensor(o, ps, gc, maskAL[:], ALU.subtract, ALU.subtract),
                       waits=[t_e, st.get("dprA_used"), t_dn])
            st["A2"] = t_da
            t_da2 = vop("scalar", lambda e, o=sl["decA"][:], i=sl["dprA"][:]:
                        e.activation(o, i, AF.Exp), waits=[t_da, st.get("decA_used")])
            st["dprA_used"] = t_da2
            xn0 = sl["xnxt"][0]
            t_xn = vop("vector", lambda e, ps=bkA[:, 0:128], bc=colap(SCbneg[:], scol + bch),
                       o=xn0[:, 0:128], d=sl["decN"][:]:
                       e.scalar_tensor_tensor(o, ps, bc, d, ALU.mult, ALU.mult),
                       waits=[t_g1, t_dn2, st.get("xn_used")])
            st["decN_used"] = t_xn
            st["A0"] = t_xn
            t_al = vop("vector", lambda e, ps=bkA[:, 128:256], o=sl["alT"][:], d=sl["decA"][:]:
                       e.tensor_tensor(o, ps, d, ALU.mult),
                       waits=[t_g2, t_da2, st.get("al_used")])
            st["decA_used"] = t_al
            st["A1"] = t_al
            ph = sl["pbh_id"]
            t_xt = tr(ph, pbh[ph - 6][:], xn0[:, 0:128], idb[:], waits=[t_xn])
            t_xte = vop("scalar", lambda e, o=xn0[:, 128:256], i=pbh[ph - 6][:]:
                        e.copy(o, i), waits=[t_xt], bank=ph)
            t_t0 = vop("vector", lambda e, o=sl["tch"][0][:], i=xn0[:, 128:256]:
                       e.tensor_add(o, i, idb[:]), waits=[t_xte, st.get("tch_used")])
            cur = 0
            t_prev_t = t_t0
            t_prev_x = [t_xn, t_xte]
            for lev in range(6):
                nxt = 1 - cur
                xc, xv = sl["xnxt"][cur], sl["xnxt"][nxt]
                t_sq1 = mm(None, bkB[:, 0:128], xc[:, 128:256], xc[:, 0:128],
                           tok=(lev == 5), waits=t_prev_x + [st.get("Bsq")])
                if lev < 5:
                    t_sq2 = mm(None, bkB[:, 128:256], xc[:, 0:128], xc[:, 128:256], tok=True)
                    t_ev = vop("vector", lambda e, o=xv[:, 0:256], i=bkB[:, 0:256]:
                               e.tensor_copy(o, i), waits=[t_sq2])
                else:
                    t_ev = vop("vector", lambda e, o=xv[:, 0:128], i=bkB[:, 0:128]:
                               e.tensor_copy(o, i), waits=[t_sq1])
                st["Bsq"] = t_ev
                t_xi = vop("gpsimd", lambda e, i=xv[:, 0:128], o=sl["xnI"][:]:
                           e.tensor_tensor(o, i, idb[:], ALU.add),
                           waits=[t_ev, st.get("xnI_used")])
                st["xnI_used"] = t_xi
                t_xi_last = t_xi
                t_ap = mm(None, bkA[:, 384:512], sl["xnI"][:], sl["tch"][cur][:], tok=True,
                          waits=[t_xi, t_prev_t, st.get("A3")])
                t_te = vop("scalar", lambda e, o=sl["tch"][nxt][:], i=bkA[:, 384:512]:
                           e.copy(o, i), waits=[t_ap])
                st["A3"] = t_te
                t_prev_t = t_te
                t_prev_x = [t_ev]
                cur = nxt
            t_chain = t_prev_t
            st["xn_used"] = t_xi_last
            t_r1 = vop("vector", lambda e, i=vTt[:, vtcol:vtcol + 128],
                       s=colap(SCb[:], scol + bch), o=sl["rti"][:, 0:128]:
                       e.tensor_scalar(o, i, s, None, ALU.mult),
                       waits=[vt_tok, st.get("r_used")] + sc_ready)
            t_r2 = vop("vector", lambda e, i=khatT[:, ktcol:ktcol + 128],
                       s=colap(SCbe[:], scol + bch), o=sl["rti"][:, 128:256]:
                       e.tensor_scalar(o, i, s, None, ALU.mult), waits=qk_toks)
            t_kd = vop("gpsimd", lambda e, i=khatT[:, ktcol:ktcol + 128],
                       s=colap(SCkd[:], scol + ach), o=sl["kdec"][:]:
                       e.tensor_scalar(o, i, s, None, ALU.mult),
                       waits=qk_toks + [st.get("kd_used")] + sc_ready)
            t_wt = mm(None, bkB[:, 0:128], sl["rti"][:, 128:256], sl["tch"][cur][:], tok=True,
                      waits=[t_chain, t_r2, st.get("Bsq")])
            st["tch_used"] = t_wt
            st["r_used"] = t_wt
            t_wtn = vop("scalar", lambda e, o=sl["wtn"][:], i=bkB[:, 0:128]:
                        e.mul(o, i, -1.0), waits=[t_wt, st.get("wtn_used")])
            st["Bsq"] = t_wtn
            t_uw = mm(None, bkB[:, 256:512], sl["tch"][cur][:], sl["rti"][:, 0:256],
                      start=True, stop=False, tok=True,
                      waits=[t_chain, t_r1, t_r2, t_wtn, st.get("Buw")])
            t_vn = mm(None, bkB[:, 256:384], sl["wtn"][:], sl["s16"][:],
                      start=False, stop=True, tok=True, waits=[t_wtn] + st["sready"])
            st["wtn_used"] = t_vn
            t_vne = vop("vector", lambda e, o=sl["vnew"][:], i=bkB[:, 256:384]:
                        e.tensor_copy(o, i), waits=[t_vn, st.get("vn_used")])
            st["Buw"] = t_vne
            t_q = mm(None, bkA[:, 0:128], qhat[:, kcol:kcol + 128], sl["s16"][:], tok=True,
                     waits=qk_toks + st["sready"] + [st.get("A0")])
            t_qe = vop("scalar", lambda e, s=colap(SCeg[:], scol + ach), o=sl["qs_b"][:],
                       i=bkA[:, 0:128]:
                       e.activation(o, i, AF.Identity, scale=s),
                       waits=[t_q, st.get("qs_used")])
            st["A0"] = t_qe
            t_a = mm(None, bkA[:, 128:256], sl["alT"][:], sl["vnew"][:], tok=True,
                     waits=[t_vne, t_al, t_qe, st.get("A1")])
            st["al_used"] = t_a
            t_oc = vop("vector", lambda e, o=sl["ostrip"][:, n * 128:(n + 1) * 128],
                       i=sl["qs_b"][:], ps=bkA[:, 128:256]:
                       e.tensor_tensor(o, i, ps, ALU.add),
                       waits=[t_qe, t_a, sl["ospill"]])
            st["qs_used"] = t_oc
            st["A1"] = t_oc
            st["o_done"] = t_oc
            t_s = mm(None, bkA[:, 256:384], sl["kdec"][:], sl["vnew"][:], tok=True,
                     waits=[t_kd, t_vne, t_oc, st.get("A2")])
            st["vn_used"] = t_s
            st["kd_used"] = t_s
            t_su = vop("vector", lambda e, s=colap(SCgl[:], scol + ach), o=sl["s32"][:],
                       ps=bkA[:, 256:384]:
                       e.scalar_tensor_tensor(o, o, s, ps, ALU.mult, ALU.add),
                       waits=[t_s] + st["sready"])
            st["A2"] = t_su
            t_s16 = vop("gpsimd", lambda e, o=sl["s16"][:], i=sl["s32"][:]:
                        e.tensor_copy(o, i), waits=[t_su])
            st["s_used"] = t_su
            st["s16_used"] = t_s16
            st["sready"] = [t_su, t_s16]
            st["Abank_dve"] = t_su
            st["Abank_act"] = t_qe

        def delta_finish(sl):
            st = sl["st"]
            hv = st["hv"]
            s_idx = slots.index(sl)
            p.waits("gpsimd", [st["o_done"]])
            tsp = dma("gpsimd", "qo%d" % s_idx, o_h[hv].rearrange("(n q) d -> q n d", q=128),
                      sl["ostrip"][:].rearrange("q (n d) -> q n d", d=128))
            sl["ospill"] = tsp
            state["ospill"][s_idx] = tsp
            state["o_last"] = tsp
            return tsp

        phase_barrier([xw_spill_all[0], xw_spill_all[1], z_spill_all[0], t_wba])
        o_fin = state["ospill"]
        o_spill_last = None
        gy_free = a2_pe_done
        hat_guard_q = None
        hat_guard_k = None
        hat_guard_kt = None
        hat_guard_v = None
        for a in range(4):
            cq = conv_head(a, gy_free)
            q_tok = emit_hat(qhat, True, cq, hat_guard_q)
            gy_free = q_tok
            ck = conv_head(4 + a, gy_free)
            k_tok = emit_hat(khat, False, ck, hat_guard_k)
            kt_tok = emit_transposes(khat, khatT, [k_tok], hat_guard_kt)
            gy_free = k_tok
            qk_tok_max = [q_tok, k_tok, kt_tok]
            for vh in range(2):
                cv = conv_head(8 + 2 * a + vh, gy_free)
                vt_tok = emit_transposes(gy, vTt, cv, hat_guard_v)
                gy_free = vt_tok
                for jp in ((0, 1), (2, 3)):
                    for s_i, j in enumerate(jp):
                        delta_setup(slots[s_i], 8 * a + 4 * vh + j, a, j)
                    for n in range(NCH):
                        delta_chunk(slots[0], n, qk_tok_max, vt_tok)
                        delta_chunk(slots[1], n, qk_tok_max, vt_tok)
                    for s_i in range(2):
                        o_spill_last = delta_finish(slots[s_i])
                hat_guard_v = state["o_last"]
            hat_guard_q = state["o_last"]
            hat_guard_k = state["o_last"]
            hat_guard_kt = state["o_last"]

        stB.close()
        stSC.close()

        stC = ExitStack()
        wout16 = sb("wout16", [128, 32 * 2048], BF16, stC)
        ogfm = sb("ogfm", [128, 32 * 256], BF16, stC)
        ozin = sb("ozin", [128, 2048], BF16, stC)
        ozin2 = sb("ozin2", [128, 2048], BF16, stC)
        zh_f = sb("zh_f", [128, 2048], F32, stC)
        ogh_f = sb("ogh_f", [128, 2048], F32, stC)
        attnq = sb("attnq", [128, 16 * 256], F32, stC)
        lnbuf = [sb("lnbuf%d" % i, [128, 256], BF16, stC) for i in range(2)]
        xinc = sb("xinc", [128, 2048], F32, stC)
        msum = sb("msum", [128, 16], F32, stC)
        msr = sb("msr", [128, 16], F32, stC)
        msr2 = sb("msr2", [128, 16], F32, stC)
        x1bpc = sb("x1bpc", [128, 256], BF16, stC)

        phase_barrier([o_fin[0], o_fin[1], z_spill_all[0]] + bounce_fin)
        wout_toks = []
        for vc in range(32):
            p.waits("gpsimd", [a2_pe_done, o_fin[0], o_fin[1]])
            wout_toks.append(dma("gpsimd", "qwo", wout16[:, vc * 2048:(vc + 1) * 2048],
                                 wout_h[vc * 128:(vc + 1) * 128, :]))
        wout_last = wout_toks[-1]

        x1_spill_last = None
        x1f_last = None
        lnst_guard = None
        state_c = {}
        for oc in range(8):
            og_last = None
            for tbl in range(2):
                tb = oc * 2 + tbl
                for h2 in range(2):
                    p.waits("sync", [o_fin[0], o_fin[1], z_spill_all[0], og_last])
                    t_o = dma("sync", "qco", ozin[:].rearrange("q (h d) -> q h d", d=128),
                              o_h[h2 * 16:(h2 + 1) * 16, tb * 128:(tb + 1) * 128, :].rearrange(
                                  "h q d -> q h d"))
                    t_z = dma("sync", "qcz", ozin2[:].rearrange("q (h c) -> q h c", c=256),
                              z_h[h2 * 8:(h2 + 1) * 8, tb * 128:(tb + 1) * 128, :].rearrange(
                                  "h q c -> q h c"))
                    t_sz = vop("scalar", lambda e: e.activation(zh_f[:], ozin2[:], AF.Silu),
                               waits=[t_z, og_last, state_c.get("t_ms")])
                    t_og = vop("vector", lambda e: e.tensor_tensor(ogh_f[:], ozin[:], zh_f[:],
                                                                   ALU.mult),
                               waits=[t_sz, t_o, og_last])
                    t_sq = vop("scalar", lambda e: e.activation(zh_f[:], ogh_f[:], AF.Square),
                               waits=[t_og])
                    t_ms = vop("vector", lambda e:
                               e.tensor_reduce(msum[:], zh_f[:].rearrange("q (h d) -> q h d", d=128),
                                               AX.X, ALU.add), waits=[t_sq])
                    state_c["t_ms"] = t_ms
                    t_mq = vop("scalar", lambda e: e.activation(msr[:], msum[:], AF.Sqrt,
                                                                bias=1e-6, scale=1.0 / 128.0),
                               waits=[t_ms])
                    t_mr = vop("vector", lambda e: e.reciprocal(msr2[:], msr[:]), waits=[t_mq])
                    t_on = vop("vector", lambda e: e.tensor_tensor(
                        ogh_f[:].rearrange("q (h d) -> q h d", d=128),
                        ogh_f[:].rearrange("q (h d) -> q h d", d=128),
                        bass.AP(msr2.ap().tensor, 0, [msr2.ap().ap[0], [1, 16], [0, 128]]),
                        ALU.mult), waits=[t_mr])
                    for vcl in range(16):
                        vc = h2 * 16 + vcl
                        bk = vcl % 2
                        t_tr = tr(bk, pb[bk][:, 0:128], ogh_f[:, vcl * 128:(vcl + 1) * 128],
                                  idf[:], waits=[t_on])
                        og_last = vop("scalar", lambda e,
                                      o=ogfm[:, vc * 256 + tbl * 128: vc * 256 + (tbl + 1) * 128],
                                      i=pb[bk][:, 0:128]:
                                      e.activation(o, i, AF.Identity, scale=normw[:]),
                                      waits=[t_tr], bank=bk)
            attn_last = None
            for dc in range(16):
                bk = 2 + dc % 2
                p.waits("tensor", [og_last, wout_last])
                for vc in range(32):
                    t_mm = mm(bk if vc == 0 else None, pb[bk][:, 0:256],
                              wout16[:, vc * 2048 + dc * 128: vc * 2048 + dc * 128 + 128],
                              ogfm[:, vc * 256:(vc + 1) * 256], start=(vc == 0),
                              stop=(vc == 31), tok=(vc == 31))
                attn_last = vop("vector", lambda e, o=attnq[:, dc * 256:(dc + 1) * 256],
                                i=pb[bk][:, 0:256]: e.tensor_copy(o, i),
                                waits=[t_mm, x1_spill_last, x1f_last], bank=bk)
            res_last = None
            for tbl in range(2):
                tb = oc * 2 + tbl
                p.waits("sync", [res_last])
                t_x = dma("sync", "qcx", xinc[:], x_h[tb * 128:(tb + 1) * 128, :])
                for dc in range(16):
                    bk = 4 + dc % 2
                    t_tr = tr(bk, pb[bk][:, 0:128], xinc[:, dc * 128:(dc + 1) * 128], idf[:],
                              waits=[t_x, attn_last])
                    res_last = vop("vector", lambda e,
                                   o=attnq[:, dc * 256 + tbl * 128: dc * 256 + (tbl + 1) * 128],
                                   i=pb[bk][:, 0:128]:
                                   e.tensor_tensor(o, o, i, ALU.add), waits=[t_tr], bank=bk)
            mm_tok = [None, None]
            for dc in range(16):
                tc1 = vop("vector", lambda e, o=lnbuf[0][:], i=attnq[:, dc * 256:(dc + 1) * 256]:
                          e.tensor_copy(o, i), waits=[res_last, mm_tok[0], lnst_guard])
                mm_tok[0] = mm(0 if dc == 0 else None, pb[0][0:1, 0:256], onesb[:, 0:1],
                               lnbuf[0][:], start=(dc == 0), stop=(dc == 15), waits=[tc1],
                               tok=True)
                tc2 = vop("scalar", lambda e, o=lnbuf[1][:], i=attnq[:, dc * 256:(dc + 1) * 256]:
                          e.activation(o, i, AF.Square), waits=[res_last, mm_tok[1], lnst_guard])
                mm_tok[1] = mm(1 if dc == 0 else None, pb[1][0:1, 0:256], onesb[:, 0:1],
                               lnbuf[1][:], start=(dc == 0), stop=(dc == 15), waits=[tc2],
                               tok=True)
            t_mu = vop("vector", lambda e: e.tensor_scalar(murow[0:1, 0:256], pb[0][0:1, 0:256],
                                                           1.0 / 2048.0, None, ALU.mult),
                       waits=[mm_tok[0]], bank=0)
            t_m2 = vop("vector", lambda e: e.tensor_scalar(srow[0:1, 0:256], pb[1][0:1, 0:256],
                                                           1.0 / 2048.0, None, ALU.mult),
                       waits=[mm_tok[1]], bank=1)
            t_v1 = vop("vector", lambda e: e.tensor_tensor(s2row[0:1, 0:256], murow[0:1, 0:256],
                                                           murow[0:1, 0:256], ALU.mult),
                       waits=[t_mu])
            t_v2 = vop("vector", lambda e: e.tensor_sub(s2row[0:1, 0:256], srow[0:1, 0:256],
                                                        s2row[0:1, 0:256]), waits=[t_m2, t_v1])
            t_v3 = vop("scalar", lambda e: e.activation(s2row[0:1, 0:256], s2row[0:1, 0:256],
                                                        AF.Sqrt, bias=1e-5), waits=[t_v2])
            t_v4 = vop("vector", lambda e: e.reciprocal(invrow[0:1, 0:256], s2row[0:1, 0:256]),
                       waits=[t_v3])
            t_b1 = mm(2, pb[2][:, 0:256], onesrf[0:1, :], murow[0:1, 0:256], tok=True,
                      waits=[t_mu])
            t_b2 = mm(3, pb[3][:, 0:256], onesrf[0:1, :], invrow[0:1, 0:256], tok=True,
                      waits=[t_v4])
            ln_last = None
            for dc in range(16):
                t_n1 = vop("vector", lambda e, o=attnq[:, dc * 256:(dc + 1) * 256]:
                           e.tensor_sub(o, o, pb[2][:, 0:256]), waits=[t_b1, t_b2, mm_tok[0],
                                                                       mm_tok[1]])
                t_n2 = vop("vector", lambda e, o=attnq[:, dc * 256:(dc + 1) * 256]:
                           e.tensor_tensor(o, o, pb[3][:, 0:256], ALU.mult), waits=[t_n1])
                t_n3 = vop("scalar", lambda e, o=x1bpc[:], i=attnq[:, dc * 256:(dc + 1) * 256]:
                           e.copy(o, i), waits=[t_n2, x1_spill_last])
                p.waits("gpsimd", [t_n2, t_n3, x1f_last])
                x1f_last = dma("gpsimd", "qx1f", x1f_h[dc, :, oc * 256:(oc + 1) * 256],
                               attnq[:, dc * 256:(dc + 1) * 256])
                x1_spill_last = dma("gpsimd", "qx1b", x1b_h[dc, :, oc * 256:(oc + 1) * 256],
                                    x1bpc[:])
                ln_last = t_n2
            p.bank_guard[2] = ln_last
            p.bank_guard[3] = ln_last
            lnst_guard = ln_last

        stC.close()

        stD = ExitStack()
        x1bt = sb("x1bt", [128, 16 * 2048], BF16, stD)
        wgts = [sb("wgts%d" % i, [128, 16 * 512], BF16, stD) for i in range(2)]
        wuts = [sb("wuts%d" % i, [128, 16 * 512], BF16, stD) for i in range(2)]
        sg_scr = sb("sg_scr", [128, 512], F32, stD)
        act_d = [sb("act_d%d" % i, [128, 512], BF16, stD) for i in range(2)]
        asum_d = sb("asum_d", [1, 2048], F32, stD)

        phase_barrier([x1_spill_last, x1f_last])
        t_az = vop("vector", lambda e: e.memset(asum_d[:], 0.0))
        x1b_tok = None
        for dc in range(16):
            p.waits("sync", [x1_spill_last])
            x1b_tok = dma("sync", "qdx", x1bt[:, dc * 2048:(dc + 1) * 2048], x1b_h[dc, :, :])
        wg_use = [None, None]
        wg_tok = [None, None]
        act_spill_last = None
        t_as_prev = t_az
        t_sg_prev = None
        act_guard = [None, None]
        for ihs in range(16):
            par = ihs % 2
            p.waits("sync", [wg_use[par]] + bounce_fin)
            tw = None
            for hc in range(16):
                dma("sync", "qdw%d" % par, wgts[par][:, hc * 512:(hc + 1) * 512],
                    wg16_h[ihs * 512:(ihs + 1) * 512, hc * 128:(hc + 1) * 128],
                    transpose=True)
                tw = dma("sync", "qdw%d" % par, wuts[par][:, hc * 512:(hc + 1) * 512],
                         wu16_h[ihs * 512:(ihs + 1) * 512, hc * 128:(hc + 1) * 128],
                         transpose=True)
            wg_tok[par] = tw
            for icl in range(4):
                ic = ihs * 4 + icl
                for tq in range(4):
                    bg = tq % 2
                    bu = 2 + tq % 2
                    p.waits("tensor", [x1b_tok, tw])
                    for hc in range(16):
                        t_gm = mm(bg if hc == 0 else None, pb[bg][:, 0:512],
                                  wgts[par][:, hc * 512 + icl * 128: hc * 512 + (icl + 1) * 128],
                                  x1bt[:, hc * 2048 + tq * 512: hc * 2048 + (tq + 1) * 512],
                                  start=(hc == 0), stop=(hc == 15), tok=(hc == 15))
                    for hc in range(16):
                        t_um = mm(bu if hc == 0 else None, pb[bu][:, 0:512],
                                  wuts[par][:, hc * 512 + icl * 128: hc * 512 + (icl + 1) * 128],
                                  x1bt[:, hc * 2048 + tq * 512: hc * 2048 + (tq + 1) * 512],
                                  start=(hc == 0), stop=(hc == 15), tok=(hc == 15))
                    t_sg = vop("scalar", lambda e, i=pb[bg][:, 0:512], s=colap(sgt[:], ic):
                               e.activation(sg_scr[:], i, AF.Silu, scale=s),
                               waits=[t_gm, t_sg_prev], bank=bg)
                    apar = (ic * 4 + tq) % 2
                    t_ac = vop("vector", lambda e, o=act_d[apar][:], s=colap(sut[:], ic),
                               i=pb[bu][:, 0:512]:
                               e.scalar_tensor_tensor(o, sg_scr[:], s, i, ALU.mult, ALU.mult),
                               waits=[t_sg, t_um, act_guard[apar]], bank=bu)
                    t_sg_prev = t_ac
                    ba_ = 4 + tq % 2
                    t_am = mm(ba_, pb[ba_][0:1, 0:512], onesb[:, 0:1], act_d[apar][:], tok=True,
                              waits=[t_ac, t_as_prev])
                    t_as = vop("vector", lambda e, o=asum_d[0:1, tq * 512:(tq + 1) * 512],
                               i=pb[ba_][0:1, 0:512]:
                               e.tensor_tensor(o, o, i, ALU.add), waits=[t_am], bank=ba_)
                    t_as_prev = t_as
                    p.waits("gpsimd", [t_ac])
                    act_spill_last = dma("gpsimd", "qda%d" % apar,
                                         act_h[ic, :, tq * 512:(tq + 1) * 512],
                                         act_d[apar][:])
                    act_guard[apar] = act_spill_last
            wg_use[par] = t_um

        stD.close()

        stE = ExitStack()
        act_e = sb("act_e", [128, 64 * 512], BF16, stE)
        wdt_e = [sb("wdt_e%d" % i, [128, 512], BF16, stE) for i in range(2)]
        resq = sb("resq", [128, 16 * 512], F32, stE)
        x1f_e = [sb("x1f_e%d" % i, [128, 512], F32, stE) for i in range(2)]
        t1_e = sb("t1_e", [128, 512], F32, stE)
        asumbc = sb("asumbc", [128, 2048], F32, stE)
        lnb_e = [sb("lnb_e%d" % i, [128, 512], BF16, stE) for i in range(2)]
        outtm = [sb("outtm%d" % i, [128, 2048], F32, stE) for i in range(2)]

        phase_barrier([act_guard[0], act_guard[1]] + bounce_fin)
        ab_last = None
        y_fin = [None, None]
        x1f_guard = [None, None]
        for tq in range(4):
            bk = tq % 2
            t_bm = mm(bk, pb[bk][:, 0:512], onesrf[0:1, :], asum_d[0:1, tq * 512:(tq + 1) * 512],
                      tok=True, waits=[t_as_prev])
            ab_last = vop("vector", lambda e, o=asumbc[:, tq * 512:(tq + 1) * 512],
                          i=pb[bk][:, 0:512]: e.tensor_copy(o, i), waits=[t_bm], bank=bk)

        y_last = None
        act_ld = [None] * 64
        outtm_guard = [None, None]
        prev_tq_mm = None
        prev_tq_ev = None
        for tq in range(4):
            for ic in range(64):
                p.waits("sync", [act_guard[0], act_guard[1], prev_tq_mm])
                act_ld[ic] = dma("sync", "qea", act_e[:, ic * 512:(ic + 1) * 512],
                                 act_h[ic, :, tq * 512:(tq + 1) * 512])
            act_ready = act_ld[63]
            dn_last = prev_tq_ev
            for hq4 in range(4):
                wd_tok = [None, None]
                for ic in range(64):
                    par = ic % 2
                    p.waits("sync", [wd_tok[par]] + bounce_fin)
                    t_wi = dma("sync", "qew%d" % par, wdt_e[par][:],
                               wd16_h[hq4 * 512:(hq4 + 1) * 512, ic * 128:(ic + 1) * 128],
                               transpose=True)
                    p.waits("tensor", [t_wi, act_ready])
                    for h4 in range(4):
                        wd_tok[par] = mm(h4 if ic == 0 else None, pb[h4][:, 0:512],
                                         wdt_e[par][:, h4 * 128:(h4 + 1) * 128],
                                         act_e[:, ic * 512:(ic + 1) * 512],
                                         start=(ic == 0), stop=(ic == 63), tok=(h4 == 3))
                t_dmm = wd_tok[(63) % 2]
                for h4 in range(4):
                    hc = hq4 * 4 + h4
                    par = h4 % 2
                    p.waits("sync", [x1f_guard[par], x1f_last])
                    t_xf = dma("sync", "qef%d" % par, x1f_e[par][:],
                               x1f_h[hc, :, tq * 512:(tq + 1) * 512])
                    t_e1 = vop("vector", lambda e, ps=pb[h4][:, 0:512],
                               a=asumbc[:, tq * 512:(tq + 1) * 512]:
                               e.scalar_tensor_tensor(t1_e[:], a, -128.0, ps, ALU.mult, ALU.add),
                               waits=[t_dmm, ab_last, dn_last])
                    t_e2 = vop("vector", lambda e, o=resq[:, hc * 512:(hc + 1) * 512],
                               s=colap(sdt[:], hc), xf=x1f_e[par][:]:
                               e.scalar_tensor_tensor(o, t1_e[:], s, xf, ALU.mult, ALU.add),
                               waits=[t_e1, t_xf], bank=h4)
                    x1f_guard[par] = t_e2
                    dn_last = t_e2
            mm_tok = [None, None]
            for hc in range(16):
                tc1 = vop("vector", lambda e, o=lnb_e[0][:], i=resq[:, hc * 512:(hc + 1) * 512]:
                          e.tensor_copy(o, i), waits=[dn_last, mm_tok[0]])
                mm_tok[0] = mm(0 if hc == 0 else None, pb[0][0:1, 0:512], onesb[:, 0:1],
                               lnb_e[0][:], start=(hc == 0), stop=(hc == 15), waits=[tc1],
                               tok=True)
                tc2 = vop("scalar", lambda e, o=lnb_e[1][:], i=resq[:, hc * 512:(hc + 1) * 512]:
                          e.activation(o, i, AF.Square), waits=[dn_last, mm_tok[1]])
                mm_tok[1] = mm(1 if hc == 0 else None, pb[1][0:1, 0:512], onesb[:, 0:1],
                               lnb_e[1][:], start=(hc == 0), stop=(hc == 15), waits=[tc2],
                               tok=True)
            t_mu = vop("vector", lambda e: e.tensor_scalar(murow[0:1, 0:512], pb[0][0:1, 0:512],
                                                           1.0 / 2048.0, None, ALU.mult),
                       waits=[mm_tok[0]], bank=0)
            t_m2 = vop("vector", lambda e: e.tensor_scalar(srow[0:1, 0:512], pb[1][0:1, 0:512],
                                                           1.0 / 2048.0, None, ALU.mult),
                       waits=[mm_tok[1]], bank=1)
            t_v1 = vop("vector", lambda e: e.tensor_tensor(s2row[0:1, 0:512], murow[0:1, 0:512],
                                                           murow[0:1, 0:512], ALU.mult),
                       waits=[t_mu])
            t_v2 = vop("vector", lambda e: e.tensor_sub(s2row[0:1, 0:512], srow[0:1, 0:512],
                                                        s2row[0:1, 0:512]), waits=[t_m2, t_v1])
            t_v3 = vop("scalar", lambda e: e.activation(s2row[0:1, 0:512], s2row[0:1, 0:512],
                                                        AF.Sqrt, bias=1e-5), waits=[t_v2])
            t_v4 = vop("vector", lambda e: e.reciprocal(invrow[0:1, 0:512], s2row[0:1, 0:512]),
                       waits=[t_v3])
            t_b1 = mm(2, pb[2][:, 0:512], onesrf[0:1, :], murow[0:1, 0:512], tok=True,
                      waits=[t_mu])
            t_b2 = mm(3, pb[3][:, 0:512], onesrf[0:1, :], invrow[0:1, 0:512], tok=True,
                      waits=[t_v4])
            nm_last = None
            for hc in range(16):
                t_n1 = vop("vector", lambda e, o=resq[:, hc * 512:(hc + 1) * 512]:
                           e.tensor_sub(o, o, pb[2][:, 0:512]),
                           waits=[t_b1, t_b2, mm_tok[0], mm_tok[1]])
                nm_last = vop("vector", lambda e, o=resq[:, hc * 512:(hc + 1) * 512]:
                              e.tensor_tensor(o, o, pb[3][:, 0:512], ALU.mult), waits=[t_n1])
            p.bank_guard[2] = nm_last
            p.bank_guard[3] = nm_last
            for tbl in range(4):
                tb = tq * 4 + tbl
                opar = tbl % 2
                ev_last = None
                for dc in range(16):
                    bk = 4 + dc % 2
                    t_tr = tr(bk, pb[bk][:, 0:128],
                              resq[:, dc * 512 + tbl * 128: dc * 512 + (tbl + 1) * 128], idf[:],
                              waits=[nm_last])
                    ev_last = vop("scalar", lambda e, o=outtm[opar][:, dc * 128:(dc + 1) * 128],
                                  i=pb[bk][:, 0:128]: e.copy(o, i),
                                  waits=[t_tr, outtm_guard[opar]], bank=bk)
                p.waits("scalar", [ev_last])
                y_last = dma("scalar", "qy%d" % opar, y_h[tb * 128:(tb + 1) * 128, :],
                             outtm[opar][:])
                outtm_guard[opar] = y_last
                y_fin[opar] = y_last
            prev_tq_mm = mm_tok[1]
            prev_tq_ev = ev_last

        stE.close()

        p.waits("scalar", y_fin)

        for s in sorted(DMA_SEMS) + ["sp", "sv", "sa", "sgp"]:
            p.handles[s] = ctx.enter_context(nc.semaphore(s))
        block = ctx.enter_context(nc.Block())

        @block.tensor
        def _(e):
            for f in p.plan["tensor"]:
                f(e)

        @block.vector
        def _(e):
            for f in p.plan["vector"]:
                f(e)

        @block.scalar
        def _(e):
            for f in p.plan["scalar"]:
                f(e)

        @block.gpsimd
        def _(e):
            for f in p.plan["gpsimd"]:
                f(e)

        @block.sync
        def _(e):
            for f in p.plan["sync"]:
                f(e)

    return nc


_cached = {}


def kernel(**inputs):
    if "nc" not in _cached:
        _cached["nc"] = build_program()
    nc = _cached["nc"]
    x = np.ascontiguousarray(inputs["L0_attn_x"])
    dtbv = np.zeros((64, 1), np.float32)
    algv = np.zeros((64, 1), np.float32)
    dtb_in = np.asarray(inputs["L0_attn_dt_bias"], np.float32)
    alog_in = np.asarray(inputs["L0_attn_A_log"], np.float32)
    for hv in range(HV):
        ach = (hv // 2) * 4 + 2 + hv % 2
        dtbv[ach, 0] = dtb_in[hv]
        algv[ach, 0] = alog_in[hv]
    cwl = np.ascontiguousarray(
        np.asarray(inputs["L0_attn_conv_w"], np.float32).reshape(16, 4, 128, 4)
        .transpose(2, 0, 1, 3).reshape(128, 256))
    common = {
        "wqkvz": np.ascontiguousarray(inputs["L0_attn_W_qkvz"], dtype=np.float32),
        "wba": np.ascontiguousarray(inputs["L0_attn_W_ba"], dtype=np.float32),
        "cw": cwl,
        "dtb": dtbv,
        "alog": algv,
        "normw": np.ascontiguousarray(np.asarray(inputs["L0_attn_norm_w"], np.float32).reshape(128, 1)),
        "wout": np.ascontiguousarray(inputs["L0_attn_W_out"], dtype=np.float32),
        "gw": np.ascontiguousarray(inputs["L0_ffn_gate_w"], dtype=np.int32),
        "uw": np.ascontiguousarray(inputs["L0_ffn_up_w"], dtype=np.int32),
        "dw": np.ascontiguousarray(inputs["L0_ffn_down_w"], dtype=np.int32),
        "sgt": np.ascontiguousarray(np.asarray(inputs["L0_ffn_gate_scale"], np.float32).reshape(64, 128).T),
        "sut": np.ascontiguousarray(np.asarray(inputs["L0_ffn_up_scale"], np.float32).reshape(64, 128).T),
        "sdt": np.ascontiguousarray(np.asarray(inputs["L0_ffn_down_scale"], np.float32).reshape(16, 128).T),
    }
    in_maps = []
    for b in range(8):
        m = dict(common)
        m["x"] = np.ascontiguousarray(x[b])
        in_maps.append(m)
    res = run_bass_kernel_spmd(nc, in_maps, core_ids=list(range(8)))
    return np.stack([res.results[b]["y"] for b in range(8)], axis=0)


# version 04
# speedup vs baseline: 1.1178x; 1.1178x over previous
import numpy as np
import ml_dtypes
from contextlib import ExitStack

import concourse.bass as bass
import concourse.mybir as mybir
from concourse.bass_utils import run_bass_kernel_spmd

F32 = mybir.dt.float32
BF16 = mybir.dt.bfloat16
I32 = mybir.dt.int32
AF = mybir.ActivationFunctionType
ALU = mybir.AluOpType
AX = mybir.AxisListType

T = 2048
D = 2048
HK = 16
HV = 32
C = 128
NCH = T // C
BIG = 1e30
DEBUG = False


class Prog:
    ENGINES = ("tensor", "vector", "scalar", "gpsimd", "sync")

    def __init__(self):
        self.plan = {k: [] for k in self.ENGINES}
        self.cnt = {}
        self.handles = {}
        self.watermark = {k: {} for k in self.ENGINES}
        self.bank_guard = {}

    def op(self, eng, fn, inc=None):
        tok = None
        if inc is not None:
            name, amt = inc
            self.cnt[name] = self.cnt.get(name, 0) + amt
            tok = (name, self.cnt[name])

        def wrapped(e, fn=fn, inc=inc):
            inst = fn(e)
            if inc is not None:
                inst.then_inc(self.handles[inc[0]], inc[1])
        self.plan[eng].append(wrapped)
        return tok

    def wait(self, eng, tok):
        if tok is None:
            return
        name, val = tok
        wm = self.watermark[eng]
        if wm.get(name, 0) >= val:
            return
        wm[name] = val
        self.plan[eng].append(lambda e, n=name, v=val: e.wait_ge(self.handles[n], v))

    def waits(self, eng, toks):
        best = {}
        for t in toks:
            if t is None:
                continue
            n, v = t
            if best.get(n, -1) < v:
                best[n] = v
        for n, v in best.items():
            self.wait(eng, (n, v))


def build_program():
    nc = bass.Bass("TRN2", target_bir_lowering=False)

    def inp(name, shape, dt):
        return nc.dram_tensor(name, shape, dt, kind="ExternalInput").ap()

    x_h = inp("x", [T, D], F32)
    wqkvz_h = inp("wqkvz", [D, 12288], F32)
    wba_h = inp("wba", [D, 64], F32)
    cw_h = inp("cw", [128, 256], F32)
    dtb_h = inp("dtb", [64, 1], F32)
    alog_h = inp("alog", [64, 1], F32)
    normw_h = inp("normw", [128, 1], F32)
    wout_h = inp("wout", [4096, D], F32)
    gw_h = inp("gw", [8192, D], I32)
    uw_h = inp("uw", [8192, D], I32)
    dw_h = inp("dw", [D, 8192], I32)
    sg_h = inp("sgt", [128, 64], F32)
    su_h = inp("sut", [128, 64], F32)
    sd_h = inp("sdt", [128, 16], F32)

    y_h = nc.dram_tensor("y", [T, D], F32, kind="ExternalOutput").ap()

    scr_kind = "ExternalOutput" if DEBUG else "Internal"
    xwtm_h = nc.dram_tensor("xwtm", [HK, T, 512], BF16, kind="Internal").ap()
    z_h = nc.dram_tensor("zscr", [HK, T, 256], BF16, kind=scr_kind).ap()
    o_h = nc.dram_tensor("oscr", [HV, T, 128], BF16, kind=scr_kind).ap()
    act_h = nc.dram_tensor("actscr", [64, 128, T], BF16, kind="Internal").ap()
    x1f_h = nc.dram_tensor("x1fscr", [16, 128, T], F32, kind=scr_kind).ap()
    x1b_h = nc.dram_tensor("x1bscr", [16, 128, T], BF16, kind="Internal").ap()
    wg16_h = nc.dram_tensor("wg16", [8192, D], BF16, kind="Internal").ap()
    wu16_h = nc.dram_tensor("wu16", [8192, D], BF16, kind="Internal").ap()
    wd16_h = nc.dram_tensor("wd16", [D, 8192], BF16, kind="Internal").ap()

    idf_c = nc.inline_tensor(np.eye(128, dtype=np.float32), "idf_c").ap()
    idb_c = nc.inline_tensor(np.eye(128, dtype=ml_dtypes.bfloat16), "idb_c").ap()
    e127_c = nc.inline_tensor(
        np.broadcast_to((np.arange(128) == 127).astype(np.float32)[:, None], (128, 128)).copy(),
        "e127_c").ap()
    onesb_c = nc.inline_tensor(np.ones((128, 8), ml_dtypes.bfloat16), "onesb_c").ap()
    onesrb_c = nc.inline_tensor(np.ones((8, 128), ml_dtypes.bfloat16), "onesrb_c").ap()
    onesrf_c = nc.inline_tensor(np.ones((8, 128), np.float32), "onesrf_c").ap()
    ones64_c = nc.inline_tensor(np.ones((64, 128), np.float32), "ones64_c").ap()
    mN = np.where(np.tril(np.ones((C, C)), -1) > 0, 0.0, BIG).astype(np.float32)
    mAL = np.where(np.triu(np.ones((C, C)), 0) > 0, 0.0, BIG).astype(np.float32)
    maskN_c = nc.inline_tensor(mN, "maskN_c").ap()
    maskAL_c = nc.inline_tensor(mAL, "maskAL_c").ap()
    CONSTVALS = [128.0, 1.0 / 128.0, -1.0, 128.0e-6, 1e-6, 1e-5, 1.0, 0.0]
    cvals_c = nc.inline_tensor(
        np.broadcast_to(np.asarray(CONSTVALS, np.float32)[None, :], (128, len(CONSTVALS))).copy(),
        "cvals_c").ap()

    p = Prog()

    with ExitStack() as ctx:
        def sb(name, shape, dt, st=None):
            return (st or ctx).enter_context(nc.sbuf_tensor(name, shape, dt))

        idf = sb("idf", [128, 128], F32)
        idb = sb("idb", [128, 128], BF16)
        e127 = sb("e127", [128, 128], F32)
        onesb = sb("onesb", [128, 8], BF16)
        onesrb = sb("onesrb", [8, 128], BF16)
        onesrf = sb("onesrf", [8, 128], F32)
        ones64 = sb("ones64", [64, 128], F32)
        maskN = sb("maskN", [128, 128], F32)
        maskAL = sb("maskAL", [128, 128], F32)
        cw = sb("cw_s", [128, 256], F32)
        dtb = sb("dtb_s", [64, 1], F32)
        alog = sb("alog_s", [64, 1], F32)
        nega = sb("nega", [64, 1], F32)
        normw = sb("normw_s", [128, 1], F32)
        sgt = sb("sgt_s", [128, 64], F32)
        sut = sb("sut_s", [128, 64], F32)
        sdt = sb("sdt_s", [128, 16], F32)
        cvals = sb("cvals", [128, 8], F32)
        rows4 = sb("rows4", [1, 2048], F32)
        srow = rows4.ap()[:, 0:512]
        s2row = rows4.ap()[:, 512:1024]
        murow = rows4.ap()[:, 1024:1536]
        invrow = rows4.ap()[:, 1536:2048]

        stSC = ExitStack()
        SCb = sb("SCb", [128, 1024], F32, stSC)
        SCbneg = sb("SCbneg", [128, 1024], F32, stSC)
        SCg = sb("SCg", [128, 1024], F32, stSC)
        SCeg = sb("SCeg", [128, 1024], F32, stSC)
        SCbe = sb("SCbe", [128, 1024], F32, stSC)
        SCkd = sb("SCkd", [128, 1024], F32, stSC)
        SCgl = sb("SCgl", [128, 1024], F32, stSC)
        wbounce = [sb("wb%d" % i, [128, 2048], BF16, stSC) for i in range(2)]

        pb = [ctx.enter_context(nc.psum_tensor("pb%d" % i, [128, 512], F32)) for i in range(6)]
        pbh = [ctx.enter_context(nc.psum_tensor("pbh%d" % i, [128, 128], BF16)) for i in range(2)]

        DMA_SEMS = set()

        def dma(q, sem, outap, inap, transpose=False):
            DMA_SEMS.add(sem)
            return p.op(q, lambda e, o=outap, i=inap, tr=transpose:
                        e.dma_start(out=o, in_=i, transpose=tr),
                        inc=(sem, 16))

        def mm(bank, outap, lhsT, rhs, start=True, stop=True, tok=False, waits=()):
            w = list(waits)
            if start and bank is not None:
                g = p.bank_guard.get(bank)
                w += (g if isinstance(g, list) else [g])
            p.waits("tensor", w)
            return p.op("tensor", lambda e, o=outap, l=lhsT, r=rhs, s1=start, s2=stop:
                        e.matmul(o, l, r, start=s1, stop=s2),
                        inc=("sp", 1) if tok else None)

        def tr(bank, outap, inap, ident, tok=True, waits=()):
            g = p.bank_guard.get(bank)
            w = list(waits) + (g if isinstance(g, list) else [g])
            p.waits("tensor", w)
            return p.op("tensor", lambda e, o=outap, i=inap, d=ident:
                        e.transpose(o, i, d), inc=("sp", 1) if tok else None)

        def vop(eng, fn, tok=True, waits=(), bank=None):
            sem = {"vector": "sv", "scalar": "sa", "gpsimd": "sgp"}[eng]
            p.waits(eng, waits)
            t = p.op(eng, fn, inc=(sem, 1) if tok else None)
            if bank is not None:
                p.bank_guard[bank] = t
            return t

        def colap(tile_ap, col):
            return tile_ap[:, col:col + 1]

        def bcast_lhsT(tile_ap, col, n=128):
            base = tile_ap[:, col:col + 1]
            return bass.AP(base.tensor, base.offset, [base.ap[0], [0, n]])

        def phase_barrier(dma_toks=()):
            toks = [("sp", p.cnt.get("sp", 0)), ("sv", p.cnt.get("sv", 0)),
                    ("sa", p.cnt.get("sa", 0)), ("sgp", p.cnt.get("sgp", 0))]
            toks += [t for t in dma_toks if t is not None]
            for eng in ("tensor", "vector", "scalar", "gpsimd", "sync"):
                p.waits(eng, toks)

        for dst, src in ((idf, idf_c), (idb, idb_c), (e127, e127_c), (onesb, onesb_c),
                         (onesrb, onesrb_c), (onesrf, onesrf_c), (ones64, ones64_c),
                         (maskN, maskN_c), (maskAL, maskAL_c), (cvals, cvals_c),
                         (cw, cw_h), (dtb, dtb_h),
                         (alog, alog_h), (normw, normw_h), (sgt, sg_h), (sut, su_h),
                         (sdt, sd_h)):
            tok_consts = dma("sync", "qcst", dst[:], src)

        consts_tok = tok_consts
        for _eng in ("tensor", "vector", "scalar", "gpsimd"):
            p.waits(_eng, [consts_tok])
        for _i, _v in enumerate(CONSTVALS):
            nc.const_aps.register(F32, _v, cvals.ap()[:, _i:_i + 1])

        bounce_done = []
        prev_b = [None, None]
        pieces = ([(gw_h, wg16_h, s, 0) for s in range(64)]
                  + [(uw_h, wu16_h, s, 0) for s in range(64)]
                  + [(dw_h, wd16_h, s, q) for s in range(16) for q in range(4)])
        for pi, (srct, dstt, s, q) in enumerate(pieces):
            par = pi % 2
            p.waits("gpsimd", [prev_b[par]])
            t_in = dma("gpsimd", "qb%d" % par, wbounce[par][:],
                       srct[s * 128:(s + 1) * 128, q * 2048:(q + 1) * 2048])
            p.waits("gpsimd", [t_in])
            t_out = dma("gpsimd", "qb%d" % par, dstt[s * 128:(s + 1) * 128, q * 2048:(q + 1) * 2048],
                        wbounce[par][:])
            prev_b[par] = t_out
            bounce_done.append(t_out)
        bounce_fin = [prev_b[0], prev_b[1]]

        stA = ExitStack()
        xfm = sb("xfm", [128, 16 * 2048], BF16, stA)
        wbab = sb("wbab", [128, 16 * 64], BF16, stA)
        stA2 = ExitStack()
        whb = sb("whb", [128, 16 * 768], BF16, stA2)
        xwtm_b = [sb("xwtmb%d" % i, [128, 16 * 512], BF16, stA2) for i in range(2)]
        zstr = sb("zstr", [128, 16 * 256], BF16, stA2)
        xin = [sb("xin%d" % i, [128, 2048], F32, stA2) for i in range(2)]

        t_wba = dma("gpsimd", "qwba", wbab[:].rearrange("q (c w) -> q c w", w=64),
                    wba_h.rearrange("(c q) w -> q c w", q=128))
        consts_tok2 = t_wba
        for _eng in ("tensor", "vector", "scalar", "gpsimd"):
            p.waits(_eng, [consts_tok2])

        xin_used = [None, None]
        for tt in range(16):
            par = tt % 2
            p.waits("sync", [xin_used[par]])
            tin = dma("sync", "qx%d" % par, xin[par][:], x_h[tt * 128:(tt + 1) * 128, :])
            for dc in range(16):
                bk = 4 + dc % 2
                ttr = tr(bk, pb[bk][:, 0:128], xin[par][:, dc * 128:(dc + 1) * 128], idf[:],
                         waits=[tin])
                te = vop("scalar", lambda e,
                         o=xfm[:, dc * 2048 + tt * 128: dc * 2048 + (tt + 1) * 128],
                         i=pb[bk][:, 0:128]: e.copy(o, i), waits=[ttr], bank=bk)
            xin_used[par] = ttr
        xfm_ready = te

        wh_free = None
        xw_spill = [None, None]
        z_spill = None
        for h in range(HK):
            par = h % 2
            p.waits("gpsimd", [wh_free])
            wh_tok = dma("gpsimd", "qwh", whb[:].rearrange("q (c w) -> q c w", w=768),
                         wqkvz_h[:, 768 * h:768 * (h + 1)].rearrange("(c q) w -> q c w", q=128))
            ev_mixed = []
            ev_z = []
            for tt in range(16):
                bk = tt % 2
                p.waits("tensor", [xfm_ready, wh_tok])
                for dc in range(16):
                    t_mm = mm(bk if dc == 0 else None, pb[bk][:, 0:512],
                              xfm[:, dc * 2048 + tt * 128: dc * 2048 + (tt + 1) * 128],
                              whb[:, dc * 768: dc * 768 + 512],
                              start=(dc == 0), stop=(dc == 15), tok=(dc == 15))
                te = vop("vector", lambda e, o=xwtm_b[par][:, tt * 512:(tt + 1) * 512],
                         i=pb[bk][:, 0:512]: e.tensor_copy(o, i),
                         waits=[t_mm, xw_spill[par]], bank=bk)
                ev_mixed.append(te)
                bz = 2 + tt % 2
                for dc in range(16):
                    t_mmz = mm(bz if dc == 0 else None, pb[bz][:, 0:256],
                               xfm[:, dc * 2048 + tt * 128: dc * 2048 + (tt + 1) * 128],
                               whb[:, dc * 768 + 512: dc * 768 + 768],
                               start=(dc == 0), stop=(dc == 15), tok=(dc == 15))
                tez = vop("scalar", lambda e, o=zstr[:, tt * 256:(tt + 1) * 256],
                          i=pb[bz][:, 0:256]: e.copy(o, i),
                          waits=[t_mmz, z_spill], bank=bz)
                ev_z.append(tez)
            wh_free = t_mmz
            p.waits("gpsimd", ev_mixed + ev_z)
            xw_spill[par] = dma("gpsimd", "qxw%d" % par,
                                xwtm_h[h].rearrange("(tt q) c -> q tt c", q=128),
                                xwtm_b[par][:].rearrange("q (tt c) -> q tt c", c=512))
            z_spill = dma("gpsimd", "qz", z_h[h].rearrange("(tt q) c -> q tt c", q=128),
                          zstr[:].rearrange("q (tt c) -> q tt c", c=256))
        xw_spill_all = xw_spill[:]
        z_spill_all = [z_spill]

        stA2.close()
        bapk = [sb("bapk%d" % i, [128, 2048], F32, stA) for i in range(3)]
        bafm = bapk[0].ap()[0:64, :]
        bsig = bapk[0].ap()[64:128, :]
        gpre = bapk[1].ap()[0:64, :]
        gcum = bapk[1].ap()[64:128, :]
        spt = bapk[2].ap()[0:64, :]
        spt2 = bapk[2].ap()[64:128, :]

        ba_toks = []
        for ts4 in range(4):
            bk = ts4 % 2
            p.waits("tensor", [xfm_ready, t_wba])
            for dc in range(16):
                t_mm = mm(bk if dc == 0 else None, pb[bk][0:64, 0:512],
                          wbab[:, dc * 64:(dc + 1) * 64],
                          xfm[:, dc * 2048 + ts4 * 512: dc * 2048 + (ts4 + 1) * 512],
                          start=(dc == 0), stop=(dc == 15), tok=(dc == 15))
            te = vop("vector", lambda e, o=bafm[:, ts4 * 512:(ts4 + 1) * 512],
                     i=pb[bk][0:64, 0:512]: e.tensor_copy(o, i), waits=[t_mm], bank=bk)
            ba_toks.append(te)
        a2_pe_done = t_mm

        p.waits("scalar", ba_toks)
        t_sig = vop("scalar", lambda e: e.activation(bsig, bafm, AF.Sigmoid))
        t_sp1 = vop("scalar", lambda e: e.activation(spt, bafm, AF.Abs, bias=dtb[:]))
        t_sp2 = vop("scalar", lambda e: e.activation(spt2, spt, AF.Exp, scale=-1.0), waits=[t_sp1])
        t_sp3 = vop("scalar", lambda e: e.activation(gpre, spt2, AF.Ln, bias=1.0), waits=[t_sp2])
        t_sp4 = vop("scalar", lambda e: e.activation(spt, bafm, AF.Relu, bias=dtb[:]), waits=[t_sp2])
        t_na1 = vop("scalar", lambda e: e.activation(nega[:], alog[:], AF.Exp))
        t_sp5 = vop("vector", lambda e: e.tensor_add(gpre, gpre, spt), waits=[t_sp3, t_sp4])
        t_na2 = vop("vector", lambda e: e.tensor_scalar_mul(nega[:], nega[:], -1.0), waits=[t_na1])
        t_gp = vop("vector", lambda e: e.tensor_scalar(gpre, gpre, nega[:], None, ALU.mult),
                   waits=[t_sp5, t_na2])
        cs_toks = []
        for n in range(NCH):
            tcs = vop("vector", lambda e, o=gcum[:, n * 128:(n + 1) * 128],
                      i=gpre[:, n * 128:(n + 1) * 128]:
                      e.tensor_tensor_scan(o, i, ones64[:], 0.0, ALU.add, ALU.mult),
                      waits=[t_gp])
            cs_toks.append(tcs)
        sc_toks = []
        for n in range(NCH):
            bk = 4 + n % 2
            t1 = tr(bk, pb[bk][0:128, 0:64], bsig[:, n * 128:(n + 1) * 128],
                    idf[64:128, 64:128], waits=[t_sig])
            e1 = vop("scalar", lambda e, o=SCb[:, n * 64:(n + 1) * 64], i=pb[bk][0:128, 0:64]:
                     e.copy(o, i), waits=[t1], bank=bk)
            t2 = tr(bk, pb[bk][0:128, 64:128], gcum[:, n * 128:(n + 1) * 128],
                    idf[64:128, 64:128], waits=[cs_toks[n]])
            e2 = vop("scalar", lambda e, o=SCg[:, n * 64:(n + 1) * 64], i=pb[bk][0:128, 64:128]:
                     e.copy(o, i), waits=[t2], bank=bk)
            sc_toks += [e1, e2]
        t_scbn = vop("vector", lambda e: e.tensor_scalar_mul(SCbneg[:], SCb[:], -1.0), waits=sc_toks)
        t_sceg = vop("scalar", lambda e: e.activation(SCeg[:], SCg[:], AF.Exp), waits=sc_toks)
        gl_toks = []
        for half in range(2):
            bk = half % 2
            tm = mm(bk, pb[bk][:, 0:512], e127[:], SCg[:, half * 512:(half + 1) * 512],
                    tok=True, waits=sc_toks)
            te = vop("vector", lambda e, o=SCkd[:, half * 512:(half + 1) * 512], i=pb[bk][:, 0:512]:
                     e.tensor_copy(o, i), waits=[tm], bank=bk)
            gl_toks.append(te)
        t_gl2 = vop("scalar", lambda e: e.activation(SCgl[:], SCkd[:], AF.Exp), waits=gl_toks)
        t_kd1 = vop("vector", lambda e: e.tensor_sub(SCkd[:], SCkd[:], SCg[:]), waits=gl_toks + [t_gl2])
        t_kd2 = vop("scalar", lambda e: e.activation(SCkd[:], SCkd[:], AF.Exp), waits=[t_kd1])
        t_be = vop("vector", lambda e: e.tensor_tensor(
            SCbe[:].rearrange("q (n m j) -> q n m j", m=16, j=4)[:, :, :, 0:2],
            SCb[:].rearrange("q (n m j) -> q n m j", m=16, j=4)[:, :, :, 0:2],
            SCeg[:].rearrange("q (n m j) -> q n m j", m=16, j=4)[:, :, :, 2:4],
            ALU.mult), waits=[t_sceg] + sc_toks)
        sc_ready = [t_scbn, t_sceg, t_kd2, t_gl2, t_be]

        stA.close()

        stB = ExitStack()
        gy = sb("gy", [128, 4 * 2048], BF16, stB)
        qhat = sb("qhat", [128, 4 * 2048], BF16, stB)
        khat = sb("khat", [128, 4 * 2048], BF16, stB)
        khatT = sb("khatT", [128, 4 * 2048], BF16, stB)
        vTt = sb("vTt", [128, 8 * 2048], BF16, stB)
        acc_b = sb("acc_b", [128, 2048], F32, stB)
        sqb = sb("sqb", [128, 2048], BF16, stB)
        arow = sb("arow", [1, 2048], F32, stB)
        arowb = sb("arowb", [1, 2048], BF16, stB)
        slots = []
        for s in range(3):
            sl = {
                "xnxt": [sb("xnxt%d_%d" % (s, i), [128, 256], BF16, stB) for i in range(2)],
                "tch": [sb("tch%d_%d" % (s, i), [128, 128], BF16, stB) for i in range(2)],
                "rti": sb("rti%d" % s, [128, 256], BF16, stB),
                "wtn": sb("wtn%d" % s, [128, 128], BF16, stB),
                "vnew": sb("vnew%d" % s, [128, 128], BF16, stB),
                "kdec": sb("kdec%d" % s, [128, 128], BF16, stB),
                "alT": sb("alT%d" % s, [128, 128], BF16, stB),
                "s32": sb("s32_%d" % s, [128, 128], F32, stB),
                "s16": sb("s16_%d" % s, [128, 128], BF16, stB),
                "qs_b": sb("qs_b%d" % s, [128, 128], F32, stB),
                "dprN": sb("dprN%d" % s, [128, 128], F32, stB),
                "dprA": sb("dprA%d" % s, [128, 128], F32, stB),
                "decN": sb("decN%d" % s, [128, 128], F32, stB),
                "decA": sb("decA%d" % s, [128, 128], F32, stB),
                "ostrip": sb("ostrip%d" % s, [128, NCH * 128], BF16, stB),
                "bkA": pb[2 * s], "bkB": pb[2 * s + 1], "pbh_id": 6 + (s % 2),
                "st": {}, "ospill": None,
            }
            slots.append(sl)

        state = {"ospill": [None, None, None], "o_last": None}

        def conv_head(h, gy_free_tok):
            p.waits("gpsimd", [gy_free_tok, xw_spill_all[h % 2]])
            fold_toks = []
            for tt in range(16):
                gt = tt // 4
                po = (tt % 4) * 32
                tf = dma("gpsimd", "qfold",
                         gy[po:po + 32, gt * 2048:(gt + 1) * 2048].rearrange(
                             "q (r c) -> q r c", r=4),
                         xwtm_h[h, tt * 128:(tt + 1) * 128, :])
                fold_toks.append(tf)
            conv_toks = []
            for gt in range(4):
                gslice = gy[:, gt * 2048:(gt + 1) * 2048]
                w3 = colap(cw[:], h * 16 + gt * 4 + 3)
                tc = vop("vector", lambda e, o=acc_b[:], i=gslice, s=w3:
                         e.tensor_scalar(o, i, s, None, ALU.mult),
                         waits=[fold_toks[-1]] + conv_toks[-1:])
                for k in (2, 1, 0):
                    sft = 3 - k
                    wk = colap(cw[:], h * 16 + gt * 4 + k)
                    tc = vop("vector", lambda e, o=acc_b[:, sft:2048],
                             i=gslice[:, 0:2048 - sft], s=wk, a=acc_b[:, sft:2048]:
                             e.scalar_tensor_tensor(o, i, s, a, ALU.mult, ALU.add),
                             waits=[tc])
                ts_ = vop("scalar", lambda e, o=gslice, i=acc_b[:]:
                          e.activation(o, i, AF.Silu), waits=[tc])
                conv_toks.append(ts_)
            return conv_toks

        def emit_hat(dst_tile, is_q, gy_toks, dst_guard):
            last = None
            tre_prev = None
            tb_prev = None
            for loc in range(4):
                gslice = gy[:, loc * 2048:(loc + 1) * 2048]
                tq = vop("scalar", lambda e, o=sqb[:], i=gslice:
                         e.activation(o, i, AF.Square), waits=gy_toks + [last])
                mv_last = tre_prev
                for q4 in range(4):
                    tm4 = mm(4, pb[4][0:1, 0:512], onesb[:, 0:1], sqb[:, q4 * 512:(q4 + 1) * 512],
                             tok=True, waits=[tq, mv_last])
                    scale = 128.0 if is_q else 1.0
                    bias = 128.0e-6 if is_q else 1e-6
                    mv_last = vop("scalar", lambda e, o=arow[0:1, q4 * 512:(q4 + 1) * 512],
                                  i=pb[4][0:1, 0:512], s=scale, b=bias:
                                  e.activation(o, i, AF.Sqrt, bias=b, scale=s), waits=[tm4],
                                  bank=4)
                def _recip_bf(e):
                    with nc.allow_low_precision(reason="l2norm scale as bf16 matmul operand"):
                        return e.reciprocal(arowb[:], arow[:])
                trb = vop("vector", _recip_bf, waits=[mv_last, tb_prev])
                tre_prev = trb
                for q4 in range(4):
                    tb = mm(4, pb[4][:, 0:512], onesrb[0:1, :], arowb[0:1, q4 * 512:(q4 + 1) * 512],
                            tok=True, waits=[trb])
                    tb_prev = tb
                    last = vop("vector", lambda e,
                               o=dst_tile[:, loc * 2048 + q4 * 512: loc * 2048 + (q4 + 1) * 512],
                               i=gslice[:, q4 * 512:(q4 + 1) * 512], b=pb[4][:, 0:512]:
                               e.tensor_tensor(o, i, b, ALU.mult),
                               waits=[tb, dst_guard], bank=4)
            return last

        def emit_transposes(src_tile, dst_tile, src_toks, dst_guard):
            last = None
            for loc in range(4):
                for n in range(NCH):
                    bk = 6 + n % 2
                    tt_ = tr(bk, pbh[n % 2][:],
                             src_tile[:, loc * 2048 + n * 128: loc * 2048 + (n + 1) * 128],
                             idb[:], waits=src_toks + [dst_guard])
                    last = vop("scalar", lambda e,
                               o=dst_tile[:, (loc * NCH + n) * 128:(loc * NCH + n + 1) * 128],
                               i=pbh[n % 2][:]: e.copy(o, i), waits=[tt_], bank=bk)
            return last

        def delta_setup(sl, hv, a, vloc):
            st = sl["st"]
            s_idx = slots.index(sl)
            gA = p.bank_guard.get(2 * s_idx)
            gB = p.bank_guard.get(2 * s_idx + 1)
            for g in (gA if isinstance(gA, list) else [gA]):
                if g is not None:
                    st["Abank_dve"] = g if st.get("Abank_dve") is None else max(
                        st["Abank_dve"], g, key=lambda t: (t[0], t[1])) if False else g
            st["seedA"] = (gA if isinstance(gA, list) else [gA])
            st["seedB"] = (gB if isinstance(gB, list) else [gB])
            p.bank_guard[2 * s_idx] = None
            p.bank_guard[2 * s_idx + 1] = None
            st["hv"] = hv
            st["loc"] = hv // 2 - 4 * a
            st["ach"] = (hv // 2) * 4 + 2 + hv % 2
            st["bch"] = (hv // 2) * 4 + hv % 2
            st["vloc"] = vloc
            t_s0 = vop("gpsimd", lambda e, t=sl["s32"]: e.memset(t[:], 0.0),
                       waits=[st.get("s_used"), st.get("s16_used")])
            t_s1 = vop("gpsimd", lambda e, t=sl["s16"]: e.memset(t[:], 0.0),
                       waits=[st.get("s16_used")])
            st["sready"] = [t_s0, t_s1]

        def delta_chunk(sl, n, qk_toks, vt_tok):
            st = sl["st"]
            loc, ach, bch, vloc = st["loc"], st["ach"], st["bch"], st["vloc"]
            bkA, bkB = sl["bkA"], sl["bkB"]
            kcol = loc * 2048 + n * 128
            ktcol = (loc * NCH + n) * 128
            vtcol = (vloc * NCH + n) * 128
            scol = n * 64
            t_g1 = mm(None, bkA[:, 0:128], khat[:, kcol:kcol + 128], khat[:, kcol:kcol + 128],
                      tok=True, waits=qk_toks + [st.get("A0"), st.get("Abank_dve"),
                                                 st.get("Abank_act")]
                      + st.pop("seedA", []))
            t_g2 = mm(None, bkA[:, 128:256], khat[:, kcol:kcol + 128],
                      qhat[:, kcol:kcol + 128], tok=True, waits=[st.get("A1")])
            t_e = mm(None, bkA[:, 256:384], bcast_lhsT(SCg[:], scol + ach), idf[:],
                     tok=True, waits=sc_ready + [st.get("A2")])
            t_dn = vop("vector", lambda e, ps=bkA[:, 256:384], gc=colap(SCg[:], scol + ach),
                       o=sl["dprN"][:]:
                       e.scalar_tensor_tensor(o, ps, gc, maskN[:], ALU.subtract, ALU.add),
                       waits=[t_e, st.get("dprN_used")])
            t_dn2 = vop("scalar", lambda e, o=sl["decN"][:], i=sl["dprN"][:]:
                        e.activation(o, i, AF.Exp, scale=-1.0),
                        waits=[t_dn, st.get("decN_used")])
            st["dprN_used"] = t_dn2
            t_da = vop("vector", lambda e, ps=bkA[:, 256:384], gc=colap(SCg[:], scol + ach),
                       o=sl["dprA"][:]:
                       e.scalar_tensor_tensor(o, ps, gc, maskAL[:], ALU.subtract, ALU.subtract),
                       waits=[t_e, st.get("dprA_used"), t_dn])
            st["A2"] = t_da
            t_da2 = vop("scalar", lambda e, o=sl["decA"][:], i=sl["dprA"][:]:
                        e.activation(o, i, AF.Exp), waits=[t_da, st.get("decA_used")])
            st["dprA_used"] = t_da2
            xn0 = sl["xnxt"][0]
            t_xn = vop("vector", lambda e, ps=bkA[:, 0:128], bc=colap(SCbneg[:], scol + bch),
                       o=xn0[:, 0:128], d=sl["decN"][:]:
                       e.scalar_tensor_tensor(o, ps, bc, d, ALU.mult, ALU.mult),
                       waits=[t_g1, t_dn2, st.get("xn_used")])
            st["decN_used"] = t_xn
            st["A0"] = t_xn
            t_al = vop("vector", lambda e, ps=bkA[:, 128:256], o=sl["alT"][:], d=sl["decA"][:]:
                       e.tensor_tensor(o, ps, d, ALU.mult),
                       waits=[t_g2, t_da2, st.get("al_used")])
            st["decA_used"] = t_al
            st["A1"] = t_al
            ph = sl["pbh_id"]
            t_xt = tr(ph, pbh[ph - 6][:], xn0[:, 0:128], idb[:], waits=[t_xn])
            t_xte = vop("scalar", lambda e, o=xn0[:, 128:256], i=pbh[ph - 6][:]:
                        e.copy(o, i), waits=[t_xt], bank=ph)
            t_t0 = vop("vector", lambda e, o=sl["tch"][0][:], i=xn0[:, 128:256]:
                       e.tensor_add(o, i, idb[:]), waits=[t_xte, st.get("tch_used")])
            cur = 0
            t_prev_t = t_t0
            t_prev_x = [t_xn, t_xte]
            for lev in range(6):
                nxt = 1 - cur
                xc, xv = sl["xnxt"][cur], sl["xnxt"][nxt]
                t_sq1 = mm(None, bkB[:, 0:128], xc[:, 128:256], xc[:, 0:128],
                           tok=(lev == 5), waits=t_prev_x + [st.get("Bsq")]
                           + st.pop("seedB", []))
                if lev < 5:
                    t_sq2 = mm(None, bkB[:, 128:256], xc[:, 0:128], xc[:, 128:256], tok=True)
                    t_ev = vop("vector", lambda e, o=xv[:, 0:256], i=bkB[:, 0:256]:
                               e.tensor_copy(o, i), waits=[t_sq2])
                else:
                    t_ev = vop("vector", lambda e, o=xv[:, 0:128], i=bkB[:, 0:128]:
                               e.tensor_copy(o, i), waits=[t_sq1])
                st["Bsq"] = t_ev
                t_xi_last = t_ev
                mm(None, bkA[:, 384:512], xv[:, 0:128], sl["tch"][cur][:],
                   start=True, stop=False, waits=[t_ev, t_prev_t, st.get("A3")])
                t_ap = mm(None, bkA[:, 384:512], idb[:], sl["tch"][cur][:],
                          start=False, stop=True, tok=True)
                t_te = vop("scalar", lambda e, o=sl["tch"][nxt][:], i=bkA[:, 384:512]:
                           e.copy(o, i), waits=[t_ap])
                st["A3"] = t_te
                t_prev_t = t_te
                t_prev_x = [t_ev]
                cur = nxt
            t_chain = t_prev_t
            st["xn_used"] = t_xi_last
            t_r1 = vop("vector", lambda e, i=vTt[:, vtcol:vtcol + 128],
                       s=colap(SCb[:], scol + bch), o=sl["rti"][:, 0:128]:
                       e.tensor_scalar(o, i, s, None, ALU.mult),
                       waits=[vt_tok, st.get("r_used")] + sc_ready)
            t_r2 = vop("vector", lambda e, i=khatT[:, ktcol:ktcol + 128],
                       s=colap(SCbe[:], scol + bch), o=sl["rti"][:, 128:256]:
                       e.tensor_scalar(o, i, s, None, ALU.mult), waits=qk_toks)
            t_kd = vop("gpsimd", lambda e, i=khatT[:, ktcol:ktcol + 128],
                       s=colap(SCkd[:], scol + ach), o=sl["kdec"][:]:
                       e.tensor_scalar(o, i, s, None, ALU.mult),
                       waits=qk_toks + [st.get("kd_used")] + sc_ready)
            t_wt = mm(None, bkB[:, 0:128], sl["rti"][:, 128:256], sl["tch"][cur][:], tok=True,
                      waits=[t_chain, t_r2, st.get("Bsq")])
            st["tch_used"] = t_wt
            st["r_used"] = t_wt
            t_wtn = vop("scalar", lambda e, o=sl["wtn"][:], i=bkB[:, 0:128]:
                        e.mul(o, i, -1.0), waits=[t_wt, st.get("wtn_used")])
            st["Bsq"] = t_wtn
            t_uw = mm(None, bkB[:, 256:512], sl["tch"][cur][:], sl["rti"][:, 0:256],
                      start=True, stop=False, tok=True,
                      waits=[t_chain, t_r1, t_r2, t_wtn, st.get("Buw")])
            t_vn = mm(None, bkB[:, 256:384], sl["wtn"][:], sl["s16"][:],
                      start=False, stop=True, tok=True, waits=[t_wtn] + st["sready"])
            st["wtn_used"] = t_vn
            t_vne = vop("vector", lambda e, o=sl["vnew"][:], i=bkB[:, 256:384]:
                        e.tensor_copy(o, i), waits=[t_vn, st.get("vn_used")])
            st["Buw"] = t_vne
            t_q = mm(None, bkA[:, 0:128], qhat[:, kcol:kcol + 128], sl["s16"][:], tok=True,
                     waits=qk_toks + st["sready"] + [st.get("A0")])
            t_qe = vop("scalar", lambda e, s=colap(SCeg[:], scol + ach), o=sl["qs_b"][:],
                       i=bkA[:, 0:128]:
                       e.activation(o, i, AF.Identity, scale=s),
                       waits=[t_q, st.get("qs_used")])
            st["A0"] = t_qe
            t_a = mm(None, bkA[:, 128:256], sl["alT"][:], sl["vnew"][:], tok=True,
                     waits=[t_vne, t_al, t_qe, st.get("A1")])
            st["al_used"] = t_a
            t_oc = vop("vector", lambda e, o=sl["ostrip"][:, n * 128:(n + 1) * 128],
                       i=sl["qs_b"][:], ps=bkA[:, 128:256]:
                       e.tensor_tensor(o, i, ps, ALU.add),
                       waits=[t_qe, t_a, sl["ospill"]])
            st["qs_used"] = t_oc
            st["A1"] = t_oc
            st["o_done"] = t_oc
            t_s = mm(None, bkA[:, 256:384], sl["kdec"][:], sl["vnew"][:], tok=True,
                     waits=[t_kd, t_vne, t_oc, st.get("A2")])
            st["vn_used"] = t_s
            st["kd_used"] = t_s
            t_su = vop("vector", lambda e, s=colap(SCgl[:], scol + ach), o=sl["s32"][:],
                       ps=bkA[:, 256:384]:
                       e.scalar_tensor_tensor(o, o, s, ps, ALU.mult, ALU.add),
                       waits=[t_s] + st["sready"])
            st["A2"] = t_su
            t_s16 = vop("gpsimd", lambda e, o=sl["s16"][:], i=sl["s32"][:]:
                        e.tensor_copy(o, i), waits=[t_su])
            st["s_used"] = t_su
            st["s16_used"] = t_s16
            st["sready"] = [t_su, t_s16]
            st["Abank_dve"] = t_su
            st["Abank_act"] = t_qe

        def delta_finish(sl):
            st = sl["st"]
            hv = st["hv"]
            s_idx = slots.index(sl)
            p.bank_guard[2 * s_idx] = [st.get("Abank_dve"), st.get("Abank_act")]
            p.bank_guard[2 * s_idx + 1] = [st.get("Buw"), st.get("Bsq")]
            p.waits("gpsimd", [st["o_done"]])
            tsp = dma("gpsimd", "qo%d" % s_idx, o_h[hv].rearrange("(n q) d -> q n d", q=128),
                      sl["ostrip"][:].rearrange("q (n d) -> q n d", d=128))
            sl["ospill"] = tsp
            state["ospill"][s_idx] = tsp
            state["o_last"] = tsp
            return tsp

        phase_barrier([xw_spill_all[0], xw_spill_all[1], z_spill_all[0], t_wba])
        o_fin = state["ospill"]
        o_spill_last = None
        gy_free = a2_pe_done
        hat_guard_q = None
        hat_guard_k = None
        hat_guard_kt = None
        hat_guard_v = None
        for a in range(4):
            cq = conv_head(a, gy_free)
            q_tok = emit_hat(qhat, True, cq, hat_guard_q)
            gy_free = q_tok
            ck = conv_head(4 + a, gy_free)
            k_tok = emit_hat(khat, False, ck, hat_guard_k)
            kt_tok = emit_transposes(khat, khatT, [k_tok], hat_guard_kt)
            gy_free = k_tok
            qk_tok_max = [q_tok, k_tok, kt_tok]
            cv = conv_head(8 + 2 * a, gy_free)
            vt0 = emit_transposes(gy, vTt[:, 0:4 * NCH * 128], cv, hat_guard_v)
            gy_free = vt0
            grp = [(8 * a + 0, 0, vt0), (8 * a + 1, 1, vt0), (8 * a + 2, 2, vt0)]
            for s_i, (hv, vloc, vt) in enumerate(grp):
                delta_setup(slots[s_i], hv, a, vloc)
            for n in range(NCH):
                for s_i, (hv, vloc, vt) in enumerate(grp):
                    delta_chunk(slots[s_i], n, qk_tok_max, vt)
            for s_i in range(len(grp)):
                o_spill_last = delta_finish(slots[s_i])
            hat_guard_v = state["o_last"]
            cv = conv_head(8 + 2 * a + 1, gy_free)
            vt1 = emit_transposes(gy, vTt[:, 4 * NCH * 128:8 * NCH * 128], cv, hat_guard_v)
            gy_free = vt1
            grp = [(8 * a + 3, 3, vt0), (8 * a + 4, 4, vt1), (8 * a + 5, 5, vt1)]
            for s_i, (hv, vloc, vt) in enumerate(grp):
                delta_setup(slots[s_i], hv, a, vloc)
            for n in range(NCH):
                for s_i, (hv, vloc, vt) in enumerate(grp):
                    delta_chunk(slots[s_i], n, qk_tok_max, vt)
            for s_i in range(len(grp)):
                o_spill_last = delta_finish(slots[s_i])
            grp = [(8 * a + 6, 6, vt1), (8 * a + 7, 7, vt1)]
            for s_i, (hv, vloc, vt) in enumerate(grp):
                delta_setup(slots[s_i], hv, a, vloc)
            for n in range(NCH):
                for s_i, (hv, vloc, vt) in enumerate(grp):
                    delta_chunk(slots[s_i], n, qk_tok_max, vt)
            for s_i in range(len(grp)):
                o_spill_last = delta_finish(slots[s_i])
            hat_guard_v = state["o_last"]
            hat_guard_q = state["o_last"]
            hat_guard_k = state["o_last"]
            hat_guard_kt = state["o_last"]

        stB.close()
        stSC.close()

        stC = ExitStack()
        wout16 = sb("wout16", [128, 32 * 2048], BF16, stC)
        ogfm = sb("ogfm", [128, 32 * 256], BF16, stC)
        ozin = sb("ozin", [128, 2048], BF16, stC)
        ozin2 = sb("ozin2", [128, 2048], BF16, stC)
        zh_f = sb("zh_f", [128, 2048], F32, stC)
        ogh_f = sb("ogh_f", [128, 2048], F32, stC)
        attnq = sb("attnq", [128, 16 * 256], F32, stC)
        lnbuf = [sb("lnbuf%d" % i, [128, 256], BF16, stC) for i in range(2)]
        xinc = sb("xinc", [128, 2048], F32, stC)
        msum = sb("msum", [128, 16], F32, stC)
        msr = sb("msr", [128, 16], F32, stC)
        msr2 = sb("msr2", [128, 16], F32, stC)
        x1bpc = sb("x1bpc", [128, 256], BF16, stC)

        phase_barrier(o_fin + [z_spill_all[0]] + bounce_fin)
        wout_toks = []
        for vc in range(32):
            p.waits("gpsimd", [a2_pe_done] + o_fin)
            wout_toks.append(dma("gpsimd", "qwo", wout16[:, vc * 2048:(vc + 1) * 2048],
                                 wout_h[vc * 128:(vc + 1) * 128, :]))
        wout_last = wout_toks[-1]

        x1_spill_last = None
        x1f_last = None
        lnst_guard = None
        state_c = {}
        for oc in range(8):
            og_last = None
            for tbl in range(2):
                tb = oc * 2 + tbl
                for h2 in range(2):
                    p.waits("sync", o_fin + [z_spill_all[0], og_last])
                    t_o = dma("sync", "qco", ozin[:].rearrange("q (h d) -> q h d", d=128),
                              o_h[h2 * 16:(h2 + 1) * 16, tb * 128:(tb + 1) * 128, :].rearrange(
                                  "h q d -> q h d"))
                    t_z = dma("sync", "qcz", ozin2[:].rearrange("q (h c) -> q h c", c=256),
                              z_h[h2 * 8:(h2 + 1) * 8, tb * 128:(tb + 1) * 128, :].rearrange(
                                  "h q c -> q h c"))
                    t_sz = vop("scalar", lambda e: e.activation(zh_f[:], ozin2[:], AF.Silu),
                               waits=[t_z, og_last, state_c.get("t_ms")])
                    t_og = vop("vector", lambda e: e.tensor_tensor(ogh_f[:], ozin[:], zh_f[:],
                                                                   ALU.mult),
                               waits=[t_sz, t_o, og_last])
                    t_sq = vop("scalar", lambda e: e.activation(zh_f[:], ogh_f[:], AF.Square),
                               waits=[t_og])
                    t_ms = vop("vector", lambda e:
                               e.tensor_reduce(msum[:], zh_f[:].rearrange("q (h d) -> q h d", d=128),
                                               AX.X, ALU.add), waits=[t_sq])
                    state_c["t_ms"] = t_ms
                    t_mq = vop("scalar", lambda e: e.activation(msr[:], msum[:], AF.Sqrt,
                                                                bias=1e-6, scale=1.0 / 128.0),
                               waits=[t_ms])
                    t_mr = vop("vector", lambda e: e.reciprocal(msr2[:], msr[:]), waits=[t_mq])
                    t_on = vop("vector", lambda e: e.tensor_tensor(
                        ogh_f[:].rearrange("q (h d) -> q h d", d=128),
                        ogh_f[:].rearrange("q (h d) -> q h d", d=128),
                        bass.AP(msr2.ap().tensor, 0, [msr2.ap().ap[0], [1, 16], [0, 128]]),
                        ALU.mult), waits=[t_mr])
                    for vcl in range(16):
                        vc = h2 * 16 + vcl
                        bk = vcl % 2
                        t_tr = tr(bk, pb[bk][:, 0:128], ogh_f[:, vcl * 128:(vcl + 1) * 128],
                                  idf[:], waits=[t_on])
                        og_last = vop("scalar", lambda e,
                                      o=ogfm[:, vc * 256 + tbl * 128: vc * 256 + (tbl + 1) * 128],
                                      i=pb[bk][:, 0:128]:
                                      e.activation(o, i, AF.Identity, scale=normw[:]),
                                      waits=[t_tr], bank=bk)
            attn_last = None
            for dc in range(16):
                bk = 2 + dc % 2
                p.waits("tensor", [og_last, wout_last])
                for vc in range(32):
                    t_mm = mm(bk if vc == 0 else None, pb[bk][:, 0:256],
                              wout16[:, vc * 2048 + dc * 128: vc * 2048 + dc * 128 + 128],
                              ogfm[:, vc * 256:(vc + 1) * 256], start=(vc == 0),
                              stop=(vc == 31), tok=(vc == 31))
                attn_last = vop("vector", lambda e, o=attnq[:, dc * 256:(dc + 1) * 256],
                                i=pb[bk][:, 0:256]: e.tensor_copy(o, i),
                                waits=[t_mm, x1_spill_last, x1f_last], bank=bk)
            res_last = None
            for tbl in range(2):
                tb = oc * 2 + tbl
                p.waits("sync", [res_last])
                t_x = dma("sync", "qcx", xinc[:], x_h[tb * 128:(tb + 1) * 128, :])
                for dc in range(16):
                    bk = 4 + dc % 2
                    t_tr = tr(bk, pb[bk][:, 0:128], xinc[:, dc * 128:(dc + 1) * 128], idf[:],
                              waits=[t_x, attn_last])
                    res_last = vop("vector", lambda e,
                                   o=attnq[:, dc * 256 + tbl * 128: dc * 256 + (tbl + 1) * 128],
                                   i=pb[bk][:, 0:128]:
                                   e.tensor_tensor(o, o, i, ALU.add), waits=[t_tr], bank=bk)
            mm_tok = [None, None]
            for dc in range(16):
                tc1 = vop("vector", lambda e, o=lnbuf[0][:], i=attnq[:, dc * 256:(dc + 1) * 256]:
                          e.tensor_copy(o, i), waits=[res_last, mm_tok[0], lnst_guard])
                mm_tok[0] = mm(0 if dc == 0 else None, pb[0][0:1, 0:256], onesb[:, 0:1],
                               lnbuf[0][:], start=(dc == 0), stop=(dc == 15), waits=[tc1],
                               tok=True)
                tc2 = vop("scalar", lambda e, o=lnbuf[1][:], i=attnq[:, dc * 256:(dc + 1) * 256]:
                          e.activation(o, i, AF.Square), waits=[res_last, mm_tok[1], lnst_guard])
                mm_tok[1] = mm(1 if dc == 0 else None, pb[1][0:1, 0:256], onesb[:, 0:1],
                               lnbuf[1][:], start=(dc == 0), stop=(dc == 15), waits=[tc2],
                               tok=True)
            t_mu = vop("vector", lambda e: e.tensor_scalar(murow[0:1, 0:256], pb[0][0:1, 0:256],
                                                           1.0 / 2048.0, None, ALU.mult),
                       waits=[mm_tok[0]], bank=0)
            t_m2 = vop("vector", lambda e: e.tensor_scalar(srow[0:1, 0:256], pb[1][0:1, 0:256],
                                                           1.0 / 2048.0, None, ALU.mult),
                       waits=[mm_tok[1]], bank=1)
            t_v1 = vop("vector", lambda e: e.tensor_tensor(s2row[0:1, 0:256], murow[0:1, 0:256],
                                                           murow[0:1, 0:256], ALU.mult),
                       waits=[t_mu])
            t_v2 = vop("vector", lambda e: e.tensor_sub(s2row[0:1, 0:256], srow[0:1, 0:256],
                                                        s2row[0:1, 0:256]), waits=[t_m2, t_v1])
            t_v3 = vop("scalar", lambda e: e.activation(s2row[0:1, 0:256], s2row[0:1, 0:256],
                                                        AF.Sqrt, bias=1e-5), waits=[t_v2])
            t_v4 = vop("vector", lambda e: e.reciprocal(invrow[0:1, 0:256], s2row[0:1, 0:256]),
                       waits=[t_v3])
            t_b1 = mm(2, pb[2][:, 0:256], onesrf[0:1, :], murow[0:1, 0:256], tok=True,
                      waits=[t_mu])
            t_b2 = mm(3, pb[3][:, 0:256], onesrf[0:1, :], invrow[0:1, 0:256], tok=True,
                      waits=[t_v4])
            ln_last = None
            for dc in range(16):
                t_n1 = vop("vector", lambda e, o=attnq[:, dc * 256:(dc + 1) * 256]:
                           e.tensor_sub(o, o, pb[2][:, 0:256]), waits=[t_b1, t_b2, mm_tok[0],
                                                                       mm_tok[1]])
                t_n2 = vop("vector", lambda e, o=attnq[:, dc * 256:(dc + 1) * 256]:
                           e.tensor_tensor(o, o, pb[3][:, 0:256], ALU.mult), waits=[t_n1])
                t_n3 = vop("scalar", lambda e, o=x1bpc[:], i=attnq[:, dc * 256:(dc + 1) * 256]:
                           e.copy(o, i), waits=[t_n2, x1_spill_last])
                p.waits("gpsimd", [t_n2, t_n3, x1f_last])
                x1f_last = dma("gpsimd", "qx1f", x1f_h[dc, :, oc * 256:(oc + 1) * 256],
                               attnq[:, dc * 256:(dc + 1) * 256])
                x1_spill_last = dma("gpsimd", "qx1b", x1b_h[dc, :, oc * 256:(oc + 1) * 256],
                                    x1bpc[:])
                ln_last = t_n2
            p.bank_guard[2] = ln_last
            p.bank_guard[3] = ln_last
            lnst_guard = ln_last

        stC.close()

        stD = ExitStack()
        x1bt = sb("x1bt", [128, 16 * 2048], BF16, stD)
        wgts = [sb("wgts%d" % i, [128, 16 * 512], BF16, stD) for i in range(2)]
        wuts = [sb("wuts%d" % i, [128, 16 * 512], BF16, stD) for i in range(2)]
        sg_scr = sb("sg_scr", [128, 512], F32, stD)
        act_d = [sb("act_d%d" % i, [128, 512], BF16, stD) for i in range(2)]
        asum_d = sb("asum_d", [1, 2048], F32, stD)

        phase_barrier([x1_spill_last, x1f_last])
        t_az = vop("vector", lambda e: e.memset(asum_d[:], 0.0))
        x1b_tok = None
        for dc in range(16):
            p.waits("sync", [x1_spill_last])
            x1b_tok = dma("sync", "qdx", x1bt[:, dc * 2048:(dc + 1) * 2048], x1b_h[dc, :, :])
        wg_use = [None, None]
        wg_tok = [None, None]
        act_spill_last = None
        t_as_prev = t_az
        t_sg_prev = None
        act_guard = [None, None]
        for ihs in range(16):
            par = ihs % 2
            p.waits("sync", [wg_use[par]] + bounce_fin)
            tw = None
            for hc in range(16):
                dma("sync", "qdw%d" % par, wgts[par][:, hc * 512:(hc + 1) * 512],
                    wg16_h[ihs * 512:(ihs + 1) * 512, hc * 128:(hc + 1) * 128],
                    transpose=True)
                tw = dma("sync", "qdw%d" % par, wuts[par][:, hc * 512:(hc + 1) * 512],
                         wu16_h[ihs * 512:(ihs + 1) * 512, hc * 128:(hc + 1) * 128],
                         transpose=True)
            wg_tok[par] = tw
            for icl in range(4):
                ic = ihs * 4 + icl
                for tq in range(4):
                    bg = tq % 2
                    bu = 2 + tq % 2
                    p.waits("tensor", [x1b_tok, tw])
                    for hc in range(16):
                        t_gm = mm(bg if hc == 0 else None, pb[bg][:, 0:512],
                                  wgts[par][:, hc * 512 + icl * 128: hc * 512 + (icl + 1) * 128],
                                  x1bt[:, hc * 2048 + tq * 512: hc * 2048 + (tq + 1) * 512],
                                  start=(hc == 0), stop=(hc == 15), tok=(hc == 15))
                    for hc in range(16):
                        t_um = mm(bu if hc == 0 else None, pb[bu][:, 0:512],
                                  wuts[par][:, hc * 512 + icl * 128: hc * 512 + (icl + 1) * 128],
                                  x1bt[:, hc * 2048 + tq * 512: hc * 2048 + (tq + 1) * 512],
                                  start=(hc == 0), stop=(hc == 15), tok=(hc == 15))
                    t_sg = vop("scalar", lambda e, i=pb[bg][:, 0:512], s=colap(sgt[:], ic):
                               e.activation(sg_scr[:], i, AF.Silu, scale=s),
                               waits=[t_gm, t_sg_prev], bank=bg)
                    apar = (ic * 4 + tq) % 2
                    t_ac = vop("vector", lambda e, o=act_d[apar][:], s=colap(sut[:], ic),
                               i=pb[bu][:, 0:512]:
                               e.scalar_tensor_tensor(o, sg_scr[:], s, i, ALU.mult, ALU.mult),
                               waits=[t_sg, t_um, act_guard[apar]], bank=bu)
                    t_sg_prev = t_ac
                    ba_ = 4 + tq % 2
                    t_am = mm(ba_, pb[ba_][0:1, 0:512], onesb[:, 0:1], act_d[apar][:], tok=True,
                              waits=[t_ac, t_as_prev])
                    t_as = vop("vector", lambda e, o=asum_d[0:1, tq * 512:(tq + 1) * 512],
                               i=pb[ba_][0:1, 0:512]:
                               e.tensor_tensor(o, o, i, ALU.add), waits=[t_am], bank=ba_)
                    t_as_prev = t_as
                    p.waits("gpsimd", [t_ac])
                    act_spill_last = dma("gpsimd", "qda%d" % apar,
                                         act_h[ic, :, tq * 512:(tq + 1) * 512],
                                         act_d[apar][:])
                    act_guard[apar] = act_spill_last
            wg_use[par] = t_um

        stD.close()

        stE = ExitStack()
        act_e = sb("act_e", [128, 64 * 512], BF16, stE)
        wdt_e = [sb("wdt_e%d" % i, [128, 512], BF16, stE) for i in range(2)]
        resq = sb("resq", [128, 16 * 512], F32, stE)
        x1f_e = [sb("x1f_e%d" % i, [128, 512], F32, stE) for i in range(2)]
        t1_e = sb("t1_e", [128, 512], F32, stE)
        asumbc = sb("asumbc", [128, 2048], F32, stE)
        lnb_e = [sb("lnb_e%d" % i, [128, 512], BF16, stE) for i in range(2)]
        outtm = [sb("outtm%d" % i, [128, 2048], F32, stE) for i in range(2)]

        phase_barrier([act_guard[0], act_guard[1]] + bounce_fin)
        ab_last = None
        y_fin = [None, None]
        x1f_guard = [None, None]
        for tq in range(4):
            bk = tq % 2
            t_bm = mm(bk, pb[bk][:, 0:512], onesrf[0:1, :], asum_d[0:1, tq * 512:(tq + 1) * 512],
                      tok=True, waits=[t_as_prev])
            ab_last = vop("vector", lambda e, o=asumbc[:, tq * 512:(tq + 1) * 512],
                          i=pb[bk][:, 0:512]: e.tensor_copy(o, i), waits=[t_bm], bank=bk)

        y_last = None
        act_ld = [None] * 64
        outtm_guard = [None, None]
        prev_tq_mm = None
        prev_tq_ev = None
        for tq in range(4):
            for ic in range(64):
                p.waits("sync", [act_guard[0], act_guard[1], prev_tq_mm])
                act_ld[ic] = dma("sync", "qea", act_e[:, ic * 512:(ic + 1) * 512],
                                 act_h[ic, :, tq * 512:(tq + 1) * 512])
            act_ready = act_ld[63]
            dn_last = prev_tq_ev
            for hq4 in range(4):
                wd_tok = [None, None]
                for ic in range(64):
                    par = ic % 2
                    p.waits("sync", [wd_tok[par]] + bounce_fin)
                    t_wi = dma("sync", "qew%d" % par, wdt_e[par][:],
                               wd16_h[hq4 * 512:(hq4 + 1) * 512, ic * 128:(ic + 1) * 128],
                               transpose=True)
                    p.waits("tensor", [t_wi, act_ready])
                    for h4 in range(4):
                        wd_tok[par] = mm(h4 if ic == 0 else None, pb[h4][:, 0:512],
                                         wdt_e[par][:, h4 * 128:(h4 + 1) * 128],
                                         act_e[:, ic * 512:(ic + 1) * 512],
                                         start=(ic == 0), stop=(ic == 63), tok=(h4 == 3))
                t_dmm = wd_tok[(63) % 2]
                for h4 in range(4):
                    hc = hq4 * 4 + h4
                    par = h4 % 2
                    p.waits("sync", [x1f_guard[par], x1f_last])
                    t_xf = dma("sync", "qef%d" % par, x1f_e[par][:],
                               x1f_h[hc, :, tq * 512:(tq + 1) * 512])
                    t_e1 = vop("vector", lambda e, ps=pb[h4][:, 0:512],
                               a=asumbc[:, tq * 512:(tq + 1) * 512]:
                               e.scalar_tensor_tensor(t1_e[:], a, -128.0, ps, ALU.mult, ALU.add),
                               waits=[t_dmm, ab_last, dn_last])
                    t_e2 = vop("vector", lambda e, o=resq[:, hc * 512:(hc + 1) * 512],
                               s=colap(sdt[:], hc), xf=x1f_e[par][:]:
                               e.scalar_tensor_tensor(o, t1_e[:], s, xf, ALU.mult, ALU.add),
                               waits=[t_e1, t_xf], bank=h4)
                    x1f_guard[par] = t_e2
                    dn_last = t_e2
            mm_tok = [None, None]
            for hc in range(16):
                tc1 = vop("vector", lambda e, o=lnb_e[0][:], i=resq[:, hc * 512:(hc + 1) * 512]:
                          e.tensor_copy(o, i), waits=[dn_last, mm_tok[0]])
                mm_tok[0] = mm(0 if hc == 0 else None, pb[0][0:1, 0:512], onesb[:, 0:1],
                               lnb_e[0][:], start=(hc == 0), stop=(hc == 15), waits=[tc1],
                               tok=True)
                tc2 = vop("scalar", lambda e, o=lnb_e[1][:], i=resq[:, hc * 512:(hc + 1) * 512]:
                          e.activation(o, i, AF.Square), waits=[dn_last, mm_tok[1]])
                mm_tok[1] = mm(1 if hc == 0 else None, pb[1][0:1, 0:512], onesb[:, 0:1],
                               lnb_e[1][:], start=(hc == 0), stop=(hc == 15), waits=[tc2],
                               tok=True)
            t_mu = vop("vector", lambda e: e.tensor_scalar(murow[0:1, 0:512], pb[0][0:1, 0:512],
                                                           1.0 / 2048.0, None, ALU.mult),
                       waits=[mm_tok[0]], bank=0)
            t_m2 = vop("vector", lambda e: e.tensor_scalar(srow[0:1, 0:512], pb[1][0:1, 0:512],
                                                           1.0 / 2048.0, None, ALU.mult),
                       waits=[mm_tok[1]], bank=1)
            t_v1 = vop("vector", lambda e: e.tensor_tensor(s2row[0:1, 0:512], murow[0:1, 0:512],
                                                           murow[0:1, 0:512], ALU.mult),
                       waits=[t_mu])
            t_v2 = vop("vector", lambda e: e.tensor_sub(s2row[0:1, 0:512], srow[0:1, 0:512],
                                                        s2row[0:1, 0:512]), waits=[t_m2, t_v1])
            t_v3 = vop("scalar", lambda e: e.activation(s2row[0:1, 0:512], s2row[0:1, 0:512],
                                                        AF.Sqrt, bias=1e-5), waits=[t_v2])
            t_v4 = vop("vector", lambda e: e.reciprocal(invrow[0:1, 0:512], s2row[0:1, 0:512]),
                       waits=[t_v3])
            t_b1 = mm(2, pb[2][:, 0:512], onesrf[0:1, :], murow[0:1, 0:512], tok=True,
                      waits=[t_mu])
            t_b2 = mm(3, pb[3][:, 0:512], onesrf[0:1, :], invrow[0:1, 0:512], tok=True,
                      waits=[t_v4])
            nm_last = None
            for hc in range(16):
                t_n1 = vop("vector", lambda e, o=resq[:, hc * 512:(hc + 1) * 512]:
                           e.tensor_sub(o, o, pb[2][:, 0:512]),
                           waits=[t_b1, t_b2, mm_tok[0], mm_tok[1]])
                nm_last = vop("vector", lambda e, o=resq[:, hc * 512:(hc + 1) * 512]:
                              e.tensor_tensor(o, o, pb[3][:, 0:512], ALU.mult), waits=[t_n1])
            p.bank_guard[2] = nm_last
            p.bank_guard[3] = nm_last
            for tbl in range(4):
                tb = tq * 4 + tbl
                opar = tbl % 2
                ev_last = None
                for dc in range(16):
                    bk = 4 + dc % 2
                    t_tr = tr(bk, pb[bk][:, 0:128],
                              resq[:, dc * 512 + tbl * 128: dc * 512 + (tbl + 1) * 128], idf[:],
                              waits=[nm_last])
                    ev_last = vop("scalar", lambda e, o=outtm[opar][:, dc * 128:(dc + 1) * 128],
                                  i=pb[bk][:, 0:128]: e.copy(o, i),
                                  waits=[t_tr, outtm_guard[opar]], bank=bk)
                p.waits("scalar", [ev_last])
                y_last = dma("scalar", "qy%d" % opar, y_h[tb * 128:(tb + 1) * 128, :],
                             outtm[opar][:])
                outtm_guard[opar] = y_last
                y_fin[opar] = y_last
            prev_tq_mm = mm_tok[1]
            prev_tq_ev = ev_last

        stE.close()

        p.waits("scalar", y_fin)

        for s in sorted(DMA_SEMS) + ["sp", "sv", "sa", "sgp"]:
            p.handles[s] = ctx.enter_context(nc.semaphore(s))
        block = ctx.enter_context(nc.Block())

        @block.tensor
        def _(e):
            for f in p.plan["tensor"]:
                f(e)

        @block.vector
        def _(e):
            for f in p.plan["vector"]:
                f(e)

        @block.scalar
        def _(e):
            for f in p.plan["scalar"]:
                f(e)

        @block.gpsimd
        def _(e):
            for f in p.plan["gpsimd"]:
                f(e)

        @block.sync
        def _(e):
            for f in p.plan["sync"]:
                f(e)

    return nc


_cached = {}


def kernel(**inputs):
    if "nc" not in _cached:
        _cached["nc"] = build_program()
    nc = _cached["nc"]
    x = np.ascontiguousarray(inputs["L0_attn_x"])
    dtbv = np.zeros((64, 1), np.float32)
    algv = np.zeros((64, 1), np.float32)
    dtb_in = np.asarray(inputs["L0_attn_dt_bias"], np.float32)
    alog_in = np.asarray(inputs["L0_attn_A_log"], np.float32)
    for hv in range(HV):
        ach = (hv // 2) * 4 + 2 + hv % 2
        dtbv[ach, 0] = dtb_in[hv]
        algv[ach, 0] = alog_in[hv]
    cwl = np.ascontiguousarray(
        np.asarray(inputs["L0_attn_conv_w"], np.float32).reshape(16, 4, 128, 4)
        .transpose(2, 0, 1, 3).reshape(128, 256))
    common = {
        "wqkvz": np.ascontiguousarray(inputs["L0_attn_W_qkvz"], dtype=np.float32),
        "wba": np.ascontiguousarray(inputs["L0_attn_W_ba"], dtype=np.float32),
        "cw": cwl,
        "dtb": dtbv,
        "alog": algv,
        "normw": np.ascontiguousarray(np.asarray(inputs["L0_attn_norm_w"], np.float32).reshape(128, 1)),
        "wout": np.ascontiguousarray(inputs["L0_attn_W_out"], dtype=np.float32),
        "gw": np.ascontiguousarray(inputs["L0_ffn_gate_w"], dtype=np.int32),
        "uw": np.ascontiguousarray(inputs["L0_ffn_up_w"], dtype=np.int32),
        "dw": np.ascontiguousarray(inputs["L0_ffn_down_w"], dtype=np.int32),
        "sgt": np.ascontiguousarray(np.asarray(inputs["L0_ffn_gate_scale"], np.float32).reshape(64, 128).T),
        "sut": np.ascontiguousarray(np.asarray(inputs["L0_ffn_up_scale"], np.float32).reshape(64, 128).T),
        "sdt": np.ascontiguousarray(np.asarray(inputs["L0_ffn_down_scale"], np.float32).reshape(16, 128).T),
    }
    in_maps = []
    for b in range(8):
        m = dict(common)
        m["x"] = np.ascontiguousarray(x[b])
        in_maps.append(m)
    res = run_bass_kernel_spmd(nc, in_maps, core_ids=list(range(8)))
    return np.stack([res.results[b]["y"] for b in range(8)], axis=0)


# version 07 (final)
# speedup vs baseline: 1.1363x; 1.0165x over previous
import numpy as np
import ml_dtypes
from contextlib import ExitStack

import concourse.bass as bass
import concourse.mybir as mybir
from concourse.bass_utils import run_bass_kernel_spmd

F32 = mybir.dt.float32
BF16 = mybir.dt.bfloat16
I32 = mybir.dt.int32
AF = mybir.ActivationFunctionType
ALU = mybir.AluOpType
AX = mybir.AxisListType

T = 2048
D = 2048
HK = 16
HV = 32
C = 128
NCH = T // C
BIG = 1e30
DEBUG = False


class Prog:
    ENGINES = ("tensor", "vector", "scalar", "gpsimd", "sync")

    def __init__(self):
        self.plan = {k: [] for k in self.ENGINES}
        self.cnt = {}
        self.handles = {}
        self.watermark = {k: {} for k in self.ENGINES}
        self.bank_guard = {}

    def op(self, eng, fn, inc=None):
        tok = None
        if inc is not None:
            name, amt = inc
            self.cnt[name] = self.cnt.get(name, 0) + amt
            tok = (name, self.cnt[name])

        def wrapped(e, fn=fn, inc=inc):
            inst = fn(e)
            if inc is not None:
                inst.then_inc(self.handles[inc[0]], inc[1])
        self.plan[eng].append(wrapped)
        return tok

    def wait(self, eng, tok):
        if tok is None:
            return
        name, val = tok
        wm = self.watermark[eng]
        if wm.get(name, 0) >= val:
            return
        wm[name] = val
        self.plan[eng].append(lambda e, n=name, v=val: e.wait_ge(self.handles[n], v))

    def waits(self, eng, toks):
        best = {}
        for t in toks:
            if t is None:
                continue
            n, v = t
            if best.get(n, -1) < v:
                best[n] = v
        for n, v in best.items():
            self.wait(eng, (n, v))


def build_program():
    nc = bass.Bass("TRN2", target_bir_lowering=False)

    def inp(name, shape, dt):
        return nc.dram_tensor(name, shape, dt, kind="ExternalInput").ap()

    x_h = inp("x", [T, D], F32)
    wqkvz_h = inp("wqkvz", [D, 12288], F32)
    wba_h = inp("wba", [D, 64], F32)
    cw_h = inp("cw", [128, 256], F32)
    dtb_h = inp("dtb", [64, 1], F32)
    alog_h = inp("alog", [64, 1], F32)
    normw_h = inp("normw", [128, 1], F32)
    wout_h = inp("wout", [4096, D], F32)
    gw_h = inp("gw", [8192, D], I32)
    uw_h = inp("uw", [8192, D], I32)
    dw_h = inp("dw", [D, 8192], I32)
    sg_h = inp("sgt", [128, 64], F32)
    su_h = inp("sut", [128, 64], F32)
    sd_h = inp("sdt", [128, 16], F32)

    y_h = nc.dram_tensor("y", [T, D], F32, kind="ExternalOutput").ap()

    scr_kind = "ExternalOutput" if DEBUG else "Internal"
    xwtm_h = nc.dram_tensor("xwtm", [HK, T, 512], BF16, kind="Internal").ap()
    z_h = nc.dram_tensor("zscr", [HK, T, 256], BF16, kind=scr_kind).ap()
    o_h = nc.dram_tensor("oscr", [HV, T, 128], BF16, kind=scr_kind).ap()
    act_h = nc.dram_tensor("actscr", [64, 128, T], BF16, kind="Internal").ap()
    x1f_h = nc.dram_tensor("x1fscr", [16, 128, T], F32, kind=scr_kind).ap()
    x1b_h = nc.dram_tensor("x1bscr", [16, 128, T], BF16, kind="Internal").ap()
    wg16_h = nc.dram_tensor("wg16", [8192, D], BF16, kind="Internal").ap()
    wu16_h = nc.dram_tensor("wu16", [8192, D], BF16, kind="Internal").ap()
    wd16_h = nc.dram_tensor("wd16", [D, 8192], BF16, kind="Internal").ap()

    idf_c = nc.inline_tensor(np.eye(128, dtype=np.float32), "idf_c").ap()
    idb_c = nc.inline_tensor(np.eye(128, dtype=ml_dtypes.bfloat16), "idb_c").ap()
    e127_c = nc.inline_tensor(
        np.broadcast_to((np.arange(128) == 127).astype(np.float32)[:, None], (128, 128)).copy(),
        "e127_c").ap()
    onesb_c = nc.inline_tensor(np.ones((128, 8), ml_dtypes.bfloat16), "onesb_c").ap()
    onesrb_c = nc.inline_tensor(np.ones((8, 128), ml_dtypes.bfloat16), "onesrb_c").ap()
    onesrf_c = nc.inline_tensor(np.ones((8, 128), np.float32), "onesrf_c").ap()
    ones64_c = nc.inline_tensor(np.ones((64, 128), np.float32), "ones64_c").ap()
    mN = np.where(np.tril(np.ones((C, C)), -1) > 0, 0.0, BIG).astype(np.float32)
    mAL = np.where(np.triu(np.ones((C, C)), 0) > 0, 0.0, BIG).astype(np.float32)
    maskN_c = nc.inline_tensor(mN, "maskN_c").ap()
    maskAL_c = nc.inline_tensor(mAL, "maskAL_c").ap()
    CONSTVALS = [128.0, 1.0 / 128.0, -1.0, 128.0e-6, 1e-6, 1e-5, 1.0, 0.0]
    cvals_c = nc.inline_tensor(
        np.broadcast_to(np.asarray(CONSTVALS, np.float32)[None, :], (128, len(CONSTVALS))).copy(),
        "cvals_c").ap()

    p = Prog()

    with ExitStack() as ctx:
        def sb(name, shape, dt, st=None):
            return (st or ctx).enter_context(nc.sbuf_tensor(name, shape, dt))

        idf = sb("idf", [128, 128], F32)
        idb = sb("idb", [128, 128], BF16)
        e127 = sb("e127", [128, 128], F32)
        onesb = sb("onesb", [128, 8], BF16)
        onesrb = sb("onesrb", [8, 128], BF16)
        onesrf = sb("onesrf", [8, 128], F32)
        ones64 = sb("ones64", [64, 128], F32)
        maskN = sb("maskN", [128, 128], F32)
        maskAL = sb("maskAL", [128, 128], F32)
        cw = sb("cw_s", [128, 256], F32)
        dtb = sb("dtb_s", [64, 1], F32)
        alog = sb("alog_s", [64, 1], F32)
        nega = sb("nega", [64, 1], F32)
        normw = sb("normw_s", [128, 1], F32)
        sgt = sb("sgt_s", [128, 64], F32)
        sut = sb("sut_s", [128, 64], F32)
        sdt = sb("sdt_s", [128, 16], F32)
        cvals = sb("cvals", [128, 8], F32)
        rows4 = sb("rows4", [1, 2048], F32)
        srow = rows4.ap()[:, 0:512]
        s2row = rows4.ap()[:, 512:1024]
        murow = rows4.ap()[:, 1024:1536]
        invrow = rows4.ap()[:, 1536:2048]

        stSC = ExitStack()
        SCb = sb("SCb", [128, 1024], F32, stSC)
        SCbneg = sb("SCbneg", [128, 1024], F32, stSC)
        SCg = sb("SCg", [128, 1024], F32, stSC)
        SCeg = sb("SCeg", [128, 1024], F32, stSC)
        SCbe = sb("SCbe", [128, 1024], F32, stSC)
        SCkd = sb("SCkd", [128, 1024], F32, stSC)
        SCgl = sb("SCgl", [128, 1024], F32, stSC)
        wbounce = [sb("wb%d" % i, [128, 2048], BF16, stSC) for i in range(2)]

        pb = [ctx.enter_context(nc.psum_tensor("pb%d" % i, [128, 512], F32)) for i in range(6)]
        pbh = [ctx.enter_context(nc.psum_tensor("pbh%d" % i, [128, 128], BF16)) for i in range(2)]

        DMA_SEMS = set()

        def dma(q, sem, outap, inap, transpose=False):
            DMA_SEMS.add(sem)
            return p.op(q, lambda e, o=outap, i=inap, tr=transpose:
                        e.dma_start(out=o, in_=i, transpose=tr),
                        inc=(sem, 16))

        def mm(bank, outap, lhsT, rhs, start=True, stop=True, tok=False, waits=()):
            w = list(waits)
            if start and bank is not None:
                g = p.bank_guard.get(bank)
                w += (g if isinstance(g, list) else [g])
            p.waits("tensor", w)
            return p.op("tensor", lambda e, o=outap, l=lhsT, r=rhs, s1=start, s2=stop:
                        e.matmul(o, l, r, start=s1, stop=s2),
                        inc=("sp", 1) if tok else None)

        def tr(bank, outap, inap, ident, tok=True, waits=()):
            g = p.bank_guard.get(bank)
            w = list(waits) + (g if isinstance(g, list) else [g])
            p.waits("tensor", w)
            return p.op("tensor", lambda e, o=outap, i=inap, d=ident:
                        e.transpose(o, i, d), inc=("sp", 1) if tok else None)

        def vop(eng, fn, tok=True, waits=(), bank=None):
            sem = {"vector": "sv", "scalar": "sa", "gpsimd": "sgp"}[eng]
            p.waits(eng, waits)
            t = p.op(eng, fn, inc=(sem, 1) if tok else None)
            if bank is not None:
                p.bank_guard[bank] = t
            return t

        def colap(tile_ap, col):
            return tile_ap[:, col:col + 1]

        def bcast_lhsT(tile_ap, col, n=128):
            base = tile_ap[:, col:col + 1]
            return bass.AP(base.tensor, base.offset, [base.ap[0], [0, n]])

        def phase_barrier(dma_toks=()):
            toks = [("sp", p.cnt.get("sp", 0)), ("sv", p.cnt.get("sv", 0)),
                    ("sa", p.cnt.get("sa", 0)), ("sgp", p.cnt.get("sgp", 0))]
            toks += [t for t in dma_toks if t is not None]
            for eng in ("tensor", "vector", "scalar", "gpsimd", "sync"):
                p.waits(eng, toks)

        for dst, src in ((idf, idf_c), (idb, idb_c), (e127, e127_c), (onesb, onesb_c),
                         (onesrb, onesrb_c), (onesrf, onesrf_c), (ones64, ones64_c),
                         (maskN, maskN_c), (maskAL, maskAL_c), (cvals, cvals_c),
                         (cw, cw_h), (dtb, dtb_h),
                         (alog, alog_h), (normw, normw_h), (sgt, sg_h), (sut, su_h),
                         (sdt, sd_h)):
            tok_consts = dma("sync", "qcst", dst[:], src)

        consts_tok = tok_consts
        for _eng in ("tensor", "vector", "scalar", "gpsimd"):
            p.waits(_eng, [consts_tok])
        for _i, _v in enumerate(CONSTVALS):
            nc.const_aps.register(F32, _v, cvals.ap()[:, _i:_i + 1])

        bounce_done = []
        prev_b = [None, None]
        pieces = ([(gw_h, wg16_h, s, 0) for s in range(64)]
                  + [(uw_h, wu16_h, s, 0) for s in range(64)]
                  + [(dw_h, wd16_h, s, q) for s in range(16) for q in range(4)])
        for pi, (srct, dstt, s, q) in enumerate(pieces):
            par = pi % 2
            p.waits("gpsimd", [prev_b[par]])
            t_in = dma("gpsimd", "qb%d" % par, wbounce[par][:],
                       srct[s * 128:(s + 1) * 128, q * 2048:(q + 1) * 2048])
            p.waits("gpsimd", [t_in])
            t_out = dma("gpsimd", "qb%d" % par, dstt[s * 128:(s + 1) * 128, q * 2048:(q + 1) * 2048],
                        wbounce[par][:])
            prev_b[par] = t_out
            bounce_done.append(t_out)
        bounce_fin = [prev_b[0], prev_b[1]]

        stA = ExitStack()
        xfm = sb("xfm", [128, 16 * 2048], BF16, stA)
        wbab = sb("wbab", [128, 16 * 64], BF16, stA)
        stA2 = ExitStack()
        whb = [sb("whb%d" % i, [128, 16 * 768], BF16, stA2) for i in range(2)]
        xwtm_b = sb("xwtmb0", [128, 16 * 512], BF16, stA2)
        zstr = sb("zstr", [128, 16 * 256], BF16, stA2)
        xin = [sb("xin%d" % i, [128, 2048], F32, stA2) for i in range(2)]

        t_wba = dma("gpsimd", "qwba", wbab[:].rearrange("q (c w) -> q c w", w=64),
                    wba_h.rearrange("(c q) w -> q c w", q=128))
        consts_tok2 = t_wba
        for _eng in ("tensor", "vector", "scalar", "gpsimd"):
            p.waits(_eng, [consts_tok2])

        xin_used = [None, None]
        for tt in range(16):
            par = tt % 2
            p.waits("sync", [xin_used[par]])
            tin = dma("sync", "qx%d" % par, xin[par][:], x_h[tt * 128:(tt + 1) * 128, :])
            for dc in range(16):
                bk = 4 + dc % 2
                ttr = tr(bk, pb[bk][:, 0:128], xin[par][:, dc * 128:(dc + 1) * 128], idf[:],
                         waits=[tin])
                te = vop("scalar", lambda e,
                         o=xfm[:, dc * 2048 + tt * 128: dc * 2048 + (tt + 1) * 128],
                         i=pb[bk][:, 0:128]: e.copy(o, i), waits=[ttr], bank=bk)
            xin_used[par] = ttr
        xfm_ready = te

        wh_free = [None, None]
        wh_tok = [None, None]
        xw_spill = None
        z_spill = None
        for h in range(HK):
            par = h % 2
            if h == 0:
                p.waits("gpsimd", [wh_free[par]])
                wh_tok[par] = dma("gpsimd", "qwh%d" % par,
                                  whb[par][:].rearrange("q (c w) -> q c w", w=768),
                                  wqkvz_h[:, 768 * h:768 * (h + 1)].rearrange(
                                      "(c q) w -> q c w", q=128))
            if h + 1 < HK:
                npar = (h + 1) % 2
                p.waits("gpsimd", [wh_free[npar]])
                wh_tok[npar] = dma("gpsimd", "qwh%d" % npar,
                                   whb[npar][:].rearrange("q (c w) -> q c w", w=768),
                                   wqkvz_h[:, 768 * (h + 1):768 * (h + 2)].rearrange(
                                       "(c q) w -> q c w", q=128))
            ev_mixed = []
            ev_z = []
            for tt in range(16):
                bk = tt % 2
                p.waits("tensor", [xfm_ready, wh_tok[par]])
                for dc in range(16):
                    t_mm = mm(bk if dc == 0 else None, pb[bk][:, 0:512],
                              xfm[:, dc * 2048 + tt * 128: dc * 2048 + (tt + 1) * 128],
                              whb[par][:, dc * 768: dc * 768 + 512],
                              start=(dc == 0), stop=(dc == 15), tok=(dc == 15))
                te = vop("vector", lambda e, o=xwtm_b[:, tt * 512:(tt + 1) * 512],
                         i=pb[bk][:, 0:512]: e.tensor_copy(o, i),
                         waits=[t_mm, xw_spill], bank=bk)
                ev_mixed.append(te)
                bz = 2 + tt % 2
                for dc in range(16):
                    t_mmz = mm(bz if dc == 0 else None, pb[bz][:, 0:256],
                               xfm[:, dc * 2048 + tt * 128: dc * 2048 + (tt + 1) * 128],
                               whb[par][:, dc * 768 + 512: dc * 768 + 768],
                               start=(dc == 0), stop=(dc == 15), tok=(dc == 15))
                tez = vop("scalar", lambda e, o=zstr[:, tt * 256:(tt + 1) * 256],
                          i=pb[bz][:, 0:256]: e.copy(o, i),
                          waits=[t_mmz, z_spill], bank=bz)
                ev_z.append(tez)
            wh_free[par] = t_mmz
            p.waits("gpsimd", ev_mixed + ev_z)
            xw_spill = dma("gpsimd", "qxw0",
                           xwtm_h[h].rearrange("(tt q) c -> q tt c", q=128),
                           xwtm_b[:].rearrange("q (tt c) -> q tt c", c=512))
            z_spill = dma("gpsimd", "qz", z_h[h].rearrange("(tt q) c -> q tt c", q=128),
                          zstr[:].rearrange("q (tt c) -> q tt c", c=256))
        xw_spill_all = [xw_spill, xw_spill]
        z_spill_all = [z_spill]

        stA2.close()
        bapk = [sb("bapk%d" % i, [128, 2048], F32, stA) for i in range(3)]
        bafm = bapk[0].ap()[0:64, :]
        bsig = bapk[0].ap()[64:128, :]
        gpre = bapk[1].ap()[0:64, :]
        gcum = bapk[1].ap()[64:128, :]
        spt = bapk[2].ap()[0:64, :]
        spt2 = bapk[2].ap()[64:128, :]

        ba_toks = []
        for ts4 in range(4):
            bk = ts4 % 2
            p.waits("tensor", [xfm_ready, t_wba])
            for dc in range(16):
                t_mm = mm(bk if dc == 0 else None, pb[bk][0:64, 0:512],
                          wbab[:, dc * 64:(dc + 1) * 64],
                          xfm[:, dc * 2048 + ts4 * 512: dc * 2048 + (ts4 + 1) * 512],
                          start=(dc == 0), stop=(dc == 15), tok=(dc == 15))
            te = vop("vector", lambda e, o=bafm[:, ts4 * 512:(ts4 + 1) * 512],
                     i=pb[bk][0:64, 0:512]: e.tensor_copy(o, i), waits=[t_mm], bank=bk)
            ba_toks.append(te)
        a2_pe_done = t_mm

        p.waits("scalar", ba_toks)
        t_sig = vop("scalar", lambda e: e.activation(bsig, bafm, AF.Sigmoid))
        t_sp1 = vop("scalar", lambda e: e.activation(spt, bafm, AF.Abs, bias=dtb[:]))
        t_sp2 = vop("scalar", lambda e: e.activation(spt2, spt, AF.Exp, scale=-1.0), waits=[t_sp1])
        t_sp3 = vop("scalar", lambda e: e.activation(gpre, spt2, AF.Ln, bias=1.0), waits=[t_sp2])
        t_sp4 = vop("scalar", lambda e: e.activation(spt, bafm, AF.Relu, bias=dtb[:]), waits=[t_sp2])
        t_na1 = vop("scalar", lambda e: e.activation(nega[:], alog[:], AF.Exp))
        t_sp5 = vop("vector", lambda e: e.tensor_add(gpre, gpre, spt), waits=[t_sp3, t_sp4])
        t_na2 = vop("vector", lambda e: e.tensor_scalar_mul(nega[:], nega[:], -1.0), waits=[t_na1])
        t_gp = vop("vector", lambda e: e.tensor_scalar(gpre, gpre, nega[:], None, ALU.mult),
                   waits=[t_sp5, t_na2])
        cs_toks = []
        for n in range(NCH):
            tcs = vop("vector", lambda e, o=gcum[:, n * 128:(n + 1) * 128],
                      i=gpre[:, n * 128:(n + 1) * 128]:
                      e.tensor_tensor_scan(o, i, ones64[:], 0.0, ALU.add, ALU.mult),
                      waits=[t_gp])
            cs_toks.append(tcs)
        sc_toks = []
        for n in range(NCH):
            bk = 4 + n % 2
            t1 = tr(bk, pb[bk][0:128, 0:64], bsig[:, n * 128:(n + 1) * 128],
                    idf[64:128, 64:128], waits=[t_sig])
            e1 = vop("scalar", lambda e, o=SCb[:, n * 64:(n + 1) * 64], i=pb[bk][0:128, 0:64]:
                     e.copy(o, i), waits=[t1], bank=bk)
            t2 = tr(bk, pb[bk][0:128, 64:128], gcum[:, n * 128:(n + 1) * 128],
                    idf[64:128, 64:128], waits=[cs_toks[n]])
            e2 = vop("scalar", lambda e, o=SCg[:, n * 64:(n + 1) * 64], i=pb[bk][0:128, 64:128]:
                     e.copy(o, i), waits=[t2], bank=bk)
            sc_toks += [e1, e2]
        t_scbn = vop("vector", lambda e: e.tensor_scalar_mul(SCbneg[:], SCb[:], -1.0), waits=sc_toks)
        t_sceg = vop("scalar", lambda e: e.activation(SCeg[:], SCg[:], AF.Exp), waits=sc_toks)
        gl_toks = []
        for half in range(2):
            bk = half % 2
            tm = mm(bk, pb[bk][:, 0:512], e127[:], SCg[:, half * 512:(half + 1) * 512],
                    tok=True, waits=sc_toks)
            te = vop("vector", lambda e, o=SCkd[:, half * 512:(half + 1) * 512], i=pb[bk][:, 0:512]:
                     e.tensor_copy(o, i), waits=[tm], bank=bk)
            gl_toks.append(te)
        t_gl2 = vop("scalar", lambda e: e.activation(SCgl[:], SCkd[:], AF.Exp), waits=gl_toks)
        t_kd1 = vop("vector", lambda e: e.tensor_sub(SCkd[:], SCkd[:], SCg[:]), waits=gl_toks + [t_gl2])
        t_kd2 = vop("scalar", lambda e: e.activation(SCkd[:], SCkd[:], AF.Exp), waits=[t_kd1])
        t_be = vop("vector", lambda e: e.tensor_tensor(
            SCbe[:].rearrange("q (n m j) -> q n m j", m=16, j=4)[:, :, :, 0:2],
            SCb[:].rearrange("q (n m j) -> q n m j", m=16, j=4)[:, :, :, 0:2],
            SCeg[:].rearrange("q (n m j) -> q n m j", m=16, j=4)[:, :, :, 2:4],
            ALU.mult), waits=[t_sceg] + sc_toks)
        sc_ready = [t_scbn, t_sceg, t_kd2, t_gl2, t_be]

        stA.close()

        stB = ExitStack()
        gy = sb("gy", [128, 4 * 2048], BF16, stB)
        qhat = sb("qhat", [128, 4 * 2048], BF16, stB)
        khat = sb("khat", [128, 4 * 2048], BF16, stB)
        khatT = sb("khatT", [128, 4 * 2048], BF16, stB)
        vTt = sb("vTt", [128, 8 * 2048], BF16, stB)
        acc_b = sb("acc_b", [128, 2048], F32, stB)
        sqb = sb("sqb", [128, 2048], BF16, stB)
        arow = sb("arow", [1, 2048], F32, stB)
        arowb = sb("arowb", [1, 2048], BF16, stB)
        slots = []
        for s in range(3):
            sl = {
                "xnxt": [sb("xnxt%d_%d" % (s, i), [128, 256], BF16, stB) for i in range(2)],
                "tch": [sb("tch%d_%d" % (s, i), [128, 128], BF16, stB) for i in range(2)],
                "rti": sb("rti%d" % s, [128, 256], BF16, stB),
                "wtn": sb("wtn%d" % s, [128, 128], BF16, stB),
                "vnew": sb("vnew%d" % s, [128, 128], BF16, stB),
                "kdec": sb("kdec%d" % s, [128, 128], BF16, stB),
                "alT": sb("alT%d" % s, [128, 128], BF16, stB),
                "s32": sb("s32_%d" % s, [128, 128], F32, stB),
                "s16": sb("s16_%d" % s, [128, 128], BF16, stB),
                "qs_b": sb("qs_b%d" % s, [128, 128], F32, stB),
                "dprN": sb("dprN%d" % s, [128, 128], F32, stB),
                "dprA": sb("dprA%d" % s, [128, 128], F32, stB),
                "decN": sb("decN%d" % s, [128, 128], F32, stB),
                "decA": sb("decA%d" % s, [128, 128], F32, stB),
                "ostrip": sb("ostrip%d" % s, [128, NCH * 128], BF16, stB),
                "bkA": pb[2 * s], "bkB": pb[2 * s + 1], "pbh_id": 6 + (s % 2),
                "st": {}, "ospill": None,
            }
            slots.append(sl)

        state = {"ospill": [None, None, None], "o_last": None}

        def conv_head(h, gy_free_tok):
            p.waits("gpsimd", [gy_free_tok, xw_spill_all[h % 2]])
            fold_toks = []
            for tt in range(16):
                gt = tt // 4
                po = (tt % 4) * 32
                tf = dma("gpsimd", "qfold",
                         gy[po:po + 32, gt * 2048:(gt + 1) * 2048].rearrange(
                             "q (r c) -> q r c", r=4),
                         xwtm_h[h, tt * 128:(tt + 1) * 128, :])
                fold_toks.append(tf)
            conv_toks = []
            for gt in range(4):
                gslice = gy[:, gt * 2048:(gt + 1) * 2048]
                w3 = colap(cw[:], h * 16 + gt * 4 + 3)
                tc = vop("vector", lambda e, o=acc_b[:], i=gslice, s=w3:
                         e.tensor_scalar(o, i, s, None, ALU.mult),
                         waits=[fold_toks[-1]] + conv_toks[-1:])
                for k in (2, 1, 0):
                    sft = 3 - k
                    wk = colap(cw[:], h * 16 + gt * 4 + k)
                    tc = vop("vector", lambda e, o=acc_b[:, sft:2048],
                             i=gslice[:, 0:2048 - sft], s=wk, a=acc_b[:, sft:2048]:
                             e.scalar_tensor_tensor(o, i, s, a, ALU.mult, ALU.add),
                             waits=[tc])
                ts_ = vop("scalar", lambda e, o=gslice, i=acc_b[:]:
                          e.activation(o, i, AF.Silu), waits=[tc])
                conv_toks.append(ts_)
            return conv_toks

        def emit_hat(dst_tile, is_q, gy_toks, dst_guard):
            last = None
            tre_prev = None
            tb_prev = None
            for loc in range(4):
                gslice = gy[:, loc * 2048:(loc + 1) * 2048]
                tq = vop("scalar", lambda e, o=sqb[:], i=gslice:
                         e.activation(o, i, AF.Square), waits=gy_toks + [last])
                mv_last = tre_prev
                for q4 in range(4):
                    tm4 = mm(4, pb[4][0:1, 0:512], onesb[:, 0:1], sqb[:, q4 * 512:(q4 + 1) * 512],
                             tok=True, waits=[tq, mv_last])
                    scale = 128.0 if is_q else 1.0
                    bias = 128.0e-6 if is_q else 1e-6
                    mv_last = vop("scalar", lambda e, o=arow[0:1, q4 * 512:(q4 + 1) * 512],
                                  i=pb[4][0:1, 0:512], s=scale, b=bias:
                                  e.activation(o, i, AF.Sqrt, bias=b, scale=s), waits=[tm4],
                                  bank=4)
                def _recip_bf(e):
                    with nc.allow_low_precision(reason="l2norm scale as bf16 matmul operand"):
                        return e.reciprocal(arowb[:], arow[:])
                trb = vop("vector", _recip_bf, waits=[mv_last, tb_prev])
                tre_prev = trb
                for q4 in range(4):
                    tb = mm(4, pb[4][:, 0:512], onesrb[0:1, :], arowb[0:1, q4 * 512:(q4 + 1) * 512],
                            tok=True, waits=[trb])
                    tb_prev = tb
                    last = vop("vector", lambda e,
                               o=dst_tile[:, loc * 2048 + q4 * 512: loc * 2048 + (q4 + 1) * 512],
                               i=gslice[:, q4 * 512:(q4 + 1) * 512], b=pb[4][:, 0:512]:
                               e.tensor_tensor(o, i, b, ALU.mult),
                               waits=[tb, dst_guard], bank=4)
            return last

        def emit_transposes(src_tile, dst_tile, src_toks, dst_guard):
            last = None
            for loc in range(4):
                for n in range(NCH):
                    bk = 6 + n % 2
                    tt_ = tr(bk, pbh[n % 2][:],
                             src_tile[:, loc * 2048 + n * 128: loc * 2048 + (n + 1) * 128],
                             idb[:], waits=src_toks + [dst_guard])
                    last = vop("scalar", lambda e,
                               o=dst_tile[:, (loc * NCH + n) * 128:(loc * NCH + n + 1) * 128],
                               i=pbh[n % 2][:]: e.copy(o, i), waits=[tt_], bank=bk)
            return last

        def delta_setup(sl, hv, a, vloc):
            st = sl["st"]
            s_idx = slots.index(sl)
            gA = p.bank_guard.get(2 * s_idx)
            gB = p.bank_guard.get(2 * s_idx + 1)
            for g in (gA if isinstance(gA, list) else [gA]):
                if g is not None:
                    st["Abank_dve"] = g if st.get("Abank_dve") is None else max(
                        st["Abank_dve"], g, key=lambda t: (t[0], t[1])) if False else g
            st["seedA"] = (gA if isinstance(gA, list) else [gA])
            st["seedB"] = (gB if isinstance(gB, list) else [gB])
            p.bank_guard[2 * s_idx] = None
            p.bank_guard[2 * s_idx + 1] = None
            st["hv"] = hv
            st["loc"] = hv // 2 - 4 * a
            st["ach"] = (hv // 2) * 4 + 2 + hv % 2
            st["bch"] = (hv // 2) * 4 + hv % 2
            st["vloc"] = vloc
            t_s0 = vop("gpsimd", lambda e, t=sl["s32"]: e.memset(t[:], 0.0),
                       waits=[st.get("s_used"), st.get("s16_used")])
            t_s1 = vop("gpsimd", lambda e, t=sl["s16"]: e.memset(t[:], 0.0),
                       waits=[st.get("s16_used")])
            st["sready"] = [t_s0, t_s1]

        def delta_chunk(sl, n, qk_toks, vt_tok):
            st = sl["st"]
            loc, ach, bch, vloc = st["loc"], st["ach"], st["bch"], st["vloc"]
            bkA, bkB = sl["bkA"], sl["bkB"]
            kcol = loc * 2048 + n * 128
            ktcol = (loc * NCH + n) * 128
            vtcol = (vloc * NCH + n) * 128
            scol = n * 64
            t_g1 = mm(None, bkA[:, 0:128], khat[:, kcol:kcol + 128], khat[:, kcol:kcol + 128],
                      tok=True, waits=qk_toks + [st.get("A0"), st.get("Abank_dve"),
                                                 st.get("Abank_act")]
                      + st.pop("seedA", []))
            t_g2 = mm(None, bkA[:, 128:256], khat[:, kcol:kcol + 128],
                      qhat[:, kcol:kcol + 128], tok=True, waits=[st.get("A1")])
            t_e = mm(None, bkA[:, 256:384], bcast_lhsT(SCg[:], scol + ach), idf[:],
                     tok=True, waits=sc_ready + [st.get("A2")])
            t_dn = vop("vector", lambda e, ps=bkA[:, 256:384], gc=colap(SCg[:], scol + ach),
                       o=sl["dprN"][:]:
                       e.scalar_tensor_tensor(o, ps, gc, maskN[:], ALU.subtract, ALU.add),
                       waits=[t_e, st.get("dprN_used")])
            t_dn2 = vop("scalar", lambda e, o=sl["decN"][:], i=sl["dprN"][:]:
                        e.activation(o, i, AF.Exp, scale=-1.0),
                        waits=[t_dn, st.get("decN_used")])
            st["dprN_used"] = t_dn2
            t_da = vop("vector", lambda e, ps=bkA[:, 256:384], gc=colap(SCg[:], scol + ach),
                       o=sl["dprA"][:]:
                       e.scalar_tensor_tensor(o, ps, gc, maskAL[:], ALU.subtract, ALU.subtract),
                       waits=[t_e, st.get("dprA_used"), t_dn])
            st["A2"] = t_da
            t_da2 = vop("scalar", lambda e, o=sl["decA"][:], i=sl["dprA"][:]:
                        e.activation(o, i, AF.Exp), waits=[t_da, st.get("decA_used")])
            st["dprA_used"] = t_da2
            xn0 = sl["xnxt"][0]
            t_xn = vop("vector", lambda e, ps=bkA[:, 0:128], bc=colap(SCbneg[:], scol + bch),
                       o=xn0[:, 0:128], d=sl["decN"][:]:
                       e.scalar_tensor_tensor(o, ps, bc, d, ALU.mult, ALU.mult),
                       waits=[t_g1, t_dn2, st.get("xn_used")])
            st["decN_used"] = t_xn
            st["A0"] = t_xn
            t_al = vop("vector", lambda e, ps=bkA[:, 128:256], o=sl["alT"][:], d=sl["decA"][:]:
                       e.tensor_tensor(o, ps, d, ALU.mult),
                       waits=[t_g2, t_da2, st.get("al_used")])
            st["decA_used"] = t_al
            st["A1"] = t_al
            ph = sl["pbh_id"]
            t_xt = tr(ph, pbh[ph - 6][:], xn0[:, 0:128], idb[:], waits=[t_xn])
            t_xte = vop("scalar", lambda e, o=xn0[:, 128:256], i=pbh[ph - 6][:]:
                        e.copy(o, i), waits=[t_xt], bank=ph)
            t_t0 = vop("vector", lambda e, o=sl["tch"][0][:], i=xn0[:, 128:256]:
                       e.tensor_add(o, i, idb[:]), waits=[t_xte, st.get("tch_used")])
            cur = 0
            t_prev_t = t_t0
            t_prev_x = [t_xn, t_xte]
            for lev in range(6):
                nxt = 1 - cur
                xc, xv = sl["xnxt"][cur], sl["xnxt"][nxt]
                t_sq1 = mm(None, bkB[:, 0:128], xc[:, 128:256], xc[:, 0:128],
                           tok=(lev == 5), waits=t_prev_x + [st.get("Bsq")]
                           + st.pop("seedB", []))
                if lev < 5:
                    t_sq2 = mm(None, bkB[:, 128:256], xc[:, 0:128], xc[:, 128:256], tok=True)
                    t_ev = vop("vector", lambda e, o=xv[:, 0:256], i=bkB[:, 0:256]:
                               e.tensor_copy(o, i), waits=[t_sq2])
                else:
                    t_ev = vop("vector", lambda e, o=xv[:, 0:128], i=bkB[:, 0:128]:
                               e.tensor_copy(o, i), waits=[t_sq1])
                st["Bsq"] = t_ev
                t_xi_last = t_ev
                mm(None, bkA[:, 384:512], xv[:, 0:128], sl["tch"][cur][:],
                   start=True, stop=False, waits=[t_ev, t_prev_t, st.get("A3")])
                t_ap = mm(None, bkA[:, 384:512], idb[:], sl["tch"][cur][:],
                          start=False, stop=True, tok=True)
                t_te = vop("scalar", lambda e, o=sl["tch"][nxt][:], i=bkA[:, 384:512]:
                           e.copy(o, i), waits=[t_ap])
                st["A3"] = t_te
                t_prev_t = t_te
                t_prev_x = [t_ev]
                cur = nxt
            t_chain = t_prev_t
            st["xn_used"] = t_xi_last
            t_r1 = vop("vector", lambda e, i=vTt[:, vtcol:vtcol + 128],
                       s=colap(SCb[:], scol + bch), o=sl["rti"][:, 0:128]:
                       e.tensor_scalar(o, i, s, None, ALU.mult),
                       waits=[vt_tok, st.get("r_used")] + sc_ready)
            t_r2 = vop("vector", lambda e, i=khatT[:, ktcol:ktcol + 128],
                       s=colap(SCbe[:], scol + bch), o=sl["rti"][:, 128:256]:
                       e.tensor_scalar(o, i, s, None, ALU.mult), waits=qk_toks)
            t_kd = vop("gpsimd", lambda e, i=khatT[:, ktcol:ktcol + 128],
                       s=colap(SCkd[:], scol + ach), o=sl["kdec"][:]:
                       e.tensor_scalar(o, i, s, None, ALU.mult),
                       waits=qk_toks + [st.get("kd_used")] + sc_ready)
            t_wt = mm(None, bkB[:, 0:128], sl["rti"][:, 128:256], sl["tch"][cur][:], tok=True,
                      waits=[t_chain, t_r2, st.get("Bsq")])
            st["tch_used"] = t_wt
            st["r_used"] = t_wt
            t_wtn = vop("scalar", lambda e, o=sl["wtn"][:], i=bkB[:, 0:128]:
                        e.mul(o, i, -1.0), waits=[t_wt, st.get("wtn_used")])
            st["Bsq"] = t_wtn
            t_uw = mm(None, bkB[:, 256:512], sl["tch"][cur][:], sl["rti"][:, 0:256],
                      start=True, stop=False, tok=True,
                      waits=[t_chain, t_r1, t_r2, t_wtn, st.get("Buw")])
            t_vn = mm(None, bkB[:, 256:384], sl["wtn"][:], sl["s16"][:],
                      start=False, stop=True, tok=True, waits=[t_wtn] + st["sready"])
            st["wtn_used"] = t_vn
            t_vne = vop("vector", lambda e, o=sl["vnew"][:], i=bkB[:, 256:384]:
                        e.tensor_copy(o, i), waits=[t_vn, st.get("vn_used")])
            st["Buw"] = t_vne
            t_q = mm(None, bkA[:, 0:128], qhat[:, kcol:kcol + 128], sl["s16"][:], tok=True,
                     waits=qk_toks + st["sready"] + [st.get("A0")])
            t_qe = vop("scalar", lambda e, s=colap(SCeg[:], scol + ach), o=sl["qs_b"][:],
                       i=bkA[:, 0:128]:
                       e.activation(o, i, AF.Identity, scale=s),
                       waits=[t_q, st.get("qs_used")])
            st["A0"] = t_qe
            t_a = mm(None, bkA[:, 128:256], sl["alT"][:], sl["vnew"][:], tok=True,
                     waits=[t_vne, t_al, t_qe, st.get("A1")])
            st["al_used"] = t_a
            t_oc = vop("vector", lambda e, o=sl["ostrip"][:, n * 128:(n + 1) * 128],
                       i=sl["qs_b"][:], ps=bkA[:, 128:256]:
                       e.tensor_tensor(o, i, ps, ALU.add),
                       waits=[t_qe, t_a, sl["ospill"]])
            st["qs_used"] = t_oc
            st["A1"] = t_oc
            st["o_done"] = t_oc
            t_s = mm(None, bkA[:, 256:384], sl["kdec"][:], sl["vnew"][:], tok=True,
                     waits=[t_kd, t_vne, t_oc, st.get("A2")])
            st["vn_used"] = t_s
            st["kd_used"] = t_s
            t_su = vop("vector", lambda e, s=colap(SCgl[:], scol + ach), o=sl["s32"][:],
                       ps=bkA[:, 256:384]:
                       e.scalar_tensor_tensor(o, o, s, ps, ALU.mult, ALU.add),
                       waits=[t_s] + st["sready"])
            st["A2"] = t_su
            t_s16 = vop("gpsimd", lambda e, o=sl["s16"][:], i=sl["s32"][:]:
                        e.tensor_copy(o, i), waits=[t_su])
            st["s_used"] = t_su
            st["s16_used"] = t_s16
            st["sready"] = [t_su, t_s16]
            st["Abank_dve"] = t_su
            st["Abank_act"] = t_qe

        def delta_finish(sl):
            st = sl["st"]
            hv = st["hv"]
            s_idx = slots.index(sl)
            p.bank_guard[2 * s_idx] = [st.get("Abank_dve"), st.get("Abank_act")]
            p.bank_guard[2 * s_idx + 1] = [st.get("Buw"), st.get("Bsq")]
            p.waits("gpsimd", [st["o_done"]])
            tsp = dma("gpsimd", "qo%d" % s_idx, o_h[hv].rearrange("(n q) d -> q n d", q=128),
                      sl["ostrip"][:].rearrange("q (n d) -> q n d", d=128))
            sl["ospill"] = tsp
            state["ospill"][s_idx] = tsp
            state["o_last"] = tsp
            return tsp

        phase_barrier([xw_spill_all[0], xw_spill_all[1], z_spill_all[0], t_wba])
        o_fin = state["ospill"]
        o_spill_last = None
        gy_free = a2_pe_done
        hat_guard_q = None
        hat_guard_k = None
        hat_guard_kt = None
        hat_guard_v = None
        for a in range(4):
            cq = conv_head(a, gy_free)
            q_tok = emit_hat(qhat, True, cq, hat_guard_q)
            gy_free = q_tok
            ck = conv_head(4 + a, gy_free)
            k_tok = emit_hat(khat, False, ck, hat_guard_k)
            kt_tok = emit_transposes(khat, khatT, [k_tok], hat_guard_kt)
            gy_free = k_tok
            qk_tok_max = [q_tok, k_tok, kt_tok]
            cv = conv_head(8 + 2 * a, gy_free)
            vt0 = emit_transposes(gy, vTt[:, 0:4 * NCH * 128], cv, hat_guard_v)
            gy_free = vt0
            grp = [(8 * a + 0, 0, vt0), (8 * a + 1, 1, vt0), (8 * a + 2, 2, vt0)]
            for s_i, (hv, vloc, vt) in enumerate(grp):
                delta_setup(slots[s_i], hv, a, vloc)
            for n in range(NCH):
                for s_i, (hv, vloc, vt) in enumerate(grp):
                    delta_chunk(slots[s_i], n, qk_tok_max, vt)
            for s_i in range(len(grp)):
                o_spill_last = delta_finish(slots[s_i])
            hat_guard_v = state["o_last"]
            cv = conv_head(8 + 2 * a + 1, gy_free)
            vt1 = emit_transposes(gy, vTt[:, 4 * NCH * 128:8 * NCH * 128], cv, hat_guard_v)
            gy_free = vt1
            grp = [(8 * a + 3, 3, vt0), (8 * a + 4, 4, vt1), (8 * a + 5, 5, vt1)]
            for s_i, (hv, vloc, vt) in enumerate(grp):
                delta_setup(slots[s_i], hv, a, vloc)
            for n in range(NCH):
                for s_i, (hv, vloc, vt) in enumerate(grp):
                    delta_chunk(slots[s_i], n, qk_tok_max, vt)
            for s_i in range(len(grp)):
                o_spill_last = delta_finish(slots[s_i])
            grp = [(8 * a + 6, 6, vt1), (8 * a + 7, 7, vt1)]
            for s_i, (hv, vloc, vt) in enumerate(grp):
                delta_setup(slots[s_i], hv, a, vloc)
            for n in range(NCH):
                for s_i, (hv, vloc, vt) in enumerate(grp):
                    delta_chunk(slots[s_i], n, qk_tok_max, vt)
            for s_i in range(len(grp)):
                o_spill_last = delta_finish(slots[s_i])
            hat_guard_v = state["o_last"]
            hat_guard_q = state["o_last"]
            hat_guard_k = state["o_last"]
            hat_guard_kt = state["o_last"]

        stB.close()
        stSC.close()

        stC = ExitStack()
        wout16 = sb("wout16", [128, 32 * 2048], BF16, stC)
        ogfm = sb("ogfm", [128, 32 * 256], BF16, stC)
        ozin = sb("ozin", [128, 2048], BF16, stC)
        ozin2 = sb("ozin2", [128, 2048], BF16, stC)
        zh_f = sb("zh_f", [128, 2048], F32, stC)
        ogh_f = sb("ogh_f", [128, 2048], F32, stC)
        attnq = sb("attnq", [128, 16 * 256], F32, stC)
        lnbuf = [sb("lnbuf%d" % i, [128, 256], BF16, stC) for i in range(2)]
        xinc = sb("xinc", [128, 2048], F32, stC)
        msum = sb("msum", [128, 16], F32, stC)
        msr = sb("msr", [128, 16], F32, stC)
        msr2 = sb("msr2", [128, 16], F32, stC)
        x1bpc = sb("x1bpc", [128, 256], BF16, stC)

        phase_barrier(o_fin + [z_spill_all[0]] + bounce_fin)
        wout_toks = []
        for vc in range(32):
            p.waits("gpsimd", [a2_pe_done] + o_fin)
            wout_toks.append(dma("gpsimd", "qwo", wout16[:, vc * 2048:(vc + 1) * 2048],
                                 wout_h[vc * 128:(vc + 1) * 128, :]))
        wout_last = wout_toks[-1]

        x1_spill_last = None
        x1f_last = None
        lnst_guard = None
        state_c = {}
        for oc in range(8):
            og_last = None
            for tbl in range(2):
                tb = oc * 2 + tbl
                for h2 in range(2):
                    p.waits("sync", o_fin + [z_spill_all[0], og_last])
                    t_o = dma("sync", "qco", ozin[:].rearrange("q (h d) -> q h d", d=128),
                              o_h[h2 * 16:(h2 + 1) * 16, tb * 128:(tb + 1) * 128, :].rearrange(
                                  "h q d -> q h d"))
                    t_z = dma("sync", "qcz", ozin2[:].rearrange("q (h c) -> q h c", c=256),
                              z_h[h2 * 8:(h2 + 1) * 8, tb * 128:(tb + 1) * 128, :].rearrange(
                                  "h q c -> q h c"))
                    t_sz = vop("scalar", lambda e: e.activation(zh_f[:], ozin2[:], AF.Silu),
                               waits=[t_z, og_last, state_c.get("t_ms")])
                    t_og = vop("vector", lambda e: e.tensor_tensor(ogh_f[:], ozin[:], zh_f[:],
                                                                   ALU.mult),
                               waits=[t_sz, t_o, og_last])
                    t_sq = vop("scalar", lambda e: e.activation(zh_f[:], ogh_f[:], AF.Square),
                               waits=[t_og])
                    t_ms = vop("vector", lambda e:
                               e.tensor_reduce(msum[:], zh_f[:].rearrange("q (h d) -> q h d", d=128),
                                               AX.X, ALU.add), waits=[t_sq])
                    state_c["t_ms"] = t_ms
                    t_mq = vop("scalar", lambda e: e.activation(msr[:], msum[:], AF.Sqrt,
                                                                bias=1e-6, scale=1.0 / 128.0),
                               waits=[t_ms])
                    t_mr = vop("vector", lambda e: e.reciprocal(msr2[:], msr[:]), waits=[t_mq])
                    t_on = vop("vector", lambda e: e.tensor_tensor(
                        ogh_f[:].rearrange("q (h d) -> q h d", d=128),
                        ogh_f[:].rearrange("q (h d) -> q h d", d=128),
                        bass.AP(msr2.ap().tensor, 0, [msr2.ap().ap[0], [1, 16], [0, 128]]),
                        ALU.mult), waits=[t_mr])
                    for vcl in range(16):
                        vc = h2 * 16 + vcl
                        bk = vcl % 2
                        t_tr = tr(bk, pb[bk][:, 0:128], ogh_f[:, vcl * 128:(vcl + 1) * 128],
                                  idf[:], waits=[t_on])
                        og_last = vop("scalar", lambda e,
                                      o=ogfm[:, vc * 256 + tbl * 128: vc * 256 + (tbl + 1) * 128],
                                      i=pb[bk][:, 0:128]:
                                      e.activation(o, i, AF.Identity, scale=normw[:]),
                                      waits=[t_tr], bank=bk)
            attn_last = None
            for dc in range(16):
                bk = 2 + dc % 2
                p.waits("tensor", [og_last, wout_last])
                for vc in range(32):
                    t_mm = mm(bk if vc == 0 else None, pb[bk][:, 0:256],
                              wout16[:, vc * 2048 + dc * 128: vc * 2048 + dc * 128 + 128],
                              ogfm[:, vc * 256:(vc + 1) * 256], start=(vc == 0),
                              stop=(vc == 31), tok=(vc == 31))
                attn_last = vop("vector", lambda e, o=attnq[:, dc * 256:(dc + 1) * 256],
                                i=pb[bk][:, 0:256]: e.tensor_copy(o, i),
                                waits=[t_mm, x1_spill_last, x1f_last], bank=bk)
            res_last = None
            for tbl in range(2):
                tb = oc * 2 + tbl
                p.waits("sync", [res_last])
                t_x = dma("sync", "qcx", xinc[:], x_h[tb * 128:(tb + 1) * 128, :])
                for dc in range(16):
                    bk = 4 + dc % 2
                    t_tr = tr(bk, pb[bk][:, 0:128], xinc[:, dc * 128:(dc + 1) * 128], idf[:],
                              waits=[t_x, attn_last])
                    res_last = vop("vector", lambda e,
                                   o=attnq[:, dc * 256 + tbl * 128: dc * 256 + (tbl + 1) * 128],
                                   i=pb[bk][:, 0:128]:
                                   e.tensor_tensor(o, o, i, ALU.add), waits=[t_tr], bank=bk)
            mm_tok = [None, None]
            for dc in range(16):
                tc1 = vop("vector", lambda e, o=lnbuf[0][:], i=attnq[:, dc * 256:(dc + 1) * 256]:
                          e.tensor_copy(o, i), waits=[res_last, mm_tok[0], lnst_guard])
                mm_tok[0] = mm(0 if dc == 0 else None, pb[0][0:1, 0:256], onesb[:, 0:1],
                               lnbuf[0][:], start=(dc == 0), stop=(dc == 15), waits=[tc1],
                               tok=True)
                tc2 = vop("scalar", lambda e, o=lnbuf[1][:], i=attnq[:, dc * 256:(dc + 1) * 256]:
                          e.activation(o, i, AF.Square), waits=[res_last, mm_tok[1], lnst_guard])
                mm_tok[1] = mm(1 if dc == 0 else None, pb[1][0:1, 0:256], onesb[:, 0:1],
                               lnbuf[1][:], start=(dc == 0), stop=(dc == 15), waits=[tc2],
                               tok=True)
            t_mu = vop("vector", lambda e: e.tensor_scalar(murow[0:1, 0:256], pb[0][0:1, 0:256],
                                                           1.0 / 2048.0, None, ALU.mult),
                       waits=[mm_tok[0]], bank=0)
            t_m2 = vop("vector", lambda e: e.tensor_scalar(srow[0:1, 0:256], pb[1][0:1, 0:256],
                                                           1.0 / 2048.0, None, ALU.mult),
                       waits=[mm_tok[1]], bank=1)
            t_v1 = vop("vector", lambda e: e.tensor_tensor(s2row[0:1, 0:256], murow[0:1, 0:256],
                                                           murow[0:1, 0:256], ALU.mult),
                       waits=[t_mu])
            t_v2 = vop("vector", lambda e: e.tensor_sub(s2row[0:1, 0:256], srow[0:1, 0:256],
                                                        s2row[0:1, 0:256]), waits=[t_m2, t_v1])
            t_v3 = vop("scalar", lambda e: e.activation(s2row[0:1, 0:256], s2row[0:1, 0:256],
                                                        AF.Sqrt, bias=1e-5), waits=[t_v2])
            t_v4 = vop("vector", lambda e: e.reciprocal(invrow[0:1, 0:256], s2row[0:1, 0:256]),
                       waits=[t_v3])
            t_b1 = mm(2, pb[2][:, 0:256], onesrf[0:1, :], murow[0:1, 0:256], tok=True,
                      waits=[t_mu])
            t_b2 = mm(3, pb[3][:, 0:256], onesrf[0:1, :], invrow[0:1, 0:256], tok=True,
                      waits=[t_v4])
            ln_last = None
            for dc in range(16):
                t_n1 = vop("vector", lambda e, o=attnq[:, dc * 256:(dc + 1) * 256]:
                           e.tensor_sub(o, o, pb[2][:, 0:256]), waits=[t_b1, t_b2, mm_tok[0],
                                                                       mm_tok[1]])
                t_n2 = vop("vector", lambda e, o=attnq[:, dc * 256:(dc + 1) * 256]:
                           e.tensor_tensor(o, o, pb[3][:, 0:256], ALU.mult), waits=[t_n1])
                t_n3 = vop("scalar", lambda e, o=x1bpc[:], i=attnq[:, dc * 256:(dc + 1) * 256]:
                           e.copy(o, i), waits=[t_n2, x1_spill_last])
                p.waits("gpsimd", [t_n2, t_n3, x1f_last])
                x1f_last = dma("gpsimd", "qx1f", x1f_h[dc, :, oc * 256:(oc + 1) * 256],
                               attnq[:, dc * 256:(dc + 1) * 256])
                x1_spill_last = dma("gpsimd", "qx1b", x1b_h[dc, :, oc * 256:(oc + 1) * 256],
                                    x1bpc[:])
                ln_last = t_n2
            p.bank_guard[2] = ln_last
            p.bank_guard[3] = ln_last
            lnst_guard = ln_last

        stC.close()

        stD = ExitStack()
        x1bt = sb("x1bt", [128, 16 * 2048], BF16, stD)
        wgts = [sb("wgts%d" % i, [128, 16 * 512], BF16, stD) for i in range(2)]
        wuts = [sb("wuts%d" % i, [128, 16 * 512], BF16, stD) for i in range(2)]
        sg_scr = sb("sg_scr", [128, 512], F32, stD)
        act_d = [sb("act_d%d" % i, [128, 512], BF16, stD) for i in range(2)]
        asum_d = sb("asum_d", [1, 2048], F32, stD)

        phase_barrier([x1_spill_last, x1f_last])
        t_az = vop("vector", lambda e: e.memset(asum_d[:], 0.0))
        x1b_tok = None
        for dc in range(16):
            p.waits("sync", [x1_spill_last])
            x1b_tok = dma("sync", "qdx", x1bt[:, dc * 2048:(dc + 1) * 2048], x1b_h[dc, :, :])
        wg_use = [None, None]
        wg_tok = [None, None]
        act_spill_last = None
        t_as_prev = t_az
        t_sg_prev = None
        act_guard = [None, None]
        for ihs in range(16):
            par = ihs % 2
            p.waits("sync", [wg_use[par]] + bounce_fin)
            tw = None
            for hc in range(16):
                dma("sync", "qdw%d" % par, wgts[par][:, hc * 512:(hc + 1) * 512],
                    wg16_h[ihs * 512:(ihs + 1) * 512, hc * 128:(hc + 1) * 128],
                    transpose=True)
                tw = dma("sync", "qdw%d" % par, wuts[par][:, hc * 512:(hc + 1) * 512],
                         wu16_h[ihs * 512:(ihs + 1) * 512, hc * 128:(hc + 1) * 128],
                         transpose=True)
            wg_tok[par] = tw
            for icl in range(4):
                ic = ihs * 4 + icl
                for tq in range(4):
                    bg = tq % 2
                    bu = 2 + tq % 2
                    p.waits("tensor", [x1b_tok, tw])
                    for hc in range(16):
                        t_gm = mm(bg if hc == 0 else None, pb[bg][:, 0:512],
                                  wgts[par][:, hc * 512 + icl * 128: hc * 512 + (icl + 1) * 128],
                                  x1bt[:, hc * 2048 + tq * 512: hc * 2048 + (tq + 1) * 512],
                                  start=(hc == 0), stop=(hc == 15), tok=(hc == 15))
                    for hc in range(16):
                        t_um = mm(bu if hc == 0 else None, pb[bu][:, 0:512],
                                  wuts[par][:, hc * 512 + icl * 128: hc * 512 + (icl + 1) * 128],
                                  x1bt[:, hc * 2048 + tq * 512: hc * 2048 + (tq + 1) * 512],
                                  start=(hc == 0), stop=(hc == 15), tok=(hc == 15))
                    t_sg = vop("scalar", lambda e, i=pb[bg][:, 0:512], s=colap(sgt[:], ic):
                               e.activation(sg_scr[:], i, AF.Silu, scale=s),
                               waits=[t_gm, t_sg_prev], bank=bg)
                    apar = (ic * 4 + tq) % 2
                    t_ac = vop("vector", lambda e, o=act_d[apar][:], s=colap(sut[:], ic),
                               i=pb[bu][:, 0:512]:
                               e.scalar_tensor_tensor(o, sg_scr[:], s, i, ALU.mult, ALU.mult),
                               waits=[t_sg, t_um, act_guard[apar]], bank=bu)
                    t_sg_prev = t_ac
                    ba_ = 4 + tq % 2
                    t_am = mm(ba_, pb[ba_][0:1, 0:512], onesb[:, 0:1], act_d[apar][:], tok=True,
                              waits=[t_ac, t_as_prev])
                    t_as = vop("vector", lambda e, o=asum_d[0:1, tq * 512:(tq + 1) * 512],
                               i=pb[ba_][0:1, 0:512]:
                               e.tensor_tensor(o, o, i, ALU.add), waits=[t_am], bank=ba_)
                    t_as_prev = t_as
                    p.waits("gpsimd", [t_ac])
                    act_spill_last = dma("gpsimd", "qda%d" % apar,
                                         act_h[ic, :, tq * 512:(tq + 1) * 512],
                                         act_d[apar][:])
                    act_guard[apar] = act_spill_last
            wg_use[par] = t_um

        stD.close()

        stE = ExitStack()
        act_e = sb("act_e", [128, 64 * 512], BF16, stE)
        wdt_e = [sb("wdt_e%d" % i, [128, 512], BF16, stE) for i in range(2)]
        resq = sb("resq", [128, 16 * 512], F32, stE)
        x1f_e = [sb("x1f_e%d" % i, [128, 512], F32, stE) for i in range(2)]
        t1_e = sb("t1_e", [128, 512], F32, stE)
        asumbc = sb("asumbc", [128, 2048], F32, stE)
        lnb_e = [sb("lnb_e%d" % i, [128, 512], BF16, stE) for i in range(2)]
        outtm = [sb("outtm%d" % i, [128, 2048], F32, stE) for i in range(2)]

        phase_barrier([act_guard[0], act_guard[1]] + bounce_fin)
        ab_last = None
        y_fin = [None, None]
        x1f_guard = [None, None]
        for tq in range(4):
            bk = tq % 2
            t_bm = mm(bk, pb[bk][:, 0:512], onesrf[0:1, :], asum_d[0:1, tq * 512:(tq + 1) * 512],
                      tok=True, waits=[t_as_prev])
            ab_last = vop("vector", lambda e, o=asumbc[:, tq * 512:(tq + 1) * 512],
                          i=pb[bk][:, 0:512]: e.tensor_copy(o, i), waits=[t_bm], bank=bk)

        y_last = None
        act_ld = [None] * 64
        outtm_guard = [None, None]
        prev_tq_mm = None
        prev_tq_ev = None
        for tq in range(4):
            for ic in range(64):
                p.waits("sync", [act_guard[0], act_guard[1], prev_tq_mm])
                act_ld[ic] = dma("sync", "qea", act_e[:, ic * 512:(ic + 1) * 512],
                                 act_h[ic, :, tq * 512:(tq + 1) * 512])
            act_ready = act_ld[63]
            dn_last = prev_tq_ev
            for hq4 in range(4):
                wd_tok = [None, None]
                for ic in range(64):
                    par = ic % 2
                    p.waits("sync", [wd_tok[par]] + bounce_fin)
                    t_wi = dma("sync", "qew%d" % par, wdt_e[par][:],
                               wd16_h[hq4 * 512:(hq4 + 1) * 512, ic * 128:(ic + 1) * 128],
                               transpose=True)
                    p.waits("tensor", [t_wi, act_ready])
                    for h4 in range(4):
                        wd_tok[par] = mm(h4 if ic == 0 else None, pb[h4][:, 0:512],
                                         wdt_e[par][:, h4 * 128:(h4 + 1) * 128],
                                         act_e[:, ic * 512:(ic + 1) * 512],
                                         start=(ic == 0), stop=(ic == 63), tok=(h4 == 3))
                t_dmm = wd_tok[(63) % 2]
                for h4 in range(4):
                    hc = hq4 * 4 + h4
                    par = h4 % 2
                    p.waits("sync", [x1f_guard[par], x1f_last])
                    t_xf = dma("sync", "qef%d" % par, x1f_e[par][:],
                               x1f_h[hc, :, tq * 512:(tq + 1) * 512])
                    t_e1 = vop("vector", lambda e, ps=pb[h4][:, 0:512],
                               a=asumbc[:, tq * 512:(tq + 1) * 512]:
                               e.scalar_tensor_tensor(t1_e[:], a, -128.0, ps, ALU.mult, ALU.add),
                               waits=[t_dmm, ab_last, dn_last])
                    t_e2 = vop("vector", lambda e, o=resq[:, hc * 512:(hc + 1) * 512],
                               s=colap(sdt[:], hc), xf=x1f_e[par][:]:
                               e.scalar_tensor_tensor(o, t1_e[:], s, xf, ALU.mult, ALU.add),
                               waits=[t_e1, t_xf], bank=h4)
                    x1f_guard[par] = t_e2
                    dn_last = t_e2
            mm_tok = [None, None]
            for hc in range(16):
                tc1 = vop("vector", lambda e, o=lnb_e[0][:], i=resq[:, hc * 512:(hc + 1) * 512]:
                          e.tensor_copy(o, i), waits=[dn_last, mm_tok[0]])
                mm_tok[0] = mm(0 if hc == 0 else None, pb[0][0:1, 0:512], onesb[:, 0:1],
                               lnb_e[0][:], start=(hc == 0), stop=(hc == 15), waits=[tc1],
                               tok=True)
                tc2 = vop("scalar", lambda e, o=lnb_e[1][:], i=resq[:, hc * 512:(hc + 1) * 512]:
                          e.activation(o, i, AF.Square), waits=[dn_last, mm_tok[1]])
                mm_tok[1] = mm(1 if hc == 0 else None, pb[1][0:1, 0:512], onesb[:, 0:1],
                               lnb_e[1][:], start=(hc == 0), stop=(hc == 15), waits=[tc2],
                               tok=True)
            t_mu = vop("vector", lambda e: e.tensor_scalar(murow[0:1, 0:512], pb[0][0:1, 0:512],
                                                           1.0 / 2048.0, None, ALU.mult),
                       waits=[mm_tok[0]], bank=0)
            t_m2 = vop("vector", lambda e: e.tensor_scalar(srow[0:1, 0:512], pb[1][0:1, 0:512],
                                                           1.0 / 2048.0, None, ALU.mult),
                       waits=[mm_tok[1]], bank=1)
            t_v1 = vop("vector", lambda e: e.tensor_tensor(s2row[0:1, 0:512], murow[0:1, 0:512],
                                                           murow[0:1, 0:512], ALU.mult),
                       waits=[t_mu])
            t_v2 = vop("vector", lambda e: e.tensor_sub(s2row[0:1, 0:512], srow[0:1, 0:512],
                                                        s2row[0:1, 0:512]), waits=[t_m2, t_v1])
            t_v3 = vop("scalar", lambda e: e.activation(s2row[0:1, 0:512], s2row[0:1, 0:512],
                                                        AF.Sqrt, bias=1e-5), waits=[t_v2])
            t_v4 = vop("vector", lambda e: e.reciprocal(invrow[0:1, 0:512], s2row[0:1, 0:512]),
                       waits=[t_v3])
            t_b1 = mm(2, pb[2][:, 0:512], onesrf[0:1, :], murow[0:1, 0:512], tok=True,
                      waits=[t_mu])
            t_b2 = mm(3, pb[3][:, 0:512], onesrf[0:1, :], invrow[0:1, 0:512], tok=True,
                      waits=[t_v4])
            nm_last = None
            for hc in range(16):
                t_n1 = vop("vector", lambda e, o=resq[:, hc * 512:(hc + 1) * 512]:
                           e.tensor_sub(o, o, pb[2][:, 0:512]),
                           waits=[t_b1, t_b2, mm_tok[0], mm_tok[1]])
                nm_last = vop("vector", lambda e, o=resq[:, hc * 512:(hc + 1) * 512]:
                              e.tensor_tensor(o, o, pb[3][:, 0:512], ALU.mult), waits=[t_n1])
            p.bank_guard[2] = nm_last
            p.bank_guard[3] = nm_last
            for tbl in range(4):
                tb = tq * 4 + tbl
                opar = tbl % 2
                ev_last = None
                for dc in range(16):
                    bk = 4 + dc % 2
                    t_tr = tr(bk, pb[bk][:, 0:128],
                              resq[:, dc * 512 + tbl * 128: dc * 512 + (tbl + 1) * 128], idf[:],
                              waits=[nm_last])
                    ev_last = vop("scalar", lambda e, o=outtm[opar][:, dc * 128:(dc + 1) * 128],
                                  i=pb[bk][:, 0:128]: e.copy(o, i),
                                  waits=[t_tr, outtm_guard[opar]], bank=bk)
                p.waits("scalar", [ev_last])
                y_last = dma("scalar", "qy%d" % opar, y_h[tb * 128:(tb + 1) * 128, :],
                             outtm[opar][:])
                outtm_guard[opar] = y_last
                y_fin[opar] = y_last
            prev_tq_mm = mm_tok[1]
            prev_tq_ev = ev_last

        stE.close()

        p.waits("scalar", y_fin)

        for s in sorted(DMA_SEMS) + ["sp", "sv", "sa", "sgp"]:
            p.handles[s] = ctx.enter_context(nc.semaphore(s))
        block = ctx.enter_context(nc.Block())

        @block.tensor
        def _(e):
            for f in p.plan["tensor"]:
                f(e)

        @block.vector
        def _(e):
            for f in p.plan["vector"]:
                f(e)

        @block.scalar
        def _(e):
            for f in p.plan["scalar"]:
                f(e)

        @block.gpsimd
        def _(e):
            for f in p.plan["gpsimd"]:
                f(e)

        @block.sync
        def _(e):
            for f in p.plan["sync"]:
                f(e)

    return nc


_cached = {}


def kernel(**inputs):
    if "nc" not in _cached:
        _cached["nc"] = build_program()
    nc = _cached["nc"]
    x = np.ascontiguousarray(inputs["L0_attn_x"])
    dtbv = np.zeros((64, 1), np.float32)
    algv = np.zeros((64, 1), np.float32)
    dtb_in = np.asarray(inputs["L0_attn_dt_bias"], np.float32)
    alog_in = np.asarray(inputs["L0_attn_A_log"], np.float32)
    for hv in range(HV):
        ach = (hv // 2) * 4 + 2 + hv % 2
        dtbv[ach, 0] = dtb_in[hv]
        algv[ach, 0] = alog_in[hv]
    cwl = np.ascontiguousarray(
        np.asarray(inputs["L0_attn_conv_w"], np.float32).reshape(16, 4, 128, 4)
        .transpose(2, 0, 1, 3).reshape(128, 256))
    common = {
        "wqkvz": np.ascontiguousarray(inputs["L0_attn_W_qkvz"], dtype=np.float32),
        "wba": np.ascontiguousarray(inputs["L0_attn_W_ba"], dtype=np.float32),
        "cw": cwl,
        "dtb": dtbv,
        "alog": algv,
        "normw": np.ascontiguousarray(np.asarray(inputs["L0_attn_norm_w"], np.float32).reshape(128, 1)),
        "wout": np.ascontiguousarray(inputs["L0_attn_W_out"], dtype=np.float32),
        "gw": np.ascontiguousarray(inputs["L0_ffn_gate_w"], dtype=np.int32),
        "uw": np.ascontiguousarray(inputs["L0_ffn_up_w"], dtype=np.int32),
        "dw": np.ascontiguousarray(inputs["L0_ffn_down_w"], dtype=np.int32),
        "sgt": np.ascontiguousarray(np.asarray(inputs["L0_ffn_gate_scale"], np.float32).reshape(64, 128).T),
        "sut": np.ascontiguousarray(np.asarray(inputs["L0_ffn_up_scale"], np.float32).reshape(64, 128).T),
        "sdt": np.ascontiguousarray(np.asarray(inputs["L0_ffn_down_scale"], np.float32).reshape(16, 128).T),
    }
    in_maps = []
    for b in range(8):
        m = dict(common)
        m["x"] = np.ascontiguousarray(x[b])
        in_maps.append(m)
    res = run_bass_kernel_spmd(nc, in_maps, core_ids=list(range(8)))
    return np.stack([res.results[b]["y"] for b in range(8)], axis=0)


# version 08
# speedup vs baseline: 1.1507x; 1.0126x over previous
import numpy as np
import ml_dtypes
from contextlib import ExitStack

import concourse.bass as bass
import concourse.mybir as mybir
from concourse.bass_utils import run_bass_kernel_spmd

F32 = mybir.dt.float32
BF16 = mybir.dt.bfloat16
I32 = mybir.dt.int32
AF = mybir.ActivationFunctionType
ALU = mybir.AluOpType
AX = mybir.AxisListType

T = 2048
D = 2048
HK = 16
HV = 32
C = 128
NCH = T // C
BIG = 1e30
DEBUG = False


class Prog:
    ENGINES = ("tensor", "vector", "scalar", "gpsimd", "sync")

    def __init__(self):
        self.plan = {k: [] for k in self.ENGINES}
        self.cnt = {}
        self.handles = {}
        self.watermark = {k: {} for k in self.ENGINES}
        self.bank_guard = {}

    def op(self, eng, fn, inc=None):
        tok = None
        if inc is not None:
            name, amt = inc
            self.cnt[name] = self.cnt.get(name, 0) + amt
            tok = (name, self.cnt[name])

        def wrapped(e, fn=fn, inc=inc):
            inst = fn(e)
            if inc is not None:
                inst.then_inc(self.handles[inc[0]], inc[1])
        self.plan[eng].append(wrapped)
        return tok

    def wait(self, eng, tok):
        if tok is None:
            return
        name, val = tok
        wm = self.watermark[eng]
        if wm.get(name, 0) >= val:
            return
        wm[name] = val
        self.plan[eng].append(lambda e, n=name, v=val: e.wait_ge(self.handles[n], v))

    def waits(self, eng, toks):
        best = {}
        for t in toks:
            if t is None:
                continue
            n, v = t
            if best.get(n, -1) < v:
                best[n] = v
        for n, v in best.items():
            self.wait(eng, (n, v))


def build_program():
    nc = bass.Bass("TRN2", target_bir_lowering=False)

    def inp(name, shape, dt):
        return nc.dram_tensor(name, shape, dt, kind="ExternalInput").ap()

    x_h = inp("x", [T, D], F32)
    wqkvz_h = inp("wqkvz", [D, 12288], F32)
    wba_h = inp("wba", [D, 64], F32)
    cw_h = inp("cw", [128, 256], F32)
    dtb_h = inp("dtb", [64, 1], F32)
    alog_h = inp("alog", [64, 1], F32)
    normw_h = inp("normw", [128, 1], F32)
    wout_h = inp("wout", [4096, D], F32)
    gw_h = inp("gw", [8192, D], I32)
    uw_h = inp("uw", [8192, D], I32)
    dw_h = inp("dw", [D, 8192], I32)
    sg_h = inp("sgt", [128, 64], F32)
    su_h = inp("sut", [128, 64], F32)
    sd_h = inp("sdt", [128, 16], F32)

    y_h = nc.dram_tensor("y", [T, D], F32, kind="ExternalOutput").ap()

    scr_kind = "ExternalOutput" if DEBUG else "Internal"
    xwtm_h = nc.dram_tensor("xwtm", [HK, T, 512], BF16, kind="Internal").ap()
    z_h = nc.dram_tensor("zscr", [HK, T, 256], BF16, kind=scr_kind).ap()
    o_h = nc.dram_tensor("oscr", [HV, T, 128], BF16, kind=scr_kind).ap()
    act_h = nc.dram_tensor("actscr", [64, 128, T], BF16, kind="Internal").ap()
    x1f_h = nc.dram_tensor("x1fscr", [16, 128, T], F32, kind=scr_kind).ap()
    x1b_h = nc.dram_tensor("x1bscr", [16, 128, T], BF16, kind="Internal").ap()
    wg16_h = nc.dram_tensor("wg16", [8192, D], BF16, kind="Internal").ap()
    wu16_h = nc.dram_tensor("wu16", [8192, D], BF16, kind="Internal").ap()
    wd16_h = nc.dram_tensor("wd16", [D, 8192], BF16, kind="Internal").ap()

    idf_c = nc.inline_tensor(np.eye(128, dtype=np.float32), "idf_c").ap()
    idb_c = nc.inline_tensor(np.eye(128, dtype=ml_dtypes.bfloat16), "idb_c").ap()
    e127_c = nc.inline_tensor(
        np.broadcast_to((np.arange(128) == 127).astype(np.float32)[:, None], (128, 128)).copy(),
        "e127_c").ap()
    onesb_c = nc.inline_tensor(np.ones((128, 8), ml_dtypes.bfloat16), "onesb_c").ap()
    onesrb_c = nc.inline_tensor(np.ones((8, 128), ml_dtypes.bfloat16), "onesrb_c").ap()
    onesrf_c = nc.inline_tensor(np.ones((8, 128), np.float32), "onesrf_c").ap()
    ones64_c = nc.inline_tensor(np.ones((64, 128), np.float32), "ones64_c").ap()
    mN = np.where(np.tril(np.ones((C, C)), -1) > 0, 0.0, BIG).astype(np.float32)
    mAL = np.where(np.triu(np.ones((C, C)), 0) > 0, 0.0, BIG).astype(np.float32)
    maskN_c = nc.inline_tensor(mN, "maskN_c").ap()
    maskAL_c = nc.inline_tensor(mAL, "maskAL_c").ap()
    CONSTVALS = [128.0, 1.0 / 128.0, -1.0, 128.0e-6, 1e-6, 1e-5, 1.0, 0.0]
    cvals_c = nc.inline_tensor(
        np.broadcast_to(np.asarray(CONSTVALS, np.float32)[None, :], (128, len(CONSTVALS))).copy(),
        "cvals_c").ap()

    p = Prog()

    with ExitStack() as ctx:
        def sb(name, shape, dt, st=None):
            return (st or ctx).enter_context(nc.sbuf_tensor(name, shape, dt))

        idf = sb("idf", [128, 128], F32)
        idb = sb("idb", [128, 128], BF16)
        e127 = sb("e127", [128, 128], F32)
        onesb = sb("onesb", [128, 8], BF16)
        onesrb = sb("onesrb", [8, 128], BF16)
        onesrf = sb("onesrf", [8, 128], F32)
        ones64 = sb("ones64", [64, 128], F32)
        maskN = sb("maskN", [128, 128], F32)
        maskAL = sb("maskAL", [128, 128], F32)
        cw = sb("cw_s", [128, 256], F32)
        dtb = sb("dtb_s", [64, 1], F32)
        alog = sb("alog_s", [64, 1], F32)
        nega = sb("nega", [64, 1], F32)
        normw = sb("normw_s", [128, 1], F32)
        sgt = sb("sgt_s", [128, 64], F32)
        sut = sb("sut_s", [128, 64], F32)
        sdt = sb("sdt_s", [128, 16], F32)
        cvals = sb("cvals", [128, 8], F32)
        rows4 = sb("rows4", [1, 2048], F32)
        srow = rows4.ap()[:, 0:512]
        s2row = rows4.ap()[:, 512:1024]
        murow = rows4.ap()[:, 1024:1536]
        invrow = rows4.ap()[:, 1536:2048]

        stSC = ExitStack()
        SCb = sb("SCb", [128, 1024], F32, stSC)
        SCbneg = sb("SCbneg", [128, 1024], F32, stSC)
        SCg = sb("SCg", [128, 1024], F32, stSC)
        SCeg = sb("SCeg", [128, 1024], F32, stSC)
        SCbe = sb("SCbe", [128, 1024], F32, stSC)
        SCkd = sb("SCkd", [128, 1024], F32, stSC)
        SCgl = sb("SCgl", [128, 1024], F32, stSC)
        wbounce = [sb("wb%d" % i, [128, 2048], BF16, stSC) for i in range(2)]

        pb = [ctx.enter_context(nc.psum_tensor("pb%d" % i, [128, 512], F32)) for i in range(6)]
        pbh = [ctx.enter_context(nc.psum_tensor("pbh%d" % i, [128, 128], BF16)) for i in range(2)]

        DMA_SEMS = set()

        def dma(q, sem, outap, inap, transpose=False):
            DMA_SEMS.add(sem)
            return p.op(q, lambda e, o=outap, i=inap, tr=transpose:
                        e.dma_start(out=o, in_=i, transpose=tr),
                        inc=(sem, 16))

        def mm(bank, outap, lhsT, rhs, start=True, stop=True, tok=False, waits=()):
            w = list(waits)
            if start and bank is not None:
                g = p.bank_guard.get(bank)
                w += (g if isinstance(g, list) else [g])
            p.waits("tensor", w)
            return p.op("tensor", lambda e, o=outap, l=lhsT, r=rhs, s1=start, s2=stop:
                        e.matmul(o, l, r, start=s1, stop=s2),
                        inc=("sp", 1) if tok else None)

        def tr(bank, outap, inap, ident, tok=True, waits=()):
            g = p.bank_guard.get(bank)
            w = list(waits) + (g if isinstance(g, list) else [g])
            p.waits("tensor", w)
            return p.op("tensor", lambda e, o=outap, i=inap, d=ident:
                        e.transpose(o, i, d), inc=("sp", 1) if tok else None)

        def vop(eng, fn, tok=True, waits=(), bank=None):
            sem = {"vector": "sv", "scalar": "sa", "gpsimd": "sgp"}[eng]
            p.waits(eng, waits)
            t = p.op(eng, fn, inc=(sem, 1) if tok else None)
            if bank is not None:
                p.bank_guard[bank] = t
            return t

        def colap(tile_ap, col):
            return tile_ap[:, col:col + 1]

        def bcast_lhsT(tile_ap, col, n=128):
            base = tile_ap[:, col:col + 1]
            return bass.AP(base.tensor, base.offset, [base.ap[0], [0, n]])

        def phase_barrier(dma_toks=()):
            toks = [("sp", p.cnt.get("sp", 0)), ("sv", p.cnt.get("sv", 0)),
                    ("sa", p.cnt.get("sa", 0)), ("sgp", p.cnt.get("sgp", 0))]
            toks += [t for t in dma_toks if t is not None]
            for eng in ("tensor", "vector", "scalar", "gpsimd", "sync"):
                p.waits(eng, toks)

        for dst, src in ((idf, idf_c), (idb, idb_c), (e127, e127_c), (onesb, onesb_c),
                         (onesrb, onesrb_c), (onesrf, onesrf_c), (ones64, ones64_c),
                         (maskN, maskN_c), (maskAL, maskAL_c), (cvals, cvals_c),
                         (cw, cw_h), (dtb, dtb_h),
                         (alog, alog_h), (normw, normw_h), (sgt, sg_h), (sut, su_h),
                         (sdt, sd_h)):
            tok_consts = dma("sync", "qcst", dst[:], src)

        consts_tok = tok_consts
        for _eng in ("tensor", "vector", "scalar", "gpsimd"):
            p.waits(_eng, [consts_tok])
        for _i, _v in enumerate(CONSTVALS):
            nc.const_aps.register(F32, _v, cvals.ap()[:, _i:_i + 1])

        bounce_done = []
        prev_b = [None, None]
        pieces = ([(gw_h, wg16_h, s, 0) for s in range(64)]
                  + [(uw_h, wu16_h, s, 0) for s in range(64)]
                  + [(dw_h, wd16_h, s, q) for s in range(16) for q in range(4)])
        for pi, (srct, dstt, s, q) in enumerate(pieces):
            par = pi % 2
            p.waits("gpsimd", [prev_b[par]])
            t_in = dma("gpsimd", "qb%d" % par, wbounce[par][:],
                       srct[s * 128:(s + 1) * 128, q * 2048:(q + 1) * 2048])
            p.waits("gpsimd", [t_in])
            t_out = dma("gpsimd", "qb%d" % par, dstt[s * 128:(s + 1) * 128, q * 2048:(q + 1) * 2048],
                        wbounce[par][:])
            prev_b[par] = t_out
            bounce_done.append(t_out)
        bounce_fin = [prev_b[0], prev_b[1]]

        stA = ExitStack()
        xfm = sb("xfm", [128, 16 * 2048], BF16, stA)
        wbab = sb("wbab", [128, 16 * 64], BF16, stA)
        stA2 = ExitStack()
        whb = [sb("whb%d" % i, [128, 16 * 768], BF16, stA2) for i in range(2)]
        xwtm_b = sb("xwtmb0", [128, 16 * 512], BF16, stA2)
        zstr = sb("zstr", [128, 16 * 256], BF16, stA2)
        xin = [sb("xin%d" % i, [128, 2048], F32, stA2) for i in range(2)]

        t_wba = dma("gpsimd", "qwba", wbab[:].rearrange("q (c w) -> q c w", w=64),
                    wba_h.rearrange("(c q) w -> q c w", q=128))
        consts_tok2 = t_wba
        for _eng in ("tensor", "vector", "scalar", "gpsimd"):
            p.waits(_eng, [consts_tok2])

        xin_used = [None, None]
        for tt in range(16):
            par = tt % 2
            p.waits("sync", [xin_used[par]])
            tin = dma("sync", "qx%d" % par, xin[par][:], x_h[tt * 128:(tt + 1) * 128, :])
            for dc in range(16):
                bk = 4 + dc % 2
                ttr = tr(bk, pb[bk][:, 0:128], xin[par][:, dc * 128:(dc + 1) * 128], idf[:],
                         waits=[tin])
                te = vop("scalar", lambda e,
                         o=xfm[:, dc * 2048 + tt * 128: dc * 2048 + (tt + 1) * 128],
                         i=pb[bk][:, 0:128]: e.copy(o, i), waits=[ttr], bank=bk)
            xin_used[par] = ttr
        xfm_ready = te

        wh_free = [None, None]
        wh_tok = [None, None]
        xw_spill = None
        z_spill = None
        for h in range(HK):
            par = h % 2
            if h == 0:
                p.waits("gpsimd", [wh_free[par]])
                wh_tok[par] = dma("gpsimd", "qwh%d" % par,
                                  whb[par][:].rearrange("q (c w) -> q c w", w=768),
                                  wqkvz_h[:, 768 * h:768 * (h + 1)].rearrange(
                                      "(c q) w -> q c w", q=128))
            if h + 1 < HK:
                npar = (h + 1) % 2
                p.waits("gpsimd", [wh_free[npar]])
                wh_tok[npar] = dma("gpsimd", "qwh%d" % npar,
                                   whb[npar][:].rearrange("q (c w) -> q c w", w=768),
                                   wqkvz_h[:, 768 * (h + 1):768 * (h + 2)].rearrange(
                                       "(c q) w -> q c w", q=128))
            ev_mixed = []
            ev_z = []
            for tt in range(16):
                bk = tt % 2
                p.waits("tensor", [xfm_ready, wh_tok[par]])
                for dc in range(16):
                    t_mm = mm(bk if dc == 0 else None, pb[bk][:, 0:512],
                              xfm[:, dc * 2048 + tt * 128: dc * 2048 + (tt + 1) * 128],
                              whb[par][:, dc * 768: dc * 768 + 512],
                              start=(dc == 0), stop=(dc == 15), tok=(dc == 15))
                te = vop("vector", lambda e, o=xwtm_b[:, tt * 512:(tt + 1) * 512],
                         i=pb[bk][:, 0:512]: e.tensor_copy(o, i),
                         waits=[t_mm, xw_spill], bank=bk)
                ev_mixed.append(te)
                bz = 2 + tt % 2
                for dc in range(16):
                    t_mmz = mm(bz if dc == 0 else None, pb[bz][:, 0:256],
                               xfm[:, dc * 2048 + tt * 128: dc * 2048 + (tt + 1) * 128],
                               whb[par][:, dc * 768 + 512: dc * 768 + 768],
                               start=(dc == 0), stop=(dc == 15), tok=(dc == 15))
                tez = vop("scalar", lambda e, o=zstr[:, tt * 256:(tt + 1) * 256],
                          i=pb[bz][:, 0:256]: e.copy(o, i),
                          waits=[t_mmz, z_spill], bank=bz)
                ev_z.append(tez)
            wh_free[par] = t_mmz
            p.waits("gpsimd", ev_mixed + ev_z)
            xw_spill = dma("gpsimd", "qxw0",
                           xwtm_h[h].rearrange("(tt q) c -> q tt c", q=128),
                           xwtm_b[:].rearrange("q (tt c) -> q tt c", c=512))
            z_spill = dma("gpsimd", "qz", z_h[h].rearrange("(tt q) c -> q tt c", q=128),
                          zstr[:].rearrange("q (tt c) -> q tt c", c=256))
        xw_spill_all = [xw_spill, xw_spill]
        z_spill_all = [z_spill]

        stA2.close()
        bapk = [sb("bapk%d" % i, [128, 2048], F32, stA) for i in range(3)]
        bafm = bapk[0].ap()[0:64, :]
        bsig = bapk[0].ap()[64:128, :]
        gpre = bapk[1].ap()[0:64, :]
        gcum = bapk[1].ap()[64:128, :]
        spt = bapk[2].ap()[0:64, :]
        spt2 = bapk[2].ap()[64:128, :]

        ba_toks = []
        for ts4 in range(4):
            bk = ts4 % 2
            p.waits("tensor", [xfm_ready, t_wba])
            for dc in range(16):
                t_mm = mm(bk if dc == 0 else None, pb[bk][0:64, 0:512],
                          wbab[:, dc * 64:(dc + 1) * 64],
                          xfm[:, dc * 2048 + ts4 * 512: dc * 2048 + (ts4 + 1) * 512],
                          start=(dc == 0), stop=(dc == 15), tok=(dc == 15))
            te = vop("vector", lambda e, o=bafm[:, ts4 * 512:(ts4 + 1) * 512],
                     i=pb[bk][0:64, 0:512]: e.tensor_copy(o, i), waits=[t_mm], bank=bk)
            ba_toks.append(te)
        a2_pe_done = t_mm

        p.waits("scalar", ba_toks)
        t_sig = vop("scalar", lambda e: e.activation(bsig, bafm, AF.Sigmoid))
        t_sp1 = vop("scalar", lambda e: e.activation(spt, bafm, AF.Abs, bias=dtb[:]))
        t_sp2 = vop("scalar", lambda e: e.activation(spt2, spt, AF.Exp, scale=-1.0), waits=[t_sp1])
        t_sp3 = vop("scalar", lambda e: e.activation(gpre, spt2, AF.Ln, bias=1.0), waits=[t_sp2])
        t_sp4 = vop("scalar", lambda e: e.activation(spt, bafm, AF.Relu, bias=dtb[:]), waits=[t_sp2])
        t_na1 = vop("scalar", lambda e: e.activation(nega[:], alog[:], AF.Exp))
        t_sp5 = vop("vector", lambda e: e.tensor_add(gpre, gpre, spt), waits=[t_sp3, t_sp4])
        t_na2 = vop("vector", lambda e: e.tensor_scalar_mul(nega[:], nega[:], -1.0), waits=[t_na1])
        t_gp = vop("vector", lambda e: e.tensor_scalar(gpre, gpre, nega[:], None, ALU.mult),
                   waits=[t_sp5, t_na2])
        cs_toks = []
        for n in range(NCH):
            tcs = vop("vector", lambda e, o=gcum[:, n * 128:(n + 1) * 128],
                      i=gpre[:, n * 128:(n + 1) * 128]:
                      e.tensor_tensor_scan(o, i, ones64[:], 0.0, ALU.add, ALU.mult),
                      waits=[t_gp])
            cs_toks.append(tcs)
        sc_toks = []
        for n in range(NCH):
            bk = 4 + n % 2
            t1 = tr(bk, pb[bk][0:128, 0:64], bsig[:, n * 128:(n + 1) * 128],
                    idf[64:128, 64:128], waits=[t_sig])
            e1 = vop("scalar", lambda e, o=SCb[:, n * 64:(n + 1) * 64], i=pb[bk][0:128, 0:64]:
                     e.copy(o, i), waits=[t1], bank=bk)
            t2 = tr(bk, pb[bk][0:128, 64:128], gcum[:, n * 128:(n + 1) * 128],
                    idf[64:128, 64:128], waits=[cs_toks[n]])
            e2 = vop("scalar", lambda e, o=SCg[:, n * 64:(n + 1) * 64], i=pb[bk][0:128, 64:128]:
                     e.copy(o, i), waits=[t2], bank=bk)
            sc_toks += [e1, e2]
        t_scbn = vop("vector", lambda e: e.tensor_scalar_mul(SCbneg[:], SCb[:], -1.0), waits=sc_toks)
        t_sceg = vop("scalar", lambda e: e.activation(SCeg[:], SCg[:], AF.Exp), waits=sc_toks)
        gl_toks = []
        for half in range(2):
            bk = half % 2
            tm = mm(bk, pb[bk][:, 0:512], e127[:], SCg[:, half * 512:(half + 1) * 512],
                    tok=True, waits=sc_toks)
            te = vop("vector", lambda e, o=SCkd[:, half * 512:(half + 1) * 512], i=pb[bk][:, 0:512]:
                     e.tensor_copy(o, i), waits=[tm], bank=bk)
            gl_toks.append(te)
        t_gl2 = vop("scalar", lambda e: e.activation(SCgl[:], SCkd[:], AF.Exp), waits=gl_toks)
        t_kd1 = vop("vector", lambda e: e.tensor_sub(SCkd[:], SCkd[:], SCg[:]), waits=gl_toks + [t_gl2])
        t_kd2 = vop("scalar", lambda e: e.activation(SCkd[:], SCkd[:], AF.Exp), waits=[t_kd1])
        t_be = vop("vector", lambda e: e.tensor_tensor(
            SCbe[:].rearrange("q (n m j) -> q n m j", m=16, j=4)[:, :, :, 0:2],
            SCb[:].rearrange("q (n m j) -> q n m j", m=16, j=4)[:, :, :, 0:2],
            SCeg[:].rearrange("q (n m j) -> q n m j", m=16, j=4)[:, :, :, 2:4],
            ALU.mult), waits=[t_sceg] + sc_toks)
        sc_ready = [t_scbn, t_sceg, t_kd2, t_gl2, t_be]

        stA.close()

        stB = ExitStack()
        gy = sb("gy", [128, 4 * 2048], BF16, stB)
        qhat = sb("qhat", [128, 4 * 2048], BF16, stB)
        khat = sb("khat", [128, 4 * 2048], BF16, stB)
        khatT = sb("khatT", [128, 4 * 2048], BF16, stB)
        vTt = sb("vTt", [128, 8 * 2048], BF16, stB)
        acc_b = sb("acc_b", [128, 2048], F32, stB)
        sqb = sb("sqb", [128, 2048], BF16, stB)
        arow = sb("arow", [1, 2048], F32, stB)
        arowb = sb("arowb", [1, 2048], BF16, stB)
        slots = []
        for s in range(3):
            sl = {
                "xnxt": [sb("xnxt%d_%d" % (s, i), [128, 256], BF16, stB) for i in range(2)],
                "tch": [sb("tch%d_%d" % (s, i), [128, 128], BF16, stB) for i in range(2)],
                "rti": sb("rti%d" % s, [128, 256], BF16, stB),
                "wtn": sb("wtn%d" % s, [128, 128], BF16, stB),
                "vnew": sb("vnew%d" % s, [128, 128], BF16, stB),
                "kdec": sb("kdec%d" % s, [128, 128], BF16, stB),
                "alT": sb("alT%d" % s, [128, 128], BF16, stB),
                "s32": sb("s32_%d" % s, [128, 128], F32, stB),
                "s16": sb("s16_%d" % s, [128, 128], BF16, stB),
                "qs_b": sb("qs_b%d" % s, [128, 128], F32, stB),
                "dprN": sb("dprN%d" % s, [128, 128], F32, stB),
                "dprA": sb("dprA%d" % s, [128, 128], F32, stB),
                "decN": sb("decN%d" % s, [128, 128], F32, stB),
                "decA": sb("decA%d" % s, [128, 128], F32, stB),
                "ostrip": sb("ostrip%d" % s, [128, NCH * 128], BF16, stB),
                "bkA": pb[2 * s], "bkB": pb[2 * s + 1], "pbh_id": 6 + (s % 2),
                "st": {}, "ospill": None,
            }
            slots.append(sl)

        state = {"ospill": [None, None, None], "o_last": None}

        def conv_head(h, gy_free_tok):
            p.waits("gpsimd", [gy_free_tok, xw_spill_all[h % 2]])
            fold_toks = []
            for tt in range(16):
                gt = tt // 4
                po = (tt % 4) * 32
                tf = dma("gpsimd", "qfold",
                         gy[po:po + 32, gt * 2048:(gt + 1) * 2048].rearrange(
                             "q (r c) -> q r c", r=4),
                         xwtm_h[h, tt * 128:(tt + 1) * 128, :])
                fold_toks.append(tf)
            conv_toks = []
            for gt in range(4):
                gslice = gy[:, gt * 2048:(gt + 1) * 2048]
                w3 = colap(cw[:], h * 16 + gt * 4 + 3)
                tc = vop("vector", lambda e, o=acc_b[:], i=gslice, s=w3:
                         e.tensor_scalar(o, i, s, None, ALU.mult),
                         waits=[fold_toks[-1]] + conv_toks[-1:])
                for k in (2, 1, 0):
                    sft = 3 - k
                    wk = colap(cw[:], h * 16 + gt * 4 + k)
                    tc = vop("vector", lambda e, o=acc_b[:, sft:2048],
                             i=gslice[:, 0:2048 - sft], s=wk, a=acc_b[:, sft:2048]:
                             e.scalar_tensor_tensor(o, i, s, a, ALU.mult, ALU.add),
                             waits=[tc])
                ts_ = vop("scalar", lambda e, o=gslice, i=acc_b[:]:
                          e.activation(o, i, AF.Silu), waits=[tc])
                conv_toks.append(ts_)
            return conv_toks

        def emit_hat(dst_tile, is_q, gy_toks, dst_guard):
            last = None
            tre_prev = None
            tb_prev = None
            for loc in range(4):
                gslice = gy[:, loc * 2048:(loc + 1) * 2048]
                tq = vop("scalar", lambda e, o=sqb[:], i=gslice:
                         e.activation(o, i, AF.Square), waits=gy_toks + [last])
                mv_last = tre_prev
                for q4 in range(4):
                    tm4 = mm(4, pb[4][0:1, 0:512], onesb[:, 0:1], sqb[:, q4 * 512:(q4 + 1) * 512],
                             tok=True, waits=[tq, mv_last])
                    scale = 128.0 if is_q else 1.0
                    bias = 128.0e-6 if is_q else 1e-6
                    mv_last = vop("scalar", lambda e, o=arow[0:1, q4 * 512:(q4 + 1) * 512],
                                  i=pb[4][0:1, 0:512], s=scale, b=bias:
                                  e.activation(o, i, AF.Sqrt, bias=b, scale=s), waits=[tm4],
                                  bank=4)
                def _recip_bf(e):
                    with nc.allow_low_precision(reason="l2norm scale as bf16 matmul operand"):
                        return e.reciprocal(arowb[:], arow[:])
                trb = vop("vector", _recip_bf, waits=[mv_last, tb_prev])
                tre_prev = trb
                for q4 in range(4):
                    tb = mm(4, pb[4][:, 0:512], onesrb[0:1, :], arowb[0:1, q4 * 512:(q4 + 1) * 512],
                            tok=True, waits=[trb])
                    tb_prev = tb
                    last = vop("vector", lambda e,
                               o=dst_tile[:, loc * 2048 + q4 * 512: loc * 2048 + (q4 + 1) * 512],
                               i=gslice[:, q4 * 512:(q4 + 1) * 512], b=pb[4][:, 0:512]:
                               e.tensor_tensor(o, i, b, ALU.mult),
                               waits=[tb, dst_guard], bank=4)
            return last

        def emit_transposes(src_tile, dst_tile, src_toks, dst_guard):
            last = None
            for loc in range(4):
                for n in range(NCH):
                    bk = 6 + n % 2
                    tt_ = tr(bk, pbh[n % 2][:],
                             src_tile[:, loc * 2048 + n * 128: loc * 2048 + (n + 1) * 128],
                             idb[:], waits=src_toks + [dst_guard])
                    last = vop("scalar", lambda e,
                               o=dst_tile[:, (loc * NCH + n) * 128:(loc * NCH + n + 1) * 128],
                               i=pbh[n % 2][:]: e.copy(o, i), waits=[tt_], bank=bk)
            return last

        def delta_setup(sl, hv, a, vloc):
            st = sl["st"]
            s_idx = slots.index(sl)
            gA = p.bank_guard.get(2 * s_idx)
            gB = p.bank_guard.get(2 * s_idx + 1)
            for g in (gA if isinstance(gA, list) else [gA]):
                if g is not None:
                    st["Abank_dve"] = g if st.get("Abank_dve") is None else max(
                        st["Abank_dve"], g, key=lambda t: (t[0], t[1])) if False else g
            st["seedA"] = (gA if isinstance(gA, list) else [gA])
            st["seedB"] = (gB if isinstance(gB, list) else [gB])
            p.bank_guard[2 * s_idx] = None
            p.bank_guard[2 * s_idx + 1] = None
            st["hv"] = hv
            st["loc"] = hv // 2 - 4 * a
            st["ach"] = (hv // 2) * 4 + 2 + hv % 2
            st["bch"] = (hv // 2) * 4 + hv % 2
            st["vloc"] = vloc
            t_s0 = vop("gpsimd", lambda e, t=sl["s32"]: e.memset(t[:], 0.0),
                       waits=[st.get("s_used"), st.get("s16_used")])
            t_s1 = vop("gpsimd", lambda e, t=sl["s16"]: e.memset(t[:], 0.0),
                       waits=[st.get("s16_used")])
            st["sready"] = [t_s0, t_s1]

        def delta_chunk(sl, n, qk_toks, vt_tok):
            st = sl["st"]
            loc, ach, bch, vloc = st["loc"], st["ach"], st["bch"], st["vloc"]
            bkA, bkB = sl["bkA"], sl["bkB"]
            kcol = loc * 2048 + n * 128
            ktcol = (loc * NCH + n) * 128
            vtcol = (vloc * NCH + n) * 128
            scol = n * 64
            t_g1 = mm(None, bkA[:, 0:128], khat[:, kcol:kcol + 128], khat[:, kcol:kcol + 128],
                      tok=True, waits=qk_toks + [st.get("A0"), st.get("Abank_dve"),
                                                 st.get("Abank_act")]
                      + st.pop("seedA", []))
            t_g2 = mm(None, bkA[:, 128:256], khat[:, kcol:kcol + 128],
                      qhat[:, kcol:kcol + 128], tok=True, waits=[st.get("A1")])
            t_e = mm(None, bkA[:, 256:384], bcast_lhsT(SCg[:], scol + ach), idf[:],
                     tok=True, waits=sc_ready + [st.get("A2")])
            t_dn = vop("vector", lambda e, ps=bkA[:, 256:384], gc=colap(SCg[:], scol + ach),
                       o=sl["dprN"][:]:
                       e.scalar_tensor_tensor(o, ps, gc, maskN[:], ALU.subtract, ALU.add),
                       waits=[t_e, st.get("dprN_used")])
            t_dn2 = vop("scalar", lambda e, o=sl["decN"][:], i=sl["dprN"][:]:
                        e.activation(o, i, AF.Exp, scale=-1.0),
                        waits=[t_dn, st.get("decN_used")])
            st["dprN_used"] = t_dn2
            t_da = vop("vector", lambda e, ps=bkA[:, 256:384], gc=colap(SCg[:], scol + ach),
                       o=sl["dprA"][:]:
                       e.scalar_tensor_tensor(o, ps, gc, maskAL[:], ALU.subtract, ALU.subtract),
                       waits=[t_e, st.get("dprA_used"), t_dn])
            st["A2"] = t_da
            t_da2 = vop("scalar", lambda e, o=sl["decA"][:], i=sl["dprA"][:]:
                        e.activation(o, i, AF.Exp), waits=[t_da, st.get("decA_used")])
            st["dprA_used"] = t_da2
            xn0 = sl["xnxt"][0]
            t_xn = vop("vector", lambda e, ps=bkA[:, 0:128], bc=colap(SCbneg[:], scol + bch),
                       o=xn0[:, 0:128], d=sl["decN"][:]:
                       e.scalar_tensor_tensor(o, ps, bc, d, ALU.mult, ALU.mult),
                       waits=[t_g1, t_dn2, st.get("xn_used")])
            st["decN_used"] = t_xn
            st["A0"] = t_xn
            t_al = vop("vector", lambda e, ps=bkA[:, 128:256], o=sl["alT"][:], d=sl["decA"][:]:
                       e.tensor_tensor(o, ps, d, ALU.mult),
                       waits=[t_g2, t_da2, st.get("al_used")])
            st["decA_used"] = t_al
            st["A1"] = t_al
            ph = sl["pbh_id"]
            t_xt = tr(ph, pbh[ph - 6][:], xn0[:, 0:128], idb[:], waits=[t_xn])
            t_xte = vop("scalar", lambda e, o=xn0[:, 128:256], i=pbh[ph - 6][:]:
                        e.copy(o, i), waits=[t_xt], bank=ph)
            t_t0 = vop("vector", lambda e, o=sl["tch"][0][:], i=xn0[:, 128:256]:
                       e.tensor_add(o, i, idb[:]), waits=[t_xte, st.get("tch_used")])
            cur = 0
            t_prev_t = t_t0
            t_prev_x = [t_xn, t_xte]
            for lev in range(6):
                nxt = 1 - cur
                xc, xv = sl["xnxt"][cur], sl["xnxt"][nxt]
                t_sq1 = mm(None, bkB[:, 0:128], xc[:, 128:256], xc[:, 0:128],
                           tok=(lev == 5), waits=t_prev_x + [st.get("Bsq")]
                           + st.pop("seedB", []))
                if lev < 5:
                    t_sq2 = mm(None, bkB[:, 128:256], xc[:, 0:128], xc[:, 128:256], tok=True)
                    t_ev = vop("vector", lambda e, o=xv[:, 0:256], i=bkB[:, 0:256]:
                               e.tensor_copy(o, i), waits=[t_sq2])
                else:
                    t_ev = vop("vector", lambda e, o=xv[:, 0:128], i=bkB[:, 0:128]:
                               e.tensor_copy(o, i), waits=[t_sq1])
                st["Bsq"] = t_ev
                t_xi_last = t_ev
                mm(None, bkA[:, 384:512], xv[:, 0:128], sl["tch"][cur][:],
                   start=True, stop=False, waits=[t_ev, t_prev_t, st.get("A3")])
                t_ap = mm(None, bkA[:, 384:512], idb[:], sl["tch"][cur][:],
                          start=False, stop=True, tok=True)
                t_te = vop("scalar", lambda e, o=sl["tch"][nxt][:], i=bkA[:, 384:512]:
                           e.copy(o, i), waits=[t_ap])
                st["A3"] = t_te
                t_prev_t = t_te
                t_prev_x = [t_ev]
                cur = nxt
            t_chain = t_prev_t
            st["xn_used"] = t_xi_last
            t_r1 = vop("vector", lambda e, i=vTt[:, vtcol:vtcol + 128],
                       s=colap(SCb[:], scol + bch), o=sl["rti"][:, 0:128]:
                       e.tensor_scalar(o, i, s, None, ALU.mult),
                       waits=[vt_tok, st.get("r_used")] + sc_ready)
            t_r2 = vop("vector", lambda e, i=khatT[:, ktcol:ktcol + 128],
                       s=colap(SCbe[:], scol + bch), o=sl["rti"][:, 128:256]:
                       e.tensor_scalar(o, i, s, None, ALU.mult), waits=qk_toks)
            t_kd = vop("gpsimd", lambda e, i=khatT[:, ktcol:ktcol + 128],
                       s=colap(SCkd[:], scol + ach), o=sl["kdec"][:]:
                       e.tensor_scalar(o, i, s, None, ALU.mult),
                       waits=qk_toks + [st.get("kd_used")] + sc_ready)
            t_wt = mm(None, bkB[:, 0:128], sl["rti"][:, 128:256], sl["tch"][cur][:], tok=True,
                      waits=[t_chain, t_r2, st.get("Bsq")])
            st["tch_used"] = t_wt
            st["r_used"] = t_wt
            t_wtn = vop("scalar", lambda e, o=sl["wtn"][:], i=bkB[:, 0:128]:
                        e.mul(o, i, -1.0), waits=[t_wt, st.get("wtn_used")])
            st["Bsq"] = t_wtn
            t_uw = mm(None, bkB[:, 256:512], sl["tch"][cur][:], sl["rti"][:, 0:256],
                      start=True, stop=False, tok=True,
                      waits=[t_chain, t_r1, t_r2, t_wtn, st.get("Buw")])
            t_vn = mm(None, bkB[:, 256:384], sl["wtn"][:], sl["s16"][:],
                      start=False, stop=True, tok=True, waits=[t_wtn] + st["sready"])
            st["wtn_used"] = t_vn
            t_vne = vop("vector", lambda e, o=sl["vnew"][:], i=bkB[:, 256:384]:
                        e.tensor_copy(o, i), waits=[t_vn, st.get("vn_used")])
            st["Buw"] = t_vne
            t_q = mm(None, bkA[:, 0:128], qhat[:, kcol:kcol + 128], sl["s16"][:], tok=True,
                     waits=qk_toks + st["sready"] + [st.get("A0")])
            t_qe = vop("scalar", lambda e, s=colap(SCeg[:], scol + ach), o=sl["qs_b"][:],
                       i=bkA[:, 0:128]:
                       e.activation(o, i, AF.Identity, scale=s),
                       waits=[t_q, st.get("qs_used")])
            st["A0"] = t_qe
            t_a = mm(None, bkA[:, 128:256], sl["alT"][:], sl["vnew"][:], tok=True,
                     waits=[t_vne, t_al, t_qe, st.get("A1")])
            st["al_used"] = t_a
            t_oc = vop("vector", lambda e, o=sl["ostrip"][:, n * 128:(n + 1) * 128],
                       i=sl["qs_b"][:], ps=bkA[:, 128:256]:
                       e.tensor_tensor(o, i, ps, ALU.add),
                       waits=[t_qe, t_a, sl["ospill"]])
            st["qs_used"] = t_oc
            st["A1"] = t_oc
            st["o_done"] = t_oc
            t_s = mm(None, bkA[:, 256:384], sl["kdec"][:], sl["vnew"][:], tok=True,
                     waits=[t_kd, t_vne, t_oc, st.get("A2")])
            st["vn_used"] = t_s
            st["kd_used"] = t_s
            t_su = vop("vector", lambda e, s=colap(SCgl[:], scol + ach), o=sl["s32"][:],
                       ps=bkA[:, 256:384]:
                       e.scalar_tensor_tensor(o, o, s, ps, ALU.mult, ALU.add),
                       waits=[t_s] + st["sready"])
            st["A2"] = t_su
            t_s16 = vop("scalar", lambda e, o=sl["s16"][:], i=sl["s32"][:]:
                        e.copy(o, i), waits=[t_su])
            st["s_used"] = t_su
            st["s16_used"] = t_s16
            st["sready"] = [t_su, t_s16]
            st["Abank_dve"] = t_su
            st["Abank_act"] = t_qe

        def delta_finish(sl):
            st = sl["st"]
            hv = st["hv"]
            s_idx = slots.index(sl)
            p.bank_guard[2 * s_idx] = [st.get("Abank_dve"), st.get("Abank_act")]
            p.bank_guard[2 * s_idx + 1] = [st.get("Buw"), st.get("Bsq")]
            p.waits("gpsimd", [st["o_done"]])
            tsp = dma("gpsimd", "qo%d" % s_idx, o_h[hv].rearrange("(n q) d -> q n d", q=128),
                      sl["ostrip"][:].rearrange("q (n d) -> q n d", d=128))
            sl["ospill"] = tsp
            state["ospill"][s_idx] = tsp
            state["o_last"] = tsp
            return tsp

        phase_barrier([xw_spill_all[0], xw_spill_all[1], z_spill_all[0], t_wba])
        o_fin = state["ospill"]
        o_spill_last = None
        gy_free = a2_pe_done
        hat_guard_q = None
        hat_guard_k = None
        hat_guard_kt = None
        hat_guard_v = None
        for a in range(4):
            cq = conv_head(a, gy_free)
            q_tok = emit_hat(qhat, True, cq, hat_guard_q)
            gy_free = q_tok
            ck = conv_head(4 + a, gy_free)
            k_tok = emit_hat(khat, False, ck, hat_guard_k)
            kt_tok = emit_transposes(khat, khatT, [k_tok], hat_guard_kt)
            gy_free = k_tok
            qk_tok_max = [q_tok, k_tok, kt_tok]
            cv = conv_head(8 + 2 * a, gy_free)
            vt0 = emit_transposes(gy, vTt[:, 0:4 * NCH * 128], cv, hat_guard_v)
            gy_free = vt0
            grp = [(8 * a + 0, 0, vt0), (8 * a + 1, 1, vt0), (8 * a + 2, 2, vt0)]
            for s_i, (hv, vloc, vt) in enumerate(grp):
                delta_setup(slots[s_i], hv, a, vloc)
            for n in range(NCH):
                for s_i, (hv, vloc, vt) in enumerate(grp):
                    delta_chunk(slots[s_i], n, qk_tok_max, vt)
            for s_i in range(len(grp)):
                o_spill_last = delta_finish(slots[s_i])
            hat_guard_v = state["o_last"]
            cv = conv_head(8 + 2 * a + 1, gy_free)
            vt1 = emit_transposes(gy, vTt[:, 4 * NCH * 128:8 * NCH * 128], cv, hat_guard_v)
            gy_free = vt1
            grp = [(8 * a + 3, 3, vt0), (8 * a + 4, 4, vt1), (8 * a + 5, 5, vt1)]
            for s_i, (hv, vloc, vt) in enumerate(grp):
                delta_setup(slots[s_i], hv, a, vloc)
            for n in range(NCH):
                for s_i, (hv, vloc, vt) in enumerate(grp):
                    delta_chunk(slots[s_i], n, qk_tok_max, vt)
            for s_i in range(len(grp)):
                o_spill_last = delta_finish(slots[s_i])
            grp = [(8 * a + 6, 6, vt1), (8 * a + 7, 7, vt1)]
            for s_i, (hv, vloc, vt) in enumerate(grp):
                delta_setup(slots[s_i], hv, a, vloc)
            for n in range(NCH):
                for s_i, (hv, vloc, vt) in enumerate(grp):
                    delta_chunk(slots[s_i], n, qk_tok_max, vt)
            for s_i in range(len(grp)):
                o_spill_last = delta_finish(slots[s_i])
            hat_guard_v = state["o_last"]
            hat_guard_q = state["o_last"]
            hat_guard_k = state["o_last"]
            hat_guard_kt = state["o_last"]

        stB.close()
        stSC.close()

        stC = ExitStack()
        wout16 = sb("wout16", [128, 32 * 2048], BF16, stC)
        ogfm = sb("ogfm", [128, 32 * 256], BF16, stC)
        ozin = sb("ozin", [128, 2048], BF16, stC)
        ozin2 = sb("ozin2", [128, 2048], BF16, stC)
        zh_f = sb("zh_f", [128, 2048], F32, stC)
        ogh_f = sb("ogh_f", [128, 2048], F32, stC)
        attnq = sb("attnq", [128, 16 * 256], F32, stC)
        lnbuf = [sb("lnbuf%d" % i, [128, 256], BF16, stC) for i in range(2)]
        xinc = sb("xinc", [128, 2048], F32, stC)
        msum = sb("msum", [128, 16], F32, stC)
        msr = sb("msr", [128, 16], F32, stC)
        msr2 = sb("msr2", [128, 16], F32, stC)
        x1bpc = sb("x1bpc", [128, 256], BF16, stC)

        phase_barrier(o_fin + [z_spill_all[0]] + bounce_fin)
        wout_toks = []
        for vc in range(32):
            p.waits("gpsimd", [a2_pe_done] + o_fin)
            wout_toks.append(dma("gpsimd", "qwo", wout16[:, vc * 2048:(vc + 1) * 2048],
                                 wout_h[vc * 128:(vc + 1) * 128, :]))
        wout_last = wout_toks[-1]

        x1_spill_last = None
        x1f_last = None
        lnst_guard = None
        state_c = {}
        for oc in range(8):
            og_last = None
            for tbl in range(2):
                tb = oc * 2 + tbl
                for h2 in range(2):
                    p.waits("sync", o_fin + [z_spill_all[0], og_last])
                    t_o = dma("sync", "qco", ozin[:].rearrange("q (h d) -> q h d", d=128),
                              o_h[h2 * 16:(h2 + 1) * 16, tb * 128:(tb + 1) * 128, :].rearrange(
                                  "h q d -> q h d"))
                    t_z = dma("sync", "qcz", ozin2[:].rearrange("q (h c) -> q h c", c=256),
                              z_h[h2 * 8:(h2 + 1) * 8, tb * 128:(tb + 1) * 128, :].rearrange(
                                  "h q c -> q h c"))
                    t_sz = vop("scalar", lambda e: e.activation(zh_f[:], ozin2[:], AF.Silu),
                               waits=[t_z, og_last, state_c.get("t_ms")])
                    t_og = vop("vector", lambda e: e.tensor_tensor(ogh_f[:], ozin[:], zh_f[:],
                                                                   ALU.mult),
                               waits=[t_sz, t_o, og_last])
                    t_sq = vop("scalar", lambda e: e.activation(zh_f[:], ogh_f[:], AF.Square),
                               waits=[t_og])
                    t_ms = vop("vector", lambda e:
                               e.tensor_reduce(msum[:], zh_f[:].rearrange("q (h d) -> q h d", d=128),
                                               AX.X, ALU.add), waits=[t_sq])
                    state_c["t_ms"] = t_ms
                    t_mq = vop("scalar", lambda e: e.activation(msr[:], msum[:], AF.Sqrt,
                                                                bias=1e-6, scale=1.0 / 128.0),
                               waits=[t_ms])
                    t_mr = vop("vector", lambda e: e.reciprocal(msr2[:], msr[:]), waits=[t_mq])
                    t_on = vop("vector", lambda e: e.tensor_tensor(
                        ogh_f[:].rearrange("q (h d) -> q h d", d=128),
                        ogh_f[:].rearrange("q (h d) -> q h d", d=128),
                        bass.AP(msr2.ap().tensor, 0, [msr2.ap().ap[0], [1, 16], [0, 128]]),
                        ALU.mult), waits=[t_mr])
                    for vcl in range(16):
                        vc = h2 * 16 + vcl
                        bk = vcl % 2
                        t_tr = tr(bk, pb[bk][:, 0:128], ogh_f[:, vcl * 128:(vcl + 1) * 128],
                                  idf[:], waits=[t_on])
                        og_last = vop("scalar", lambda e,
                                      o=ogfm[:, vc * 256 + tbl * 128: vc * 256 + (tbl + 1) * 128],
                                      i=pb[bk][:, 0:128]:
                                      e.activation(o, i, AF.Identity, scale=normw[:]),
                                      waits=[t_tr], bank=bk)
            attn_last = None
            for dc in range(16):
                bk = 2 + dc % 2
                p.waits("tensor", [og_last, wout_last])
                for vc in range(32):
                    t_mm = mm(bk if vc == 0 else None, pb[bk][:, 0:256],
                              wout16[:, vc * 2048 + dc * 128: vc * 2048 + dc * 128 + 128],
                              ogfm[:, vc * 256:(vc + 1) * 256], start=(vc == 0),
                              stop=(vc == 31), tok=(vc == 31))
                attn_last = vop("vector", lambda e, o=attnq[:, dc * 256:(dc + 1) * 256],
                                i=pb[bk][:, 0:256]: e.tensor_copy(o, i),
                                waits=[t_mm, x1_spill_last, x1f_last], bank=bk)
            res_last = None
            for tbl in range(2):
                tb = oc * 2 + tbl
                p.waits("sync", [res_last])
                t_x = dma("sync", "qcx", xinc[:], x_h[tb * 128:(tb + 1) * 128, :])
                for dc in range(16):
                    bk = 4 + dc % 2
                    t_tr = tr(bk, pb[bk][:, 0:128], xinc[:, dc * 128:(dc + 1) * 128], idf[:],
                              waits=[t_x, attn_last])
                    res_last = vop("vector", lambda e,
                                   o=attnq[:, dc * 256 + tbl * 128: dc * 256 + (tbl + 1) * 128],
                                   i=pb[bk][:, 0:128]:
                                   e.tensor_tensor(o, o, i, ALU.add), waits=[t_tr], bank=bk)
            mm_tok = [None, None]
            for dc in range(16):
                tc1 = vop("vector", lambda e, o=lnbuf[0][:], i=attnq[:, dc * 256:(dc + 1) * 256]:
                          e.tensor_copy(o, i), waits=[res_last, mm_tok[0], lnst_guard])
                mm_tok[0] = mm(0 if dc == 0 else None, pb[0][0:1, 0:256], onesb[:, 0:1],
                               lnbuf[0][:], start=(dc == 0), stop=(dc == 15), waits=[tc1],
                               tok=True)
                tc2 = vop("scalar", lambda e, o=lnbuf[1][:], i=attnq[:, dc * 256:(dc + 1) * 256]:
                          e.activation(o, i, AF.Square), waits=[res_last, mm_tok[1], lnst_guard])
                mm_tok[1] = mm(1 if dc == 0 else None, pb[1][0:1, 0:256], onesb[:, 0:1],
                               lnbuf[1][:], start=(dc == 0), stop=(dc == 15), waits=[tc2],
                               tok=True)
            t_mu = vop("vector", lambda e: e.tensor_scalar(murow[0:1, 0:256], pb[0][0:1, 0:256],
                                                           1.0 / 2048.0, None, ALU.mult),
                       waits=[mm_tok[0]], bank=0)
            t_m2 = vop("vector", lambda e: e.tensor_scalar(srow[0:1, 0:256], pb[1][0:1, 0:256],
                                                           1.0 / 2048.0, None, ALU.mult),
                       waits=[mm_tok[1]], bank=1)
            t_v1 = vop("vector", lambda e: e.tensor_tensor(s2row[0:1, 0:256], murow[0:1, 0:256],
                                                           murow[0:1, 0:256], ALU.mult),
                       waits=[t_mu])
            t_v2 = vop("vector", lambda e: e.tensor_sub(s2row[0:1, 0:256], srow[0:1, 0:256],
                                                        s2row[0:1, 0:256]), waits=[t_m2, t_v1])
            t_v3 = vop("scalar", lambda e: e.activation(s2row[0:1, 0:256], s2row[0:1, 0:256],
                                                        AF.Sqrt, bias=1e-5), waits=[t_v2])
            t_v4 = vop("vector", lambda e: e.reciprocal(invrow[0:1, 0:256], s2row[0:1, 0:256]),
                       waits=[t_v3])
            t_b1 = mm(2, pb[2][:, 0:256], onesrf[0:1, :], murow[0:1, 0:256], tok=True,
                      waits=[t_mu])
            t_b2 = mm(3, pb[3][:, 0:256], onesrf[0:1, :], invrow[0:1, 0:256], tok=True,
                      waits=[t_v4])
            ln_last = None
            for dc in range(16):
                t_n1 = vop("vector", lambda e, o=attnq[:, dc * 256:(dc + 1) * 256]:
                           e.tensor_sub(o, o, pb[2][:, 0:256]), waits=[t_b1, t_b2, mm_tok[0],
                                                                       mm_tok[1]])
                t_n2 = vop("vector", lambda e, o=attnq[:, dc * 256:(dc + 1) * 256]:
                           e.tensor_tensor(o, o, pb[3][:, 0:256], ALU.mult), waits=[t_n1])
                t_n3 = vop("scalar", lambda e, o=x1bpc[:], i=attnq[:, dc * 256:(dc + 1) * 256]:
                           e.copy(o, i), waits=[t_n2, x1_spill_last])
                p.waits("gpsimd", [t_n2, t_n3, x1f_last])
                x1f_last = dma("gpsimd", "qx1f", x1f_h[dc, :, oc * 256:(oc + 1) * 256],
                               attnq[:, dc * 256:(dc + 1) * 256])
                x1_spill_last = dma("gpsimd", "qx1b", x1b_h[dc, :, oc * 256:(oc + 1) * 256],
                                    x1bpc[:])
                ln_last = t_n2
            p.bank_guard[2] = ln_last
            p.bank_guard[3] = ln_last
            lnst_guard = ln_last

        stC.close()

        stD = ExitStack()
        x1bt = sb("x1bt", [128, 16 * 2048], BF16, stD)
        wgts = [sb("wgts%d" % i, [128, 16 * 512], BF16, stD) for i in range(2)]
        wuts = [sb("wuts%d" % i, [128, 16 * 512], BF16, stD) for i in range(2)]
        sg_scr = sb("sg_scr", [128, 512], F32, stD)
        act_d = [sb("act_d%d" % i, [128, 512], BF16, stD) for i in range(2)]
        asum_d = sb("asum_d", [1, 2048], F32, stD)

        phase_barrier([x1_spill_last, x1f_last])
        t_az = vop("vector", lambda e: e.memset(asum_d[:], 0.0))
        x1b_tok = None
        for dc in range(16):
            p.waits("sync", [x1_spill_last])
            x1b_tok = dma("sync", "qdx", x1bt[:, dc * 2048:(dc + 1) * 2048], x1b_h[dc, :, :])
        wg_use = [None, None]
        wg_tok = [None, None]
        act_spill_last = None
        t_as_prev = t_az
        t_sg_prev = None
        act_guard = [None, None]
        for ihs in range(16):
            par = ihs % 2
            p.waits("sync", [wg_use[par]] + bounce_fin)
            tw = None
            for hc in range(16):
                dma("sync", "qdw%d" % par, wgts[par][:, hc * 512:(hc + 1) * 512],
                    wg16_h[ihs * 512:(ihs + 1) * 512, hc * 128:(hc + 1) * 128],
                    transpose=True)
                tw = dma("sync", "qdw%d" % par, wuts[par][:, hc * 512:(hc + 1) * 512],
                         wu16_h[ihs * 512:(ihs + 1) * 512, hc * 128:(hc + 1) * 128],
                         transpose=True)
            wg_tok[par] = tw
            for icl in range(4):
                ic = ihs * 4 + icl
                for tq in range(4):
                    bg = tq % 2
                    bu = 2 + tq % 2
                    p.waits("tensor", [x1b_tok, tw])
                    for hc in range(16):
                        t_gm = mm(bg if hc == 0 else None, pb[bg][:, 0:512],
                                  wgts[par][:, hc * 512 + icl * 128: hc * 512 + (icl + 1) * 128],
                                  x1bt[:, hc * 2048 + tq * 512: hc * 2048 + (tq + 1) * 512],
                                  start=(hc == 0), stop=(hc == 15), tok=(hc == 15))
                    for hc in range(16):
                        t_um = mm(bu if hc == 0 else None, pb[bu][:, 0:512],
                                  wuts[par][:, hc * 512 + icl * 128: hc * 512 + (icl + 1) * 128],
                                  x1bt[:, hc * 2048 + tq * 512: hc * 2048 + (tq + 1) * 512],
                                  start=(hc == 0), stop=(hc == 15), tok=(hc == 15))
                    t_sg = vop("scalar", lambda e, i=pb[bg][:, 0:512], s=colap(sgt[:], ic):
                               e.activation(sg_scr[:], i, AF.Silu, scale=s),
                               waits=[t_gm, t_sg_prev], bank=bg)
                    apar = (ic * 4 + tq) % 2
                    t_ac = vop("vector", lambda e, o=act_d[apar][:], s=colap(sut[:], ic),
                               i=pb[bu][:, 0:512]:
                               e.scalar_tensor_tensor(o, sg_scr[:], s, i, ALU.mult, ALU.mult),
                               waits=[t_sg, t_um, act_guard[apar]], bank=bu)
                    t_sg_prev = t_ac
                    ba_ = 4 + tq % 2
                    t_am = mm(ba_, pb[ba_][0:1, 0:512], onesb[:, 0:1], act_d[apar][:], tok=True,
                              waits=[t_ac, t_as_prev])
                    t_as = vop("vector", lambda e, o=asum_d[0:1, tq * 512:(tq + 1) * 512],
                               i=pb[ba_][0:1, 0:512]:
                               e.tensor_tensor(o, o, i, ALU.add), waits=[t_am], bank=ba_)
                    t_as_prev = t_as
                    p.waits("gpsimd", [t_ac])
                    act_spill_last = dma("gpsimd", "qda%d" % apar,
                                         act_h[ic, :, tq * 512:(tq + 1) * 512],
                                         act_d[apar][:])
                    act_guard[apar] = act_spill_last
            wg_use[par] = t_um

        stD.close()

        stE = ExitStack()
        act_e = sb("act_e", [128, 64 * 512], BF16, stE)
        wdt_e = [sb("wdt_e%d" % i, [128, 512], BF16, stE) for i in range(2)]
        resq = sb("resq", [128, 16 * 512], F32, stE)
        x1f_e = [sb("x1f_e%d" % i, [128, 512], F32, stE) for i in range(2)]
        t1_e = sb("t1_e", [128, 512], F32, stE)
        asumbc = sb("asumbc", [128, 2048], F32, stE)
        lnb_e = [sb("lnb_e%d" % i, [128, 512], BF16, stE) for i in range(2)]
        outtm = [sb("outtm%d" % i, [128, 2048], F32, stE) for i in range(2)]

        phase_barrier([act_guard[0], act_guard[1]] + bounce_fin)
        ab_last = None
        y_fin = [None, None]
        x1f_guard = [None, None]
        for tq in range(4):
            bk = tq % 2
            t_bm = mm(bk, pb[bk][:, 0:512], onesrf[0:1, :], asum_d[0:1, tq * 512:(tq + 1) * 512],
                      tok=True, waits=[t_as_prev])
            ab_last = vop("vector", lambda e, o=asumbc[:, tq * 512:(tq + 1) * 512],
                          i=pb[bk][:, 0:512]: e.tensor_copy(o, i), waits=[t_bm], bank=bk)

        y_last = None
        act_ld = [None] * 64
        outtm_guard = [None, None]
        prev_tq_mm = None
        prev_tq_ev = None
        for tq in range(4):
            for ic in range(64):
                p.waits("scalar", [act_guard[0], act_guard[1], prev_tq_mm])
                act_ld[ic] = dma("scalar", "qea", act_e[:, ic * 512:(ic + 1) * 512],
                                 act_h[ic, :, tq * 512:(tq + 1) * 512])
            act_ready = act_ld[63]
            dn_last = prev_tq_ev
            for hq4 in range(4):
                wd_tok = [None, None]
                for ic in range(64):
                    par = ic % 2
                    p.waits("sync", [wd_tok[par]] + bounce_fin)
                    t_wi = dma("sync", "qew%d" % par, wdt_e[par][:],
                               wd16_h[hq4 * 512:(hq4 + 1) * 512, ic * 128:(ic + 1) * 128],
                               transpose=True)
                    p.waits("tensor", [t_wi, act_ready])
                    for h4 in range(4):
                        wd_tok[par] = mm(h4 if ic == 0 else None, pb[h4][:, 0:512],
                                         wdt_e[par][:, h4 * 128:(h4 + 1) * 128],
                                         act_e[:, ic * 512:(ic + 1) * 512],
                                         start=(ic == 0), stop=(ic == 63), tok=(h4 == 3))
                t_dmm = wd_tok[(63) % 2]
                for h4 in range(4):
                    hc = hq4 * 4 + h4
                    par = h4 % 2
                    p.waits("sync", [x1f_guard[par], x1f_last])
                    t_xf = dma("sync", "qef%d" % par, x1f_e[par][:],
                               x1f_h[hc, :, tq * 512:(tq + 1) * 512])
                    t_e1 = vop("vector", lambda e, ps=pb[h4][:, 0:512],
                               a=asumbc[:, tq * 512:(tq + 1) * 512]:
                               e.scalar_tensor_tensor(t1_e[:], a, -128.0, ps, ALU.mult, ALU.add),
                               waits=[t_dmm, ab_last, dn_last])
                    t_e2 = vop("vector", lambda e, o=resq[:, hc * 512:(hc + 1) * 512],
                               s=colap(sdt[:], hc), xf=x1f_e[par][:]:
                               e.scalar_tensor_tensor(o, t1_e[:], s, xf, ALU.mult, ALU.add),
                               waits=[t_e1, t_xf], bank=h4)
                    x1f_guard[par] = t_e2
                    dn_last = t_e2
            mm_tok = [None, None]
            for hc in range(16):
                tc1 = vop("vector", lambda e, o=lnb_e[0][:], i=resq[:, hc * 512:(hc + 1) * 512]:
                          e.tensor_copy(o, i), waits=[dn_last, mm_tok[0]])
                mm_tok[0] = mm(0 if hc == 0 else None, pb[0][0:1, 0:512], onesb[:, 0:1],
                               lnb_e[0][:], start=(hc == 0), stop=(hc == 15), waits=[tc1],
                               tok=True)
                tc2 = vop("scalar", lambda e, o=lnb_e[1][:], i=resq[:, hc * 512:(hc + 1) * 512]:
                          e.activation(o, i, AF.Square), waits=[dn_last, mm_tok[1]])
                mm_tok[1] = mm(1 if hc == 0 else None, pb[1][0:1, 0:512], onesb[:, 0:1],
                               lnb_e[1][:], start=(hc == 0), stop=(hc == 15), waits=[tc2],
                               tok=True)
            t_mu = vop("vector", lambda e: e.tensor_scalar(murow[0:1, 0:512], pb[0][0:1, 0:512],
                                                           1.0 / 2048.0, None, ALU.mult),
                       waits=[mm_tok[0]], bank=0)
            t_m2 = vop("vector", lambda e: e.tensor_scalar(srow[0:1, 0:512], pb[1][0:1, 0:512],
                                                           1.0 / 2048.0, None, ALU.mult),
                       waits=[mm_tok[1]], bank=1)
            t_v1 = vop("vector", lambda e: e.tensor_tensor(s2row[0:1, 0:512], murow[0:1, 0:512],
                                                           murow[0:1, 0:512], ALU.mult),
                       waits=[t_mu])
            t_v2 = vop("vector", lambda e: e.tensor_sub(s2row[0:1, 0:512], srow[0:1, 0:512],
                                                        s2row[0:1, 0:512]), waits=[t_m2, t_v1])
            t_v3 = vop("scalar", lambda e: e.activation(s2row[0:1, 0:512], s2row[0:1, 0:512],
                                                        AF.Sqrt, bias=1e-5), waits=[t_v2])
            t_v4 = vop("vector", lambda e: e.reciprocal(invrow[0:1, 0:512], s2row[0:1, 0:512]),
                       waits=[t_v3])
            t_b1 = mm(2, pb[2][:, 0:512], onesrf[0:1, :], murow[0:1, 0:512], tok=True,
                      waits=[t_mu])
            t_b2 = mm(3, pb[3][:, 0:512], onesrf[0:1, :], invrow[0:1, 0:512], tok=True,
                      waits=[t_v4])
            nm_last = None
            for hc in range(16):
                t_n1 = vop("vector", lambda e, o=resq[:, hc * 512:(hc + 1) * 512]:
                           e.tensor_sub(o, o, pb[2][:, 0:512]),
                           waits=[t_b1, t_b2, mm_tok[0], mm_tok[1]])
                nm_last = vop("vector", lambda e, o=resq[:, hc * 512:(hc + 1) * 512]:
                              e.tensor_tensor(o, o, pb[3][:, 0:512], ALU.mult), waits=[t_n1])
            p.bank_guard[2] = nm_last
            p.bank_guard[3] = nm_last
            for tbl in range(4):
                tb = tq * 4 + tbl
                opar = tbl % 2
                ev_last = None
                for dc in range(16):
                    bk = 4 + dc % 2
                    t_tr = tr(bk, pb[bk][:, 0:128],
                              resq[:, dc * 512 + tbl * 128: dc * 512 + (tbl + 1) * 128], idf[:],
                              waits=[nm_last])
                    ev_last = vop("scalar", lambda e, o=outtm[opar][:, dc * 128:(dc + 1) * 128],
                                  i=pb[bk][:, 0:128]: e.copy(o, i),
                                  waits=[t_tr, outtm_guard[opar]], bank=bk)
                p.waits("scalar", [ev_last])
                y_last = dma("scalar", "qy%d" % opar, y_h[tb * 128:(tb + 1) * 128, :],
                             outtm[opar][:])
                outtm_guard[opar] = y_last
                y_fin[opar] = y_last
            prev_tq_mm = mm_tok[1]
            prev_tq_ev = ev_last

        stE.close()

        p.waits("scalar", y_fin)

        for s in sorted(DMA_SEMS) + ["sp", "sv", "sa", "sgp"]:
            p.handles[s] = ctx.enter_context(nc.semaphore(s))
        block = ctx.enter_context(nc.Block())

        @block.tensor
        def _(e):
            for f in p.plan["tensor"]:
                f(e)

        @block.vector
        def _(e):
            for f in p.plan["vector"]:
                f(e)

        @block.scalar
        def _(e):
            for f in p.plan["scalar"]:
                f(e)

        @block.gpsimd
        def _(e):
            for f in p.plan["gpsimd"]:
                f(e)

        @block.sync
        def _(e):
            for f in p.plan["sync"]:
                f(e)

    return nc


_cached = {}


def kernel(**inputs):
    if "nc" not in _cached:
        _cached["nc"] = build_program()
    nc = _cached["nc"]
    x = np.ascontiguousarray(inputs["L0_attn_x"])
    dtbv = np.zeros((64, 1), np.float32)
    algv = np.zeros((64, 1), np.float32)
    dtb_in = np.asarray(inputs["L0_attn_dt_bias"], np.float32)
    alog_in = np.asarray(inputs["L0_attn_A_log"], np.float32)
    for hv in range(HV):
        ach = (hv // 2) * 4 + 2 + hv % 2
        dtbv[ach, 0] = dtb_in[hv]
        algv[ach, 0] = alog_in[hv]
    cwl = np.ascontiguousarray(
        np.asarray(inputs["L0_attn_conv_w"], np.float32).reshape(16, 4, 128, 4)
        .transpose(2, 0, 1, 3).reshape(128, 256))
    common = {
        "wqkvz": np.ascontiguousarray(inputs["L0_attn_W_qkvz"], dtype=np.float32),
        "wba": np.ascontiguousarray(inputs["L0_attn_W_ba"], dtype=np.float32),
        "cw": cwl,
        "dtb": dtbv,
        "alog": algv,
        "normw": np.ascontiguousarray(np.asarray(inputs["L0_attn_norm_w"], np.float32).reshape(128, 1)),
        "wout": np.ascontiguousarray(inputs["L0_attn_W_out"], dtype=np.float32),
        "gw": np.ascontiguousarray(inputs["L0_ffn_gate_w"], dtype=np.int32),
        "uw": np.ascontiguousarray(inputs["L0_ffn_up_w"], dtype=np.int32),
        "dw": np.ascontiguousarray(inputs["L0_ffn_down_w"], dtype=np.int32),
        "sgt": np.ascontiguousarray(np.asarray(inputs["L0_ffn_gate_scale"], np.float32).reshape(64, 128).T),
        "sut": np.ascontiguousarray(np.asarray(inputs["L0_ffn_up_scale"], np.float32).reshape(64, 128).T),
        "sdt": np.ascontiguousarray(np.asarray(inputs["L0_ffn_down_scale"], np.float32).reshape(16, 128).T),
    }
    in_maps = []
    for b in range(8):
        m = dict(common)
        m["x"] = np.ascontiguousarray(x[b])
        in_maps.append(m)
    res = run_bass_kernel_spmd(nc, in_maps, core_ids=list(range(8)))
    return np.stack([res.results[b]["y"] for b in range(8)], axis=0)
